# version 09
# baseline (speedup 1.0000x reference)
import os
import sys

sys.path.insert(0, "/opt/trn_rl_repo")

import numpy as np
import ml_dtypes
import concourse.bass as bass
import concourse.mybir as mybir
from concourse.bass_utils import run_bass_kernel_spmd
from concourse.alu_op_type import AluOpType as ALU
from contextlib import ExitStack

F32 = mybir.dt.float32
BF16 = mybir.dt.bfloat16
AF = mybir.ActivationFunctionType
AX = mybir.AxisListType

P = 128
T = 2048
D = 2048
NT = T // P
ND = D // P
CH = 512
NCH = T // CH

QK_SCALE = float(P) ** -0.5
L0_SCALE = float(P) ** -0.5
LN_EPS = 1e-5
RMS_EPS = 1e-6
L0_EPS = 1e-9
N_CORES = 8



def slab_weights(WT, osz=P):
    WT = np.ascontiguousarray(WT, dtype=np.float32)
    K, O = WT.shape
    kt = K // P
    A = WT.reshape(kt, P, O // osz, osz).transpose(2, 1, 0, 3)
    return np.ascontiguousarray(A.reshape(O // osz, P, kt * osz))



def emit_xT_build(block, src_dram, row0, nrow_tiles, xT, stage, sems,
                  cast=True):
    raise NotImplementedError



def build_phase1(nc, x, wqkv, wout, eye32, eye16, R1):
    NSLAB_B = 3 * ND

    def cb(s):
        if s < NSLAB_B:
            return 4 * (s + 1)
        return 4 * NSLAB_B + 2 * (s - NSLAB_B + 1)

    with ExitStack() as ctx:
        sb = lambda n_, s_, t_: ctx.enter_context(nc.sbuf_tensor(n_, s_, t_))
        sem = lambda n_: ctx.enter_context(nc.semaphore(n_))

        outT = sb("p1_outT", [P, ND * T], BF16)
        wbuf = sb("p1_wbuf", [P, 3 * D], BF16)
        eyeb = sb("p1_eyeb", [P, P], BF16)
        eyef = sb("p1_eyef", [P, P], F32)
        ones1 = sb("p1_ones1", [P, 1], BF16)
        epsb = sb("p1_epsb", [P, 1], F32)
        lnepsb = sb("p1_lnepsb", [P, 1], F32)

        dA = sem("p1_dA")
        wld = [sem("p1_wld0"), sem("p1_wld1"), sem("p1_wld2")]
        mmB = sem("p1_mmB")
        evB = sem("p1_evB")
        trA = sem("p1_trA")
        evA = sem("p1_evA")
        kst = sem("p1_kst")
        dsem = sem("p1_dsem")
        recS = sem("p1_recS")
        numS = sem("p1_numS")
        vout = sem("p1_vout")
        vch = sem("p1_vch")
        sch = sem("p1_sch")
        trC = sem("p1_trC")
        addC = sem("p1_addC")
        lnv = sem("p1_lnv")
        lnr = sem("p1_lnr")
        lnx = sem("p1_lnx")
        lnw = sem("p1_lnw")
        dC = sem("p1_dC")
        dR = [sem("p1_dR0"), sem("p1_dR1")]
        msc = sem("p1_msc")

        ones_bcast = bass.AP(ones1, 0, [[1, P], [0, T]])

        with ExitStack() as sctx:
            sbl = lambda n_, s_, t_: sctx.enter_context(nc.sbuf_tensor(n_, s_, t_))
            psl = lambda n_, s_, t_: sctx.enter_context(nc.psum_tensor(n_, s_, t_))
            xT = sbl("p1_xT", [P, ND * T], BF16)
            xtmb = sbl("p1_xtmb", [P, D], BF16)
            vbuf = sbl("p1_vbuf", [P, T], BF16)
            kp = sbl("p1_kp", [P, T], BF16)
            kv = sbl("p1_kv", [P, T], BF16)
            qp = sbl("p1_qp", [P, T], F32)
            abuf = sbl("p1_a", [P, T], F32)
            cbuf = sbl("p1_c", [P, T], F32)
            KVc = sbl("p1_KVc", [P, T], F32)
            Kc = sbl("p1_Kc", [P, T], F32)
            psA = [psl(f"p1_psA{i}", [P, CH], F32) for i in range(4)]
            ptr = [psl(f"p1_ptr{i}", [P, 8 * P], BF16) for i in range(2)]

            with nc.Block() as block:
                @block.gpsimd
                def _(g):
                    g.memset(ones1[:, :], 1.0).then_inc(msc, 1)
                    g.memset(epsb[:, :], L0_EPS).then_inc(msc, 1)
                    g.memset(lnepsb[:, :], LN_EPS).then_inc(msc, 1)
                    g.dma_start(eyeb[:, :], eye16[:, :]).then_inc(dA, 16)
                    g.dma_start(eyef[:, :], eye32[:, :]).then_inc(dA, 16)
                    for tt in range(NT):
                        if tt >= 1:
                            g.wait_ge(trA, 2 * tt)
                        g.dma_start(
                            xtmb[:, :], x[tt * P:(tt + 1) * P, :],
                        ).then_inc(dA, 16)
                    for s in range(NSLAB_B):
                        if s >= 3:
                            g.wait_ge(mmB, cb(s - 3))
                        g.dma_start(
                            wbuf[:, (s % 3) * D:(s % 3 + 1) * D],
                            wqkv[s, :, :],
                        ).then_inc(wld[s % 3], 16)

                @block.tensor
                def _(t):
                    for tt in range(NT):
                        t.wait_ge(dA, 32 + 16 * (tt + 1))
                        if tt >= 1:
                            t.wait_ge(evA, 2 * tt)
                        for dt in range(ND):
                            inst = t.transpose(
                                ptr[dt // 8][:, (dt % 8) * P:(dt % 8 + 1) * P],
                                xtmb[:, dt * P:(dt + 1) * P],
                                eyeb[:, :])
                            if dt % 8 == 7:
                                inst.then_inc(trA, 1)
                    t.wait_ge(evA, 2 * NT)
                    for s in range(NSLAB_B):
                        t.wait_ge(wld[s % 3], 16 * (s // 3 + 1))
                        for c in range(NCH):
                            m = 4 * s + c
                            if m >= 4:
                                t.wait_ge(evB, m - 3)
                            for kt in range(ND):
                                inst = t.matmul(
                                    psA[m % 4][:, :],
                                    wbuf[:, (s % 3) * D + kt * P:
                                         (s % 3) * D + (kt + 1) * P],
                                    xT[:, kt * T + c * CH:kt * T + (c + 1) * CH],
                                    start=(kt == 0), stop=(kt == ND - 1))
                            inst.then_inc(mmB, 1)

                @block.vector
                def _(v):
                    for tt in range(NT):
                        for half in range(2):
                            v.wait_ge(trA, 2 * tt + half + 1)
                            v.tensor_copy(
                                bass.AP(xT, half * 8 * T + tt * P,
                                        [[ND * T, P], [T, 8], [1, P]]),
                                ptr[half][:, :]).then_inc(evA, 1)
                    v.wait_ge(msc, 3)
                    for i in range(ND):
                        v.wait_ge(evB, 4 * (3 * i + 1) + 4)
                        v.scalar_tensor_tensor(
                            kp[:, :], cbuf[:, :], 1.0, abuf[:, :],
                            op0=ALU.min, op1=ALU.add).then_inc(kst, 1)
                        v.wait_ge(kst, i + 1)
                        v.tensor_tensor(kv[:, :], kp[:, :], vbuf[:, :],
                                        op=ALU.mult).then_inc(vch, 1)
                        v.wait_ge(vch, 5 * i + 1)
                        v.tensor_tensor_scan(KVc[:, :], ones_bcast, kv[:, :],
                                             0.0, op0=ALU.mult,
                                             op1=ALU.add).then_inc(vch, 1)
                        v.tensor_tensor_scan(Kc[:, :], ones_bcast, kp[:, :],
                                             0.0, op0=ALU.mult,
                                             op1=ALU.add).then_inc(vch, 1)
                        v.wait_ge(evB, 4 * (3 * i + 2) + 4)
                        v.scalar_tensor_tensor(
                            qp[:, :], cbuf[:, :], 1.0, abuf[:, :],
                            op0=ALU.min, op1=ALU.add).then_inc(vch, 1)
                        v.wait_ge(vch, 5 * i + 4)
                        v.tensor_tensor(abuf[:, :], qp[:, :], Kc[:, :],
                                        op=ALU.mult).then_inc(dsem, 1)
                        v.tensor_tensor(KVc[:, :], qp[:, :], KVc[:, :],
                                        op=ALU.mult).then_inc(vch, 1)
                        v.wait_ge(recS, i + 1)
                        v.wait_ge(vch, 5 * i + 5)
                        v.tensor_tensor(outT[:, i * T:(i + 1) * T],
                                        KVc[:, :], abuf[:, :],
                                        op=ALU.mult).then_inc(vout, 1)

                @block.scalar
                def _(s_):
                    for i in range(ND):
                        if i >= 1:
                            s_.wait_ge(vout, i)
                        for c in range(NCH):
                            m = 4 * (3 * i) + c
                            s_.wait_ge(mmB, m + 1)
                            s_.activation(vbuf[:, c * CH:(c + 1) * CH],
                                          psA[m % 4][:, :],
                                          AF.Copy).then_inc(evB, 1)
                        for c in range(NCH):
                            m = 4 * (3 * i + 1) + c
                            s_.wait_ge(mmB, m + 1)
                            s_.activation(abuf[:, c * CH:(c + 1) * CH],
                                          psA[m % 4][:, :], AF.Relu)
                            s_.activation(cbuf[:, c * CH:(c + 1) * CH],
                                          psA[m % 4][:, :],
                                          AF.Exp).then_inc(evB, 1)
                        s_.wait_ge(kst, i + 1)
                        for c in range(NCH):
                            m = 4 * (3 * i + 2) + c
                            s_.wait_ge(mmB, m + 1)
                            s_.activation(abuf[:, c * CH:(c + 1) * CH],
                                          psA[m % 4][:, :], AF.Relu)
                            s_.activation(cbuf[:, c * CH:(c + 1) * CH],
                                          psA[m % 4][:, :],
                                          AF.Exp).then_inc(evB, 1)
                        s_.wait_ge(dsem, i + 1)
                        if i == 0:
                            s_.wait_ge(msc, 3)
                        s_.activation(cbuf[:, :], abuf[:, :], AF.Ln,
                                      bias=epsb[:, :]).then_inc(sch, 1)
                        s_.wait_ge(sch, i + 1)
                        s_.activation(abuf[:, :], cbuf[:, :], AF.Exp,
                                      scale=-1.0).then_inc(recS, 1)


        with ExitStack() as sctx:
            sbl = lambda n_, s_, t_: sctx.enter_context(nc.sbuf_tensor(n_, s_, t_))
            psl = lambda n_, s_, t_: sctx.enter_context(nc.psum_tensor(n_, s_, t_))
            sumh = sbl("p1_sumh", [P, 8 * D], F32)
            aT = sbl("p1_aT", [P, 2 * 1024], F32)
            mvb = sbl("p1_mv", [P, 2], F32)
            bn6 = sbl("p1_bn6", [P, 24], F32)
            rstd = sbl("p1_rstd", [P, 1], F32)
            r1t = sbl("p1_r1t", [P, 2 * D], F32)
            psC = [psl(f"p1_psC{i}", [P, CH], F32) for i in range(4)]
            psT = [psl(f"p1_psT{i}", [P, CH], F32) for i in range(2)]

            with nc.Block() as block:
                @block.sync
                def _(sy):
                    sy.wait_ge(vout, ND)
                    for th in range(2):
                        for tq in range(8):
                            sy.dma_start(
                                sumh[:, tq * D:(tq + 1) * D],
                                x[(th * 8 + tq) * P:(th * 8 + tq + 1) * P, :],
                            ).then_inc(dC, 16)
                        for tq in range(8):
                            gidx = th * 8 + tq
                            sy.wait_ge(lnw, gidx + 1)
                            sy.dma_start(
                                R1[gidx * P:(gidx + 1) * P, :],
                                r1t[:, (tq % 2) * D:(tq % 2 + 1) * D],
                            ).then_inc(dR[gidx % 2], 16)
                    sy.wait_ge(dR[0], 16 * 8)
                    sy.wait_ge(dR[1], 16 * 8)

                @block.gpsimd
                def _(g):
                    for th in range(2):
                        for dt in range(ND):
                            s = NSLAB_B + th * ND + dt
                            g.wait_ge(mmB, cb(s - 3))
                            g.dma_start(
                                wbuf[:, (s % 3) * D:(s % 3 + 1) * D],
                                wout[dt, :, :]).then_inc(wld[s % 3], 16)

                @block.tensor
                def _(t):
                    t.wait_ge(vout, ND)
                    t.wait_ge(evB, 4 * NSLAB_B)
                    for th in range(2):
                        for dt in range(ND):
                            s = NSLAB_B + th * ND + dt
                            idx = th * ND + dt
                            t.wait_ge(wld[s % 3], 16 * (s // 3 + 1))
                            for c in range(2):
                                m = 4 * NSLAB_B + 2 * idx + c
                                t.wait_ge(evB, m - 3)
                                for kt in range(ND):
                                    inst = t.matmul(
                                        psC[m % 4][:, :],
                                        wbuf[:, (s % 3) * D + kt * P:
                                             (s % 3) * D + (kt + 1) * P],
                                        outT[:, kt * T + th * 1024 + c * CH:
                                             kt * T + th * 1024 + (c + 1) * CH],
                                        start=(kt == 0), stop=(kt == ND - 1))
                                inst.then_inc(mmB, 1)
                            t.wait_ge(evB, 4 * NSLAB_B + 2 * idx + 2)
                            for pk in range(2):
                                if idx >= 1:
                                    t.wait_ge(addC, 2 * (idx - 1) + pk + 1)
                                for tq4 in range(4):
                                    tq = pk * 4 + tq4
                                    inst = t.transpose(
                                        psT[pk][:, tq4 * P:(tq4 + 1) * P],
                                        aT[:, (dt % 2) * 1024 + tq * P:
                                           (dt % 2) * 1024 + (tq + 1) * P],
                                        eyef[:, :])
                                inst.then_inc(trC, 1)

                @block.scalar
                def _(s_):
                    for th in range(2):
                        for dt in range(ND):
                            idx = th * ND + dt
                            if idx >= 2:
                                s_.wait_ge(trC, 2 * (idx - 1))
                            for c in range(2):
                                m = 4 * NSLAB_B + 2 * idx + c
                                s_.wait_ge(mmB, m + 1)
                                s_.activation(
                                    aT[:, (dt % 2) * 1024 + c * CH:
                                       (dt % 2) * 1024 + (c + 1) * CH],
                                    psC[m % 4][:, :], AF.Copy,
                                    scale=L0_SCALE).then_inc(evB, 1)
                        for tq in range(8):
                            gidx = th * 8 + tq
                            s_.wait_ge(lnv, 2 * (gidx + 1))
                            s_.activation(rstd[:, :], mvb[:, 1:2], AF.Ln,
                                          bias=lnepsb[:, :]).then_inc(lnx, 1)
                            s_.wait_ge(lnx, gidx + 1)
                            s_.activation(rstd[:, :], rstd[:, :], AF.Exp,
                                          scale=-0.5).then_inc(lnr, 1)

                @block.vector
                def _(v):
                    for th in range(2):
                        v.wait_ge(dC, 16 * 8 * (th + 1))
                        for dt in range(ND):
                            idx = th * ND + dt
                            for pk in range(2):
                                v.wait_ge(trC, 2 * idx + pk + 1)
                                dst = bass.AP(sumh, pk * 4 * D + dt * P,
                                              [[8 * D, P], [D, 4], [1, P]])
                                v.tensor_tensor(dst, psT[pk][:, :], dst,
                                                op=ALU.add).then_inc(addC, 1)
                        v.wait_ge(addC, 32 * (th + 1))
                        for tq in range(8):
                            gidx = th * 8 + tq
                            tile = sumh[:, tq * D:(tq + 1) * D]
                            for c4 in range(4):
                                inst = v.bn_stats(
                                    bn6[:, c4 * 6:(c4 + 1) * 6],
                                    tile[:, c4 * CH:(c4 + 1) * CH])
                            inst.then_inc(lnv, 1)
                            v.wait_ge(lnv, 2 * gidx + 1)
                            v.bn_aggr(mvb[:, :], bn6[:, :]).then_inc(lnv, 1)
                            v.wait_ge(lnr, gidx + 1)
                            if gidx >= 2:
                                v.wait_ge(dR[gidx % 2], 16 * (gidx // 2))
                            v.tensor_scalar(
                                r1t[:, (tq % 2) * D:(tq % 2 + 1) * D],
                                tile, mvb[:, 0:1], rstd[:, :],
                                op0=ALU.subtract,
                                op1=ALU.mult).then_inc(lnw, 1)
    return nc



def _clear_sems(nc):
    with nc.Block() as block:
        @block.sync
        def _(sy):
            sy.sem_clear(range(155, 250))



def build_phase2(nc, R1, wgu, wd, eye32, eye16, R2):
    HT = 1024
    NIT = 32

    _clear_sems(nc)

    def base(th, ip):
        return th * 320 + ip * 160

    gu_done = {}
    dn_done = {}
    for th in range(2):
        for ip in range(2):
            for ls in range(64):
                gu_done[th * 128 + ip * 64 + ls] = base(th, ip) + 2 * (ls + 1)
            for dt in range(ND):
                dn_done[th * 32 + ip * 16 + dt] = \
                    base(th, ip) + 128 + 2 * (dt + 1)

    with ExitStack() as ctx:
        sb = lambda n_, s_, t_: ctx.enter_context(nc.sbuf_tensor(n_, s_, t_))
        psl = lambda n_, s_, t_: ctx.enter_context(nc.psum_tensor(n_, s_, t_))
        sem = lambda n_: ctx.enter_context(nc.semaphore(n_))

        h = sb("p2_h", [P, NIT * HT], BF16)
        xT = sb("p2_xT", [P, ND * HT], BF16)
        sumh = sb("p2_sumh", [P, 8 * D], F32)
        wbufG = sb("p2_wbufG", [P, 2 * D], BF16)
        wbufD = sb("p2_wbufD", [P, 2 * 4096], BF16)
        gbuf = sb("p2_gbuf", [P, 2 * HT], BF16)
        ubuf = sb("p2_ubuf", [P, 2 * HT], BF16)
        aT = sb("p2_aT", [P, 2 * HT], F32)
        xtmb = sb("p2_xtmb", [P, D], BF16)
        eyeb = sb("p2_eyeb", [P, P], BF16)
        eyef = sb("p2_eyef", [P, P], F32)
        mvb = sb("p2_mv", [P, 2], F32)
        bn6 = sb("p2_bn6", [P, 24], F32)
        rstd = sb("p2_rstd", [P, 1], F32)
        lnepsb = sb("p2_lneps", [P, 1], F32)
        psA = [psl(f"p2_psA{i}", [P, CH], F32) for i in range(4)]
        ptrA = [psl(f"p2_ptrA{i}", [P, 8 * P], BF16) for i in range(2)]
        psT = [psl(f"p2_psT{i}", [P, CH], F32) for i in range(2)]

        dA = sem("p2_dA")
        trA = sem("p2_trA")
        evA = sem("p2_evA")
        wldG = [sem("p2_wldG0"), sem("p2_wldG1")]
        wldD = [sem("p2_wldD0"), sem("p2_wldD1")]
        mm = sem("p2_mm")
        ev = sem("p2_ev")
        hm = sem("p2_hm")
        dC = sem("p2_dC")
        trC = sem("p2_trC")
        addC = sem("p2_addC")
        lnv = sem("p2_lnv")
        lnr = sem("p2_lnr")
        lnx = sem("p2_lnx")
        lnw = sem("p2_lnw")
        dR = sem("p2_dR")
        msc = sem("p2_msc")

        with nc.Block() as block:
            @block.gpsimd
            def _(g):
                g.memset(lnepsb[:, :], LN_EPS).then_inc(msc, 1)
                g.dma_start(eyeb[:, :], eye16[:, :]).then_inc(dA, 16)
                g.dma_start(eyef[:, :], eye32[:, :]).then_inc(dA, 16)
                for th in range(2):
                    for tl in range(8):
                        tt = th * 8 + tl
                        if tt >= 1:
                            g.wait_ge(trA, 2 * tt)
                        g.dma_start(
                            xtmb[:, :], R1[tt * P:(tt + 1) * P, :],
                        ).then_inc(dA, 16)
                    for ip in range(2):
                        for ls in range(64):
                            gs = th * 128 + ip * 64 + ls
                            if gs >= 2:
                                g.wait_ge(mm, gu_done[gs - 2])
                            g.dma_start(
                                wbufG[:, (gs % 2) * D:(gs % 2 + 1) * D],
                                wgu[ip * 64 + ls, :, :]).then_inc(wldG[gs % 2], 16)
                        for dt in range(ND):
                            ds = th * 32 + ip * 16 + dt
                            if ds >= 2:
                                g.wait_ge(mm, dn_done[ds - 2])
                            g.dma_start(
                                wbufD[:, (ds % 2) * 4096:(ds % 2 + 1) * 4096],
                                wd[ip, dt, :, :]).then_inc(wldD[ds % 2], 16)

            @block.tensor
            def _(t):
                for th in range(2):
                    for tl in range(8):
                        tt = th * 8 + tl
                        t.wait_ge(dA, 32 + 16 * (tt + 1))
                        if tt >= 1:
                            t.wait_ge(evA, 2 * tt)
                        for dt in range(ND):
                            inst = t.transpose(
                                ptrA[dt // 8][:, (dt % 8) * P:(dt % 8 + 1) * P],
                                xtmb[:, dt * P:(dt + 1) * P], eyeb[:, :])
                            if dt % 8 == 7:
                                inst.then_inc(trA, 1)
                    t.wait_ge(evA, 2 * (th * 8 + 8))
                    for ip in range(2):
                        for ls in range(64):
                            gs = th * 128 + ip * 64 + ls
                            t.wait_ge(wldG[gs % 2], 16 * (gs // 2 + 1))
                            for c in range(2):
                                m = base(th, ip) + 2 * ls + c
                                if m >= 4:
                                    t.wait_ge(ev, m - 3)
                                for kt in range(ND):
                                    inst = t.matmul(
                                        psA[m % 4][:, :],
                                        wbufG[:, (gs % 2) * D + kt * P:
                                              (gs % 2) * D + (kt + 1) * P],
                                        xT[:, kt * HT + c * CH:
                                           kt * HT + (c + 1) * CH],
                                        start=(kt == 0), stop=(kt == ND - 1))
                                inst.then_inc(mm, 1)
                        t.wait_ge(hm, 32 * (th * 2 + ip + 1))
                        for dt in range(ND):
                            ds = th * 32 + ip * 16 + dt
                            idxg = th * 32 + ip * 16 + dt
                            t.wait_ge(wldD[ds % 2], 16 * (ds // 2 + 1))
                            for c in range(2):
                                m = base(th, ip) + 128 + 2 * dt + c
                                t.wait_ge(ev, m - 3)
                                for kt in range(NIT):
                                    inst = t.matmul(
                                        psA[m % 4][:, :],
                                        wbufD[:, (ds % 2) * 4096 + kt * P:
                                              (ds % 2) * 4096 + (kt + 1) * P],
                                        h[:, kt * HT + c * CH:
                                          kt * HT + (c + 1) * CH],
                                        start=(kt == 0), stop=(kt == NIT - 1))
                                inst.then_inc(mm, 1)
                            t.wait_ge(ev, base(th, ip) + 128 + 2 * dt + 2)
                            for pk in range(2):
                                if idxg >= 1:
                                    t.wait_ge(addC, 2 * (idxg - 1) + pk + 1)
                                for tq4 in range(4):
                                    tq = pk * 4 + tq4
                                    inst = t.transpose(
                                        psT[pk][:, tq4 * P:(tq4 + 1) * P],
                                        aT[:, (dt % 2) * HT + tq * P:
                                           (dt % 2) * HT + (tq + 1) * P],
                                        eyef[:, :])
                                inst.then_inc(trC, 1)

            @block.scalar
            def _(s_):
                for th in range(2):
                    for ip in range(2):
                        for ls in range(64):
                            il = ls // 2
                            is_g = (ls % 2 == 0)
                            gidx = (th * 2 + ip) * 32 + il
                            if gidx >= 2 and is_g:
                                s_.wait_ge(hm, gidx - 1)
                            for c in range(2):
                                m = base(th, ip) + 2 * ls + c
                                s_.wait_ge(mm, m + 1)
                                dst = (gbuf if is_g else ubuf)
                                s_.activation(
                                    dst[:, (il % 2) * HT + c * CH:
                                        (il % 2) * HT + (c + 1) * CH],
                                    psA[m % 4][:, :],
                                    AF.Silu if is_g else AF.Copy,
                                ).then_inc(ev, 1)
                        for dt in range(ND):
                            idxg = th * 32 + ip * 16 + dt
                            if idxg >= 2:
                                s_.wait_ge(trC, 2 * (idxg - 1))
                            for c in range(2):
                                m = base(th, ip) + 128 + 2 * dt + c
                                s_.wait_ge(mm, m + 1)
                                s_.activation(
                                    aT[:, (dt % 2) * HT + c * CH:
                                       (dt % 2) * HT + (c + 1) * CH],
                                    psA[m % 4][:, :], AF.Copy).then_inc(ev, 1)
                    for tl in range(8):
                        gt = th * 8 + tl
                        s_.wait_ge(lnv, 2 * (gt + 1))
                        if gt == 0:
                            s_.wait_ge(msc, 1)
                        s_.activation(rstd[:, :], mvb[:, 1:2], AF.Ln,
                                      bias=lnepsb[:, :]).then_inc(lnx, 1)
                        s_.wait_ge(lnx, gt + 1)
                        s_.activation(rstd[:, :], rstd[:, :], AF.Exp,
                                      scale=-0.5).then_inc(lnr, 1)

            @block.vector
            def _(v):
                for th in range(2):
                    for tl in range(8):
                        tt = th * 8 + tl
                        for half in range(2):
                            v.wait_ge(trA, 2 * tt + half + 1)
                            if th == 1 and tl == 0 and half == 0:
                                v.wait_ge(mm, base(0, 1) + 128)
                            v.tensor_copy(
                                bass.AP(xT, half * 8 * HT + tl * P,
                                        [[ND * HT, P], [HT, 8], [1, P]]),
                                ptrA[half][:, :]).then_inc(evA, 1)
                    for ip in range(2):
                        for il in range(NIT):
                            gidx = (th * 2 + ip) * 32 + il
                            if il == 0 and gidx >= 32:
                                v.wait_ge(mm, base(th, ip - 1) + 160
                                          if ip == 1 else base(th - 1, 1) + 160)
                            v.wait_ge(ev, base(th, ip) + 4 * il + 4)
                            v.tensor_tensor(
                                h[:, il * HT:(il + 1) * HT],
                                gbuf[:, (il % 2) * HT:(il % 2 + 1) * HT],
                                ubuf[:, (il % 2) * HT:(il % 2 + 1) * HT],
                                op=ALU.mult).then_inc(hm, 1)
                        for dt in range(ND):
                            idxg = th * 32 + ip * 16 + dt
                            if ip == 0 and dt == 0:
                                v.wait_ge(dC, 16 * 8 * (th + 1))
                            for pk in range(2):
                                v.wait_ge(trC, 2 * idxg + pk + 1)
                                dst = bass.AP(sumh, pk * 4 * D + dt * P,
                                              [[8 * D, P], [D, 4], [1, P]])
                                v.tensor_tensor(dst, psT[pk][:, :], dst,
                                                op=ALU.add).then_inc(addC, 1)
                    v.wait_ge(addC, 32 * (th + 1))
                    for tl in range(8):
                        gt = th * 8 + tl
                        tile = sumh[:, tl * D:(tl + 1) * D]
                        for c4 in range(4):
                            inst = v.bn_stats(bn6[:, c4 * 6:(c4 + 1) * 6],
                                              tile[:, c4 * CH:(c4 + 1) * CH])
                        inst.then_inc(lnv, 1)
                        v.wait_ge(lnv, 2 * gt + 1)
                        v.bn_aggr(mvb[:, :], bn6[:, :]).then_inc(lnv, 1)
                        v.wait_ge(lnr, gt + 1)
                        v.tensor_scalar(tile, tile, mvb[:, 0:1], rstd[:, :],
                                        op0=ALU.subtract,
                                        op1=ALU.mult).then_inc(lnw, 1)

            @block.sync
            def _(sy):
                for th in range(2):
                    if th == 1:
                        sy.wait_ge(dR, 16 * 8)
                    for tl in range(8):
                        sy.dma_start(
                            sumh[:, tl * D:(tl + 1) * D],
                            R1[(th * 8 + tl) * P:(th * 8 + tl + 1) * P, :],
                        ).then_inc(dC, 16)
                    for tl in range(8):
                        gt = th * 8 + tl
                        sy.wait_ge(lnw, gt + 1)
                        sy.dma_start(
                            R2[gt * P:(gt + 1) * P, :],
                            sumh[:, tl * D:(tl + 1) * D]).then_inc(dR, 16)
                sy.wait_ge(dR, 16 * NT)
    return nc


def host_prep_p2(w_gate_up, w_down):
    WT = np.asarray(w_gate_up, np.float32).T
    cols = []
    for ip in range(2):
        for il in range(32):
            i = ip * 32 + il
            cols.append(WT[:, i * P:(i + 1) * P])
            cols.append(WT[:, 8192 + i * P:8192 + (i + 1) * P])
    wgu_s = slab_weights(np.concatenate(cols, axis=1), osz=P)
    WdT = np.asarray(w_down, np.float32).T
    wd_s = np.zeros((2, 16, P, 4096), np.float32)
    for ip in range(2):
        sl = slab_weights(WdT[ip * 4096:(ip + 1) * 4096, :], osz=P)
        wd_s[ip] = sl
    return wgu_s, np.ascontiguousarray(wd_s)


def _mk_phase2_nc():
    nc = bass.Bass("TRN2", target_bir_lowering=False)
    R1 = nc.dram_tensor("R1", [T, D], F32, kind="ExternalInput")
    wgu = nc.dram_tensor("wgu", [128, P, D], F32, kind="ExternalInput")
    wd = nc.dram_tensor("wd", [2, 16, P, 4096], F32, kind="ExternalInput")
    eye32 = nc.dram_tensor("eye32", [P, P], F32, kind="ExternalInput")
    eye16 = nc.dram_tensor("eye16", [P, P], BF16, kind="ExternalInput")
    R2 = nc.dram_tensor("R2", [T, D], F32, kind="ExternalOutput")
    build_phase2(nc, R1, wgu, wd, eye32, eye16, R2)
    return nc


def run_phase2(R1_full, w_gate_up, w_down, cores=N_CORES):
    wgu_s, wd_s = host_prep_p2(w_gate_up, w_down)
    nc = _mk_phase2_nc()
    in_maps = []
    for c in range(cores):
        in_maps.append(dict(R1=np.ascontiguousarray(R1_full[c], np.float32),
                            wgu=wgu_s, wd=wd_s,
                            eye32=_EYE32, eye16=_EYE16))
    res = run_bass_kernel_spmd(nc, in_maps, core_ids=list(range(cores)))
    return np.stack([r["R2"] for r in res.results])




def build_phase3(nc, R2, wq, wk, wv, wo, nw, bq, bk, bv, bo, masks,
                 eye32, eye16, R3, upto=3, att="full", d_lvl="full",
                 qargs=None):
    _clear_sems(nc)
    qs = nc.dram_tensor("p3_qs", [16, P, T], BF16)
    vs = nc.dram_tensor("p3_vs", [16, P, T], BF16)
    os_ = nc.dram_tensor("p3_os", [16, P, D], BF16)
    NB = 48
    QPT = 256
    QH = 1024

    def q_src(i):
        w1t, w2t, _, _ = qargs
        if i < 128:
            return w1t[i // 2, :, (i % 2) * QH:(i % 2 + 1) * QH]
        j = i - 128
        ip, dt, qq_ = (j // 4) // 16, (j // 4) % 16, j % 4
        return w2t[ip, dt, :, qq_ * QH:(qq_ + 1) * QH]

    def q_dst(i):
        _, _, w1q, w2q = qargs
        if i < 128:
            return w1q[i // 2, :, (i % 2) * QH:(i % 2 + 1) * QH]
        j = i - 128
        ip, dt, qq_ = (j // 4) // 16, (j // 4) % 16, j % 4
        return w2q[ip, dt, :, qq_ * QH:(qq_ + 1) * QH]

    def q_batch(k, n_batches, total=QPT):
        a = (total * k) // n_batches
        b = (total * (k + 1)) // n_batches
        return range(a, b)

    with ExitStack() as ctx:
        sb = lambda n_, s_, t_: ctx.enter_context(nc.sbuf_tensor(n_, s_, t_))
        psl = lambda n_, s_, t_: ctx.enter_context(nc.psum_tensor(n_, s_, t_))
        sem = lambda n_: ctx.enter_context(nc.semaphore(n_))

        big = sb("p3_big", [P, ND * T], BF16)
        wbuf = sb("p3_wbuf", [P, 2 * D], BF16)
        qtile = sb("p3_qtile", [P, 2 * T], BF16)
        eyeb = sb("p3_eyeb", [P, P], BF16)
        eyef = sb("p3_eyef", [P, P], F32)
        lnepsb = sb("p3_lneps", [P, 1], F32)
        mvb = sb("p3_mv", [P, 2], F32)
        bn6 = sb("p3_bn6", [P, 24], F32)
        lrstd = sb("p3_lrstd", [P, 1], F32)
        bqb = sb("p3_bqb", [P, 16], F32)
        bkb = sb("p3_bkb", [P, 16], F32)
        bvb = sb("p3_bvb", [P, 16], F32)
        bob = sb("p3_bob", [P, 16], F32)

        psA = [psl(f"p3_psA{i}", [P, CH], F32) for i in range(2)]
        ptrB = [psl(f"p3_ptrB{i}", [P, 8 * P], BF16) for i in range(2)]
        psS = psl("p3_psS", [P, T], F32)

        dA = sem("p3_dA")
        dX = [sem("p3_dX0"), sem("p3_dX1")]
        rm1 = sem("p3_rm1")
        rm1b = sem("p3_rm1b")
        rm2 = sem("p3_rm2")
        xnv = sem("p3_xnv")
        trA = sem("p3_trA")
        evA = sem("p3_evA")
        wld = [sem("p3_wld0"), sem("p3_wld1")]
        mm = sem("p3_mm")
        ev = sem("p3_ev")
        dQ = [sem("p3_dQ0"), sem("p3_dQ1")]
        dVs = [sem("p3_dVs0"), sem("p3_dVs1")]
        dVr = [sem("p3_dVr0"), sem("p3_dVr1")]
        trV = sem("p3_trV")
        evV = sem("p3_evV")
        qld = [sem("p3_qld0"), sem("p3_qld1")]
        mmS = sem("p3_mmS")
        aux1 = sem("p3_aux1")
        sS = sem("p3_sS")
        sE = sem("p3_sE")
        trP = sem("p3_trP")
        evP = sem("p3_evP")
        rc = sem("p3_rc")
        mmO = sem("p3_mmO")
        evO = sem("p3_evO")
        dOs = [sem("p3_dOs0"), sem("p3_dOs1")]
        dOl = [sem("p3_dOl0"), sem("p3_dOl1")]
        trD = sem("p3_trD")
        evD = sem("p3_evD")
        wld2 = [sem("p3_wld20"), sem("p3_wld21")]
        mmD = sem("p3_mmD")
        evE = sem("p3_evE")
        trC = sem("p3_trC")
        addC = sem("p3_addC")
        lnv = sem("p3_lnv")
        lnr = sem("p3_lnr")
        lnx = sem("p3_lnx")
        lnw = sem("p3_lnw")
        dC = sem("p3_dC")
        dR = sem("p3_dR")
        msc = sem("p3_msc")
        qdS = [sem("p3_qdS0"), sem("p3_qdS1")]
        qdS2 = [sem("p3_qdS20"), sem("p3_qdS21")]
        qsc = sem("p3_qsc")
        qv = sem("p3_qv")
        qdW = [sem("p3_qdW0"), sem("p3_qdW1")]
        qthr = sem("p3_qthr")

        with ExitStack() as bctx:
            sbb = lambda n_, s_, t_: bctx.enter_context(nc.sbuf_tensor(n_, s_, t_))
            Kreg = sbb("p3_K", [P, ND * T], BF16)
            vtile = sbb("p3_vtile", [P, 2 * T], BF16)
            if qargs is not None:
                qstrip = sbb("p3_qstrip", [P, 2 * QH], F32)
                qpb = sbb("p3_qpb", [P, QPT], F32)

            with ExitStack() as actx:
                sba = lambda n_, s_, t_: actx.enter_context(
                    nc.sbuf_tensor(n_, s_, t_))
                xtm2 = sba("p3_xtm2", [P, 2 * D], F32)
                junk = sba("p3_junk", [P, D], F32)
                xtmb = sba("p3_xtmb", [P, D], BF16)
                nwb = sba("p3_nwb", [P, D], F32)
                rmsepsb = sba("p3_rmseps", [P, 1], F32)
                ssq = sba("p3_ssq", [P, 1], F32)
                rtmp = sba("p3_rtmp", [P, 1], F32)
                rstd = sba("p3_rstd", [P, 1], F32)

                with nc.Block() as block:
                    @block.sync
                    def _(sy):
                        sy.dma_start(eyef[:, :], eye32[:, :]).then_inc(dA, 16)
                        sy.dma_start(eyeb[:, :], eye16[:, :]).then_inc(dA, 16)
                        sy.dma_start(
                            nwb[:, :],
                            bass.AP(nw, 0, [[0, P], [1, D]])).then_inc(dA, 16)
                        sy.dma_start(bqb[:, :], bq[:, :]).then_inc(dA, 16)
                        sy.dma_start(bkb[:, :], bk[:, :]).then_inc(dA, 16)
                        sy.dma_start(bvb[:, :], bv[:, :]).then_inc(dA, 16)
                        sy.dma_start(bob[:, :], bo[:, :]).then_inc(dA, 16)
                        for tt in range(NT):
                            if tt >= 2:
                                sy.wait_ge(xnv, tt - 1)
                            sy.dma_start(
                                xtm2[:, (tt % 2) * D:(tt % 2 + 1) * D],
                                R2[tt * P:(tt + 1) * P, :],
                            ).then_inc(dX[tt % 2], 16)
                        for dt in range(ND):
                            sy.wait_ge(ev, 4 * dt + 4)
                            sy.dma_start(
                                qs[dt, :, :],
                                qtile[:, (dt % 2) * T:(dt % 2 + 1) * T],
                            ).then_inc(dQ[dt % 2], 16)
                        for dt in range(ND):
                            sy.wait_ge(ev, 4 * (32 + dt) + 4)
                            sy.dma_start(
                                vs[dt, :, :],
                                vtile[:, (dt % 2) * T:(dt % 2 + 1) * T],
                            ).then_inc(dVs[dt % 2], 16)
                        sy.wait_ge(dQ[0], 16 * 8)
                        sy.wait_ge(dQ[1], 16 * 8)
                        sy.wait_ge(dVs[0], 16 * 8)
                        sy.wait_ge(dVs[1], 16 * 8)

                    @block.gpsimd
                    def _(g):
                        g.memset(rmsepsb[:, :], RMS_EPS).then_inc(msc, 1)
                        g.memset(lnepsb[:, :], LN_EPS).then_inc(msc, 1)
                        for grp, wsrc in ((0, wq), (1, wk), (2, wv)):
                            for dt in range(ND):
                                s = grp * ND + dt
                                if s >= 2:
                                    g.wait_ge(mm, 4 * (s - 1))
                                g.dma_start(
                                    wbuf[:, (s % 2) * D:(s % 2 + 1) * D],
                                    wsrc[dt, :, :]).then_inc(wld[s % 2], 16)
                                if qargs is not None:
                                    for i in q_batch(s, NB):
                                        if i >= 2:
                                            g.wait_ge(qsc, i - 1)
                                        g.dma_start(
                                            qstrip[:, (i % 2) * QH:
                                                   (i % 2 + 1) * QH],
                                            q_src(i)).then_inc(qdS[i % 2], 16)

                    @block.tensor
                    def _(t):
                        t.wait_ge(dA, 16 * 7)
                        for tt in range(NT):
                            t.wait_ge(xnv, tt + 1)
                            if tt >= 1:
                                t.wait_ge(evA, 2 * tt)
                            for dt in range(ND):
                                inst = t.transpose(
                                    ptrB[dt // 8][:, (dt % 8) * P:
                                                  (dt % 8 + 1) * P],
                                    xtmb[:, dt * P:(dt + 1) * P], eyeb[:, :])
                                if dt % 8 == 7:
                                    inst.then_inc(trA, 1)
                        t.wait_ge(evA, 2 * NT)
                        for s in range(NB):
                            t.wait_ge(wld[s % 2], 16 * (s // 2 + 1))
                            for c in range(NCH):
                                m = 4 * s + c
                                if m >= 2:
                                    t.wait_ge(ev, m - 1)
                                for kt in range(ND):
                                    inst = t.matmul(
                                        psA[m % 2][:, :],
                                        wbuf[:, (s % 2) * D + kt * P:
                                             (s % 2) * D + (kt + 1) * P],
                                        big[:, kt * T + c * CH:
                                            kt * T + (c + 1) * CH],
                                        start=(kt == 0), stop=(kt == ND - 1))
                                inst.then_inc(mm, 1)

                    @block.scalar
                    def _(s_):
                        s_.wait_ge(msc, 2)
                        for tt in range(NT):
                            s_.wait_ge(dX[tt % 2], 16 * (tt // 2 + 1))
                            if tt >= 1:
                                s_.wait_ge(xnv, tt)
                            s_.activation(
                                junk[:, :],
                                xtm2[:, (tt % 2) * D:(tt % 2 + 1) * D],
                                AF.Square,
                                accum_out=ssq[:, :]).then_inc(rm1, 1)
                            s_.wait_ge(rm1, tt + 1)
                            s_.activation(rtmp[:, :], ssq[:, :], AF.Ln,
                                          bias=rmsepsb[:, :],
                                          scale=1.0 / D).then_inc(rm1b, 1)
                            s_.wait_ge(rm1b, tt + 1)
                            s_.activation(rstd[:, :], rtmp[:, :], AF.Exp,
                                          scale=-0.5).then_inc(rm2, 1)
                        for grp in range(3):
                            for dt in range(ND):
                                s = grp * ND + dt
                                if grp == 0 and dt >= 2:
                                    s_.wait_ge(dQ[dt % 2], 16 * (dt // 2))
                                if grp == 2 and dt >= 2:
                                    s_.wait_ge(dVs[dt % 2], 16 * (dt // 2))
                                bias = (bqb, bkb, bvb)[grp]
                                for c in range(NCH):
                                    m = 4 * s + c
                                    s_.wait_ge(mm, m + 1)
                                    if grp == 0:
                                        dst = qtile[:, (dt % 2) * T + c * CH:
                                                    (dt % 2) * T + (c + 1) * CH]
                                    elif grp == 1:
                                        dst = Kreg[:, dt * T + c * CH:
                                                   dt * T + (c + 1) * CH]
                                    else:
                                        dst = vtile[:, (dt % 2) * T + c * CH:
                                                    (dt % 2) * T + (c + 1) * CH]
                                    s_.activation(
                                        dst, psA[m % 2][:, :], AF.Identity,
                                        bias=bias[:, dt:dt + 1]).then_inc(ev, 1)
                                if qargs is not None:
                                    for i in q_batch(s, NB):
                                        s_.wait_ge(qdS[i % 2],
                                                   16 * (i // 2 + 1))
                                        s_.activation(
                                            qstrip[:, (i % 2) * QH:
                                                   (i % 2 + 1) * QH],
                                            qstrip[:, (i % 2) * QH:
                                                   (i % 2 + 1) * QH],
                                            AF.Abs,
                                            accum_out=qpb[:, i:i + 1],
                                        ).then_inc(qsc, 1)

                    @block.vector
                    def _(v):
                        v.wait_ge(dA, 16 * 7)
                        for tt in range(NT):
                            v.wait_ge(rm2, tt + 1)
                            if tt >= 1:
                                v.wait_ge(trA, 2 * tt)
                            v.scalar_tensor_tensor(
                                xtmb[:, :],
                                xtm2[:, (tt % 2) * D:(tt % 2 + 1) * D],
                                rstd[:, :], nwb[:, :],
                                op0=ALU.mult, op1=ALU.mult).then_inc(xnv, 1)
                            for half in range(2):
                                v.wait_ge(trA, 2 * tt + half + 1)
                                v.tensor_copy(
                                    bass.AP(big, half * 8 * T + tt * P,
                                            [[ND * T, P], [T, 8], [1, P]]),
                                    ptrB[half][:, :]).then_inc(evA, 1)

            if upto < 2:
                return nc
            Psb = sbb("p3_Psb", [P, T], BF16)
            PTsb = sbb("p3_PTsb", [P, T], BF16)
            qblk = sbb("p3_qblk", [P, 2 * T], BF16)
            Otile = sbb("p3_Otile", [P, 2 * D], BF16)
            maskb = sbb("p3_maskb", [P, 4 * CH], F32)
            if qargs is not None:
                qred = sbb("p3_qred", [P, 2], F32)
                qtpq = sbb("p3_qtpq", [1, 2], F32)
                qth1 = sbb("p3_qth1", [P, 2], F32)
                qth2 = sbb("p3_qth2", [P, 2], F32)
                qab = sbb("p3_qab", [P, QH], BF16)
                qbb = sbb("p3_qbb", [P, QH], BF16)
                qqb = sbb("p3_qqb", [P, 2 * QH], BF16)
                qonesf = sbb("p3_qonesf", [P, 1], F32)
                qonesr = sbb("p3_qonesr", [1, P], F32)
            rm = sbb("p3_rm", [P, 1], F32)
            nm = sbb("p3_nm", [P, 1], F32)
            lsum = sbb("p3_lsum", [P, 1], F32)
            linv = sbb("p3_linv", [P, 1], F32)
            psO = psA[0][:, 0:P]

            with nc.Block() as block:
                @block.sync
                def _(sy):
                    for v_ in range(4):
                        sy.dma_start(maskb[:, v_ * CH:(v_ + 1) * CH],
                                     masks[v_, :, :]).then_inc(dA, 16)
                    for dt in range(ND):
                        if dt >= 2:
                            sy.wait_ge(trV, 2 * (dt - 1))
                        sy.dma_start(
                            vtile[:, (dt % 2) * T:(dt % 2 + 1) * T],
                            vs[dt, :, :]).then_inc(dVr[dt % 2], 16)
                    if att != "none":
                        for qb in range(NT):
                            if qb >= 2:
                                sy.wait_ge(mmS, 16 * (qb - 1))
                            sy.dma_start(
                                qblk[:, (qb % 2) * T:(qb % 2 + 1) * T],
                                bass.AP(qs, qb * P,
                                        [[T, P], [P * T, 16], [1, P]]),
                            ).then_inc(qld[qb % 2], 16)
                            if qargs is not None:
                                for i in q_batch(qb, NT):
                                    if i < 2:
                                        pass
                                    else:
                                        sy.wait_ge(qv, 3 * (i - 1))
                                    sy.dma_start(
                                        qstrip[:, (i % 2) * QH:
                                               (i % 2 + 1) * QH],
                                        q_src(i)).then_inc(qdS2[i % 2], 16)
                                    sy.wait_ge(qv, 3 * i + 3)
                                    sy.dma_start(
                                        q_dst(i),
                                        qqb[:, (i % 2) * QH:(i % 2 + 1) * QH],
                                    ).then_inc(qdW[i % 2], 16)
                    if qargs is not None:
                        sy.wait_ge(qdW[0], 16 * 128)
                        sy.wait_ge(qdW[1], 16 * 128)
                    if att == "full":
                        sy.wait_ge(dOs[0], 16 * 8)
                        sy.wait_ge(dOs[1], 16 * 8)

                @block.gpsimd
                def _(g):
                    if qargs is not None:
                        g.memset(qonesf[:, :], 1.0).then_inc(qthr, 1)
                        g.memset(qonesr[:, :], 1.0).then_inc(qthr, 1)

                @block.tensor
                def _(t):
                    t.wait_ge(dA, 16 * 11)
                    if qargs is not None:
                        t.wait_ge(qthr, 4)
                        t.matmul(psA[1][0:1, 0:1], qonesf[:, :],
                                 qred[:, 0:1], start=True, stop=True)
                        t.matmul(psA[1][0:1, 1:2], qonesf[:, :],
                                 qred[:, 1:2], start=True,
                                 stop=True).then_inc(qthr, 1)
                        t.wait_ge(qthr, 6)
                        t.matmul(psA[1][:, 4:6], qonesr[:, :], qtpq[:, :],
                                 start=True, stop=True).then_inc(qthr, 1)
                    for dt in range(ND):
                        t.wait_ge(dVr[dt % 2], 16 * (dt // 2 + 1))
                        if dt >= 1:
                            t.wait_ge(evV, 2 * dt)
                        for tt in range(NT):
                            inst = t.transpose(
                                ptrB[tt // 8][:, (tt % 8) * P:(tt % 8 + 1) * P],
                                vtile[:, (dt % 2) * T + tt * P:
                                      (dt % 2) * T + (tt + 1) * P],
                                eyeb[:, :])
                            if tt % 8 == 7:
                                inst.then_inc(trV, 1)
                    for qb in (range(NT) if att != "none" else []):
                        t.wait_ge(qld[qb % 2], 16 * (qb // 2 + 1))
                        if qb == 0:
                            t.wait_ge(evV, 2 * NT)
                        nchk = qb // 4 + 1
                        row = (qb + 1) * P
                        for h in range(ND):
                            gidx = qb * ND + h
                            t.wait_ge(sE, gidx)
                            for c in range(nchk):
                                w = min(CH, row - c * CH)
                                inst = t.matmul(
                                    psS[:, c * CH:c * CH + w],
                                    qblk[:, (qb % 2) * T + h * P:
                                         (qb % 2) * T + (h + 1) * P],
                                    Kreg[:, h * T + c * CH:h * T + c * CH + w],
                                    start=True, stop=True)
                            inst.then_inc(mmS, 1)
                            if att == "s":
                                continue
                            t.wait_ge(sE, gidx + 1)
                            t.wait_ge(evP, gidx)
                            for j in range(qb + 1):
                                inst = t.transpose(
                                    ptrB[j // 8][:, (j % 8) * P:(j % 8 + 1) * P],
                                    Psb[:, j * P:(j + 1) * P], eyeb[:, :])
                            inst.then_inc(trP, 1)
                            if att == "pt":
                                continue
                            t.wait_ge(evP, gidx + 1)
                            if att == "full":
                                t.wait_ge(evO, gidx)
                            for j in range(qb + 1):
                                inst = t.matmul(
                                    psO,
                                    PTsb[:, j * P:(j + 1) * P],
                                    big[:, j * T + h * P:j * T + (h + 1) * P],
                                    start=(j == 0), stop=(j == qb))
                            inst.then_inc(mmO, 1)

                @block.scalar
                def _(s_):
                    if qargs is not None:
                        s_.wait_ge(qthr, 5)
                        s_.activation(qtpq[:, :], psA[1][0:1, 0:2], AF.Copy,
                                      scale=0.7 / 16777216.0).then_inc(qthr, 1)
                        s_.wait_ge(qthr, 7)
                        s_.activation(qth1[:, 0:1], psA[1][:, 4:5], AF.Copy)
                        s_.activation(qth2[:, 0:1], psA[1][:, 5:6], AF.Copy)
                        s_.activation(qth1[:, 1:2], psA[1][:, 4:5], AF.Copy,
                                      scale=-1.0)
                        s_.activation(qth2[:, 1:2], psA[1][:, 5:6], AF.Copy,
                                      scale=-1.0).then_inc(qthr, 1)
                    for qb in (range(NT) if att != "none" else []):
                        row = (qb + 1) * P
                        for h in range(ND):
                            gidx = qb * ND + h
                            s_.wait_ge(sS, gidx + 1)
                            if att == "full":
                                s_.wait_ge(rc, gidx)
                            s_.activation(Psb[:, 0:row], psS[:, 0:row],
                                          AF.Exp, bias=nm[:, :],
                                          scale=QK_SCALE,
                                          accum_out=lsum[:, :]).then_inc(sE, 1)
                            if att == "s":
                                continue
                            s_.wait_ge(trP, gidx + 1)
                            nbk = 1 if qb < 8 else 2
                            for bk_ in range(nbk):
                                wpt = min(8 * P, row - bk_ * 8 * P)
                                inst = s_.activation(
                                    PTsb[:, bk_ * 8 * P:bk_ * 8 * P + wpt],
                                    ptrB[bk_][:, 0:wpt], AF.Copy)
                            inst.then_inc(evP, 1)
                            if att in ("pt", "pv"):
                                continue
                            s_.wait_ge(mmO, gidx + 1)
                            s_.wait_ge(rc, gidx + 1)
                            if h == 0 and qb >= 2:
                                s_.wait_ge(dOs[qb % 2], 16 * (qb // 2))
                            s_.activation(
                                Otile[:, (qb % 2) * D + h * P:
                                      (qb % 2) * D + (h + 1) * P],
                                psO, AF.Identity,
                                scale=linv[:, :]).then_inc(evO, 1)
                            if h == ND - 1:
                                s_.wait_ge(evO, 16 * (qb + 1))
                                s_.dma_start(
                                    os_[qb, :, :],
                                    Otile[:, (qb % 2) * D:(qb % 2 + 1) * D],
                                ).then_inc(dOs[qb % 2], 16)

                @block.vector
                def _(v):
                    if qargs is not None:
                        v.wait_ge(qsc, 128)
                        v.reduce_sum(qred[:, 0:1], qpb[:, 0:128],
                                     axis=AX.X).then_inc(qthr, 1)
                        v.wait_ge(qsc, QPT)
                        v.reduce_sum(qred[:, 1:2], qpb[:, 128:QPT],
                                     axis=AX.X).then_inc(qthr, 1)
                    for dt in range(ND):
                        for half in range(2):
                            v.wait_ge(trV, 2 * dt + half + 1)
                            v.tensor_copy(
                                bass.AP(big, half * 8 * T + dt * P,
                                        [[ND * T, P], [T, 8], [1, P]]),
                                ptrB[half][:, :]).then_inc(evV, 1)
                    for qb in (range(NT) if att != "none" else []):
                        ci = qb // 4
                        vvar = qb % 4
                        wd_ = (vvar + 1) * P
                        row = (qb + 1) * P
                        for h in range(ND):
                            gidx = qb * ND + h
                            v.wait_ge(mmS, gidx + 1)
                            dst = psS[:, ci * CH:ci * CH + wd_]
                            v.tensor_tensor(
                                dst, dst,
                                maskb[:, vvar * CH:vvar * CH + wd_],
                                op=ALU.add).then_inc(aux1, 1)
                            v.wait_ge(aux1, 2 * gidx + 1)
                            v.reduce_max(rm[:, :], psS[:, 0:row], axis=AX.X,
                                         negate=True).then_inc(aux1, 1)
                            v.wait_ge(aux1, 2 * gidx + 2)
                            v.tensor_scalar(nm[:, :], rm[:, :], QK_SCALE,
                                            None, op0=ALU.mult).then_inc(sS, 1)
                            if att != "full":
                                continue
                            v.wait_ge(sE, gidx + 1)
                            if gidx >= 1:
                                v.wait_ge(evO, gidx)
                            v.reciprocal(linv[:, :],
                                         lsum[:, :]).then_inc(rc, 1)
                        if qargs is not None:
                            if qb == 0:
                                v.wait_ge(qthr, 8)
                            for i in q_batch(qb, NT):
                                tcol = qth1 if i < 128 else qth2
                                v.wait_ge(qdS2[i % 2], 16 * (i // 2 + 1))
                                if i >= 2:
                                    v.wait_ge(qdW[i % 2], 16 * (i // 2))
                                v.tensor_scalar(
                                    qab[:, :],
                                    qstrip[:, (i % 2) * QH:(i % 2 + 1) * QH],
                                    tcol[:, 0:1], None,
                                    op0=ALU.is_ge).then_inc(qv, 1)
                                v.tensor_scalar(
                                    qbb[:, :],
                                    qstrip[:, (i % 2) * QH:(i % 2 + 1) * QH],
                                    tcol[:, 1:2], None,
                                    op0=ALU.is_le).then_inc(qv, 1)
                                v.wait_ge(qv, 3 * i + 2)
                                v.tensor_tensor(
                                    qqb[:, (i % 2) * QH:(i % 2 + 1) * QH],
                                    qab[:, :], qbb[:, :],
                                    op=ALU.subtract).then_inc(qv, 1)

        if upto < 3:
            return nc
        with ExitStack() as dctx:
            sbd = lambda n_, s_, t_: dctx.enter_context(nc.sbuf_tensor(n_, s_, t_))
            sumh = sbd("p3_sumh", [P, 8 * D], F32)
            aT = sbd("p3_aT", [P, 2 * 1024], F32)
            psD = psA
            psT = [psS[:, 0:CH], psS[:, CH:2 * CH]]

            with nc.Block() as block:
                @block.sync
                def _(sy):
                    for tt in range(NT):
                        if tt >= 2:
                            sy.wait_ge(trD, 2 * (tt - 1))
                        sy.dma_start(
                            qtile[:, (tt % 2) * T:(tt % 2 + 1) * T],
                            os_[tt, :, :]).then_inc(dOl[tt % 2], 16)
                    if d_lvl == "full":
                        for th in range(2):
                            if th == 1:
                                sy.wait_ge(dR, 16 * 8)
                            for tl in range(8):
                                sy.dma_start(
                                    sumh[:, tl * D:(tl + 1) * D],
                                    R2[(th * 8 + tl) * P:
                                       (th * 8 + tl + 1) * P, :],
                                ).then_inc(dC, 16)
                            for tl in range(8):
                                gt = th * 8 + tl
                                sy.wait_ge(lnw, gt + 1)
                                sy.dma_start(
                                    R3[gt * P:(gt + 1) * P, :],
                                    sumh[:, tl * D:(tl + 1) * D],
                                ).then_inc(dR, 16)
                        sy.wait_ge(dR, 16 * NT)

                @block.gpsimd
                def _(g):
                    for th in (range(2) if d_lvl != "ot" else []):
                        for dt in range(ND):
                            s2 = th * ND + dt
                            if s2 >= 2:
                                g.wait_ge(mmD, 2 * (s2 - 1))
                            g.dma_start(
                                wbuf[:, (s2 % 2) * D:(s2 % 2 + 1) * D],
                                wo[dt, :, :]).then_inc(wld2[s2 % 2], 16)

                @block.tensor
                def _(t):
                    for tt in range(NT):
                        t.wait_ge(dOl[tt % 2], 16 * (tt // 2 + 1))
                        if tt >= 1:
                            t.wait_ge(evD, 2 * tt)
                        for dt in range(ND):
                            inst = t.transpose(
                                ptrB[dt // 8][:, (dt % 8) * P:(dt % 8 + 1) * P],
                                qtile[:, (tt % 2) * T + dt * P:
                                      (tt % 2) * T + (dt + 1) * P],
                                eyeb[:, :])
                            if dt % 8 == 7:
                                inst.then_inc(trD, 1)
                    if d_lvl != "ot":
                        t.wait_ge(evD, 2 * NT)
                        for th in range(2):
                            for dt in range(ND):
                                s2 = th * ND + dt
                                t.wait_ge(wld2[s2 % 2], 16 * (s2 // 2 + 1))
                                for c in range(2):
                                    m = 2 * s2 + c
                                    if m >= 2:
                                        t.wait_ge(evE, m - 1)
                                    for kt in range(ND):
                                        inst = t.matmul(
                                            psD[m % 2][:, :],
                                            wbuf[:, (s2 % 2) * D + kt * P:
                                                 (s2 % 2) * D + (kt + 1) * P],
                                            big[:, kt * T + th * 1024 + c * CH:
                                                kt * T + th * 1024 +
                                                (c + 1) * CH],
                                            start=(kt == 0),
                                            stop=(kt == ND - 1))
                                    inst.then_inc(mmD, 1)
                                if d_lvl == "mm":
                                    continue
                                t.wait_ge(evE, 2 * s2 + 2)
                                for pk in range(2):
                                    if s2 >= 1:
                                        t.wait_ge(addC, 2 * (s2 - 1) + pk + 1)
                                    for tq4 in range(4):
                                        tq = pk * 4 + tq4
                                        inst = t.transpose(
                                            psT[pk][:, tq4 * P:(tq4 + 1) * P],
                                            aT[:, (dt % 2) * 1024 + tq * P:
                                               (dt % 2) * 1024 + (tq + 1) * P],
                                            eyef[:, :])
                                    inst.then_inc(trC, 1)

                @block.scalar
                def _(s_):
                    for th in (range(2) if d_lvl != "ot" else []):
                        for dt in range(ND):
                            s2 = th * ND + dt
                            if s2 >= 2 and d_lvl == "full":
                                s_.wait_ge(trC, 2 * (s2 - 1))
                            for c in range(2):
                                m = 2 * s2 + c
                                s_.wait_ge(mmD, m + 1)
                                s_.activation(
                                    aT[:, (dt % 2) * 1024 + c * CH:
                                       (dt % 2) * 1024 + (c + 1) * CH],
                                    psD[m % 2][:, :], AF.Identity,
                                    bias=bob[:, dt:dt + 1]).then_inc(evE, 1)
                        if d_lvl == "mm":
                            continue
                        for tl in range(8):
                            gt = th * 8 + tl
                            s_.wait_ge(lnv, 2 * (gt + 1))
                            s_.activation(lrstd[:, :], mvb[:, 1:2], AF.Ln,
                                          bias=lnepsb[:, :]).then_inc(lnx, 1)
                            s_.wait_ge(lnx, gt + 1)
                            s_.activation(lrstd[:, :], lrstd[:, :], AF.Exp,
                                          scale=-0.5).then_inc(lnr, 1)

                @block.vector
                def _(v):
                    for tt in range(NT):
                        for half in range(2):
                            v.wait_ge(trD, 2 * tt + half + 1)
                            v.tensor_copy(
                                bass.AP(big, half * 8 * T + tt * P,
                                        [[ND * T, P], [T, 8], [1, P]]),
                                ptrB[half][:, :]).then_inc(evD, 1)
                    for th in (range(2) if d_lvl == "full" else []):
                        v.wait_ge(dC, 16 * 8 * (th + 1))
                        for dt in range(ND):
                            s2 = th * ND + dt
                            for pk in range(2):
                                v.wait_ge(trC, 2 * s2 + pk + 1)
                                dst = bass.AP(sumh, pk * 4 * D + dt * P,
                                              [[8 * D, P], [D, 4], [1, P]])
                                v.tensor_tensor(dst, psT[pk][:, :], dst,
                                                op=ALU.add).then_inc(addC, 1)
                        v.wait_ge(addC, 32 * (th + 1))
                        for tl in range(8):
                            gt = th * 8 + tl
                            tile = sumh[:, tl * D:(tl + 1) * D]
                            for c4 in range(4):
                                inst = v.bn_stats(
                                    bn6[:, c4 * 6:(c4 + 1) * 6],
                                    tile[:, c4 * CH:(c4 + 1) * CH])
                            inst.then_inc(lnv, 1)
                            v.wait_ge(lnv, 2 * gt + 1)
                            v.bn_aggr(mvb[:, :], bn6[:, :]).then_inc(lnv, 1)
                            v.wait_ge(lnr, gt + 1)
                            v.tensor_scalar(tile, tile, mvb[:, 0:1],
                                            lrstd[:, :], op0=ALU.subtract,
                                            op1=ALU.mult).then_inc(lnw, 1)
    return nc


def host_prep_p3(wq, wk, wv, wo, bq, bk, bv, bo):
    sl = lambda W: slab_weights(np.asarray(W, np.float32).T, osz=P)
    rb = lambda b: np.ascontiguousarray(
        np.asarray(b, np.float32).reshape(16, P).T)
    masks = np.zeros((4, P, CH), np.float32)
    for v_ in range(4):
        r = np.arange(P)[:, None]
        c = np.arange(CH)[None, :]
        masks[v_] = np.where(c <= v_ * P + r, 0.0, -1e30)
    return (sl(wq), sl(wk), sl(wv), sl(wo),
            rb(bq), rb(bk), rb(bv), rb(bo), masks)


def _mk_phase3_nc(upto=3, att="full", d_lvl="full"):
    nc = bass.Bass("TRN2", target_bir_lowering=False)
    R2 = nc.dram_tensor("R2", [T, D], F32, kind="ExternalInput")
    wq = nc.dram_tensor("wq", [16, P, D], F32, kind="ExternalInput")
    wk = nc.dram_tensor("wk", [16, P, D], F32, kind="ExternalInput")
    wv = nc.dram_tensor("wv", [16, P, D], F32, kind="ExternalInput")
    wo = nc.dram_tensor("wo", [16, P, D], F32, kind="ExternalInput")
    nw = nc.dram_tensor("nw", [D], F32, kind="ExternalInput")
    bq = nc.dram_tensor("bq", [P, 16], F32, kind="ExternalInput")
    bk = nc.dram_tensor("bk", [P, 16], F32, kind="ExternalInput")
    bv = nc.dram_tensor("bv", [P, 16], F32, kind="ExternalInput")
    bo = nc.dram_tensor("bo", [P, 16], F32, kind="ExternalInput")
    masks = nc.dram_tensor("masks", [4, P, CH], F32, kind="ExternalInput")
    eye32 = nc.dram_tensor("eye32", [P, P], F32, kind="ExternalInput")
    eye16 = nc.dram_tensor("eye16", [P, P], BF16, kind="ExternalInput")
    R3 = nc.dram_tensor("R3", [T, D], F32, kind="ExternalOutput")
    build_phase3(nc, R2, wq, wk, wv, wo, nw, bq, bk, bv, bo, masks,
                 eye32, eye16, R3, upto=upto, att=att, d_lvl=d_lvl)
    return nc


def run_phase3(R2_full, norm_w, wq, bq, wk, bk, wv, bv, wo, bo,
               cores=N_CORES, upto=3, att="full", d_lvl="full"):
    wqs, wks, wvs, wos, bqb, bkb, bvb, bob, masks = host_prep_p3(
        wq, wk, wv, wo, bq, bk, bv, bo)
    nc = _mk_phase3_nc(upto=upto, att=att, d_lvl=d_lvl)
    in_maps = []
    for c in range(cores):
        in_maps.append(dict(
            R2=np.ascontiguousarray(R2_full[c], np.float32),
            wq=wqs, wk=wks, wv=wvs, wo=wos,
            nw=np.ascontiguousarray(norm_w, np.float32),
            bq=bqb, bk=bkb, bv=bvb, bo=bob, masks=masks,
            eye32=_EYE32, eye16=_EYE16))
    res = run_bass_kernel_spmd(nc, in_maps, core_ids=list(range(cores)))
    return np.stack([r["R3"] for r in res.results])




def build_phase4(nc, R3, w1t, w2t, b1, b2, eye32, eye16, OUT,
                 do_quant=True, w1q=None, w2q=None):
    _clear_sems(nc)
    if w1q is None:
        w1q = nc.dram_tensor("p4_w1q", [64, P, 2048], BF16)
        w2q = nc.dram_tensor("p4_w2q", [2, 16, P, 4096], BF16)
    HT = 1024
    NIT = 32
    NP1 = 64
    NPT = 128
    NW1 = 16777216.0
    NW2 = 16777216.0

    def w1_chunk(th, ip, ls, c):
        return th * 192 + ip * 96 + 2 * ls + c

    def w2_chunk(th, ip, dt, c):
        return th * 192 + ip * 96 + 64 + 2 * dt + c

    with ExitStack() as ctx:
        sb = lambda n_, s_, t_: ctx.enter_context(nc.sbuf_tensor(n_, s_, t_))
        psl = lambda n_, s_, t_: ctx.enter_context(nc.psum_tensor(n_, s_, t_))
        sem = lambda n_: ctx.enter_context(nc.semaphore(n_))

        eyeb = sb("p4_eyeb", [P, P], BF16)
        eyef = sb("p4_eyef", [P, P], F32)
        onesf = sb("p4_onesf", [P, 1], F32)
        onesr = sb("p4_onesr", [1, P], F32)
        mvb = sb("p4_mv", [P, 2], F32)
        bn6 = sb("p4_bn6", [P, 24], F32)
        rstd = sb("p4_rstd", [P, 1], F32)
        lnepsb = sb("p4_lneps", [P, 1], F32)
        b1b = sb("p4_b1b", [P, 64], F32)
        b2b = sb("p4_b2b", [P, 16], F32)
        psA = [psl(f"p4_psA{i}", [P, CH], F32) for i in range(4)]
        ptr = [psl(f"p4_ptr{i}", [P, 8 * P], BF16) for i in range(2)]
        psT = [psl(f"p4_psT{i}", [P, CH], F32) for i in range(2)]

        dA = sem("p4_dA")
        dAx = sem("p4_dAx")
        dS = [sem("p4_dS0"), sem("p4_dS1")]
        qsc = sem("p4_qsc")
        qv = sem("p4_qv")
        qg = sem("p4_qg")
        dQw = [sem("p4_dQw0"), sem("p4_dQw1")]
        thr = sem("p4_thr")
        trA = sem("p4_trA")
        evA = sem("p4_evA")
        wld1 = [sem("p4_wld10"), sem("p4_wld11")]
        wld2 = [sem("p4_wld20"), sem("p4_wld21")]
        mm = sem("p4_mm")
        ev = sem("p4_ev")
        trC = sem("p4_trC")
        addC = sem("p4_addC")
        lnv = sem("p4_lnv")
        lnr = sem("p4_lnr")
        lnx = sem("p4_lnx")
        lnw = sem("p4_lnw")
        dC = sem("p4_dC")
        dR = sem("p4_dR")
        msc = sem("p4_msc")

        def src_ap(i):
            if i < NP1:
                return w1t[i, :, :]
            j = i - NP1
            ip, dt, hf = j // 32, (j % 32) // 2, j % 2
            return w2t[ip, dt, :, hf * D:(hf + 1) * D]

        def dst_ap(i):
            if i < NP1:
                return w1q[i, :, :]
            j = i - NP1
            ip, dt, hf = j // 32, (j % 32) // 2, j % 2
            return w2q[ip, dt, :, hf * D:(hf + 1) * D]

        qctx_mgr = ExitStack() if do_quant else None
        if do_quant:
          with ExitStack() as qctx:
            sbq = lambda n_, s_, t_: qctx.enter_context(nc.sbuf_tensor(n_, s_, t_))
            strip = sbq("p4_strip", [P, 2 * D], F32)
            qa = sbq("p4_qa", [P, D], BF16)
            qb_ = sbq("p4_qb", [P, D], BF16)
            qq = sbq("p4_qq", [P, 2 * D], BF16)
            pb = sbq("p4_pb", [P, NPT], F32)
            red = sbq("p4_red", [P, 2], F32)
            tpq = sbq("p4_tpq", [1, 2], F32)
            th1 = sbq("p4_th1", [P, 2], F32)
            th2 = sbq("p4_th2", [P, 2], F32)

            with nc.Block() as block:
                @block.sync
                def _(sy):
                    sy.dma_start(eyef[:, :], eye32[:, :]).then_inc(dA, 16)
                    sy.dma_start(eyeb[:, :], eye16[:, :]).then_inc(dA, 16)
                    sy.dma_start(b1b[:, :], b1[:, :]).then_inc(dA, 16)
                    sy.dma_start(b2b[:, :], b2[:, :]).then_inc(dA, 16)
                    for i in range(NPT):
                        if i >= 2:
                            sy.wait_ge(qsc, i - 1)
                        sy.dma_start(strip[:, (i % 2) * D:(i % 2 + 1) * D],
                                     src_ap(i)).then_inc(dS[i % 2], 16)
                    for i in range(NPT):
                        if i < 2:
                            sy.wait_ge(qsc, NPT)
                        else:
                            sy.wait_ge(qv, 3 * (i - 1))
                        sy.dma_start(strip[:, (i % 2) * D:(i % 2 + 1) * D],
                                     src_ap(i)).then_inc(dS[i % 2], 16)
                        sy.wait_ge(qv, 3 * i + 3)
                        sy.dma_start(dst_ap(i),
                                     qq[:, (i % 2) * D:(i % 2 + 1) * D],
                                     ).then_inc(dQw[i % 2], 16)
                    sy.wait_ge(dQw[0], 16 * 64)
                    sy.wait_ge(dQw[1], 16 * 64)

                @block.gpsimd
                def _(g):
                    g.memset(lnepsb[:, :], LN_EPS).then_inc(msc, 1)
                    g.memset(onesf[:, :], 1.0).then_inc(msc, 1)
                    g.memset(onesr[:, :], 1.0).then_inc(msc, 1)
                @block.scalar
                def _(s_):
                    for i in range(NPT):
                        s_.wait_ge(dS[i % 2], 16 * (i // 2 + 1))
                        s_.activation(
                            strip[:, (i % 2) * D:(i % 2 + 1) * D],
                            strip[:, (i % 2) * D:(i % 2 + 1) * D],
                            AF.Abs,
                            accum_out=pb[:, i:i + 1]).then_inc(qsc, 1)
                    s_.wait_ge(thr, 3)
                    s_.activation(tpq[:, :], psT[0][0:1, 0:2], AF.Copy,
                                  scale=0.7 / NW1).then_inc(thr, 1)
                    s_.wait_ge(thr, 5)
                    s_.activation(th1[:, 0:1], psT[1][:, 0:1], AF.Copy)
                    s_.activation(th2[:, 0:1], psT[1][:, 1:2], AF.Copy)
                    s_.activation(th1[:, 1:2], psT[1][:, 0:1], AF.Copy,
                                  scale=-1.0)
                    s_.activation(th2[:, 1:2], psT[1][:, 1:2], AF.Copy,
                                  scale=-1.0).then_inc(thr, 1)

                @block.vector
                def _(v):
                    v.wait_ge(qsc, NP1)
                    v.reduce_sum(red[:, 0:1], pb[:, 0:NP1],
                                 axis=AX.X).then_inc(thr, 1)
                    v.wait_ge(qsc, NPT)
                    v.reduce_sum(red[:, 1:2], pb[:, NP1:NPT],
                                 axis=AX.X).then_inc(thr, 1)
                    v.wait_ge(thr, 6)
                    for i in range(NPT):
                        tcol = th1 if i < NP1 else th2
                        v.wait_ge(dS[i % 2], 16 * (NPT // 2 + i // 2 + 1))
                        if i >= 2:
                            v.wait_ge(dQw[i % 2], 16 * (i // 2))
                        v.tensor_scalar(
                            qa[:, :],
                            strip[:, (i % 2) * D:(i % 2 + 1) * D],
                            tcol[:, 0:1], None,
                            op0=ALU.is_ge).then_inc(qv, 1)
                        v.tensor_scalar(
                            qb_[:, :],
                            strip[:, (i % 2) * D:(i % 2 + 1) * D],
                            tcol[:, 1:2], None,
                            op0=ALU.is_le).then_inc(qv, 1)
                        v.wait_ge(qv, 3 * i + 2)
                        v.tensor_tensor(qq[:, (i % 2) * D:(i % 2 + 1) * D],
                                        qa[:, :], qb_[:, :],
                                        op=ALU.subtract).then_inc(qv, 1)

                @block.tensor
                def _(t):
                    t.wait_ge(dA, 64)
                    t.wait_ge(msc, 3)
                    t.wait_ge(thr, 1)
                    t.matmul(psT[0][0:1, 0:1], onesf[:, :], red[:, 0:1],
                             start=True, stop=True)
                    t.wait_ge(thr, 2)
                    t.matmul(psT[0][0:1, 1:2], onesf[:, :], red[:, 1:2],
                             start=True, stop=True).then_inc(thr, 1)
                    t.wait_ge(thr, 4)
                    t.matmul(psT[1][:, 0:2], onesr[:, :], tpq[:, :],
                             start=True, stop=True).then_inc(thr, 1)

        with ExitStack() as mctx:
            sbm = lambda n_, s_, t_: mctx.enter_context(nc.sbuf_tensor(n_, s_, t_))
            h = sbm("p4_h", [P, NIT * HT], BF16)
            xT = sbm("p4_xT", [P, ND * HT], BF16)
            sumh = sbm("p4_sumh", [P, 8 * D], F32)
            wbuf1 = sbm("p4_wbuf1", [P, 2 * D], BF16)
            wbuf2 = sbm("p4_wbuf2", [P, 2 * 4096], BF16)
            aT = sbm("p4_aT", [P, 2 * HT], F32)
            xtmb = sbm("p4_xtmb", [P, D], BF16)
            th1b = sbm("p4_th1b", [P, 2], F32)

            with nc.Block() as block:
                @block.sync
                def _(sy):
                    if not do_quant:
                        sy.dma_start(eyef[:, :], eye32[:, :]).then_inc(dA, 16)
                        sy.dma_start(eyeb[:, :], eye16[:, :]).then_inc(dA, 16)
                        sy.dma_start(b1b[:, :], b1[:, :]).then_inc(dA, 16)
                        sy.dma_start(b2b[:, :], b2[:, :]).then_inc(dA, 16)
                    for th in range(2):
                        if th == 1:
                            sy.wait_ge(dR, 16 * 8)
                        for tl in range(8):
                            sy.dma_start(
                                sumh[:, tl * D:(tl + 1) * D],
                                R3[(th * 8 + tl) * P:
                                   (th * 8 + tl + 1) * P, :],
                            ).then_inc(dC, 16)
                        for ip in range(2):
                            for ls in range(NIT):
                                gs = (th * 2 + ip) * NIT + ls
                                if gs >= 2:
                                    g2 = gs - 2
                                    sy.wait_ge(mm, w1_chunk(
                                        g2 // 64, (g2 % 64) // 32,
                                        g2 % 32, 1) + 1)
                                it = ip * NIT + ls
                                sy.dma_start(
                                    wbuf1[:, (gs % 2) * D:(gs % 2 + 1) * D],
                                    w1q[it, :, :],
                                ).then_inc(wld1[gs % 2], 16)
                            for dt in range(ND):
                                ws = (th * 2 + ip) * ND + dt
                                if ws >= 2:
                                    w2_ = ws - 2
                                    sy.wait_ge(mm, w2_chunk(
                                        w2_ // 32, (w2_ % 32) // 16,
                                        w2_ % 16, 1) + 1)
                                sy.dma_start(
                                    wbuf2[:, (ws % 2) * 4096:
                                          (ws % 2 + 1) * 4096],
                                    w2q[ip, dt, :, :],
                                ).then_inc(wld2[ws % 2], 16)
                        for tl in range(8):
                            gt = th * 8 + tl
                            sy.wait_ge(lnw, gt + 1)
                            sy.dma_start(
                                OUT[gt * P:(gt + 1) * P, :],
                                sumh[:, tl * D:(tl + 1) * D],
                            ).then_inc(dR, 16)
                    sy.wait_ge(dR, 16 * NT)

                @block.gpsimd
                def _(g):
                    g.memset(lnepsb[:, :], LN_EPS).then_inc(msc, 1)
                    for tt in range(NT):
                        if tt >= 1:
                            g.wait_ge(trA, 2 * tt)
                        g.dma_start(
                            xtmb[:, :],
                            R3[tt * P:(tt + 1) * P, :]).then_inc(dAx, 16)

                @block.scalar
                def _(s_):
                    s_.wait_ge(dA, 64)
                    for th in range(2):
                        for ip in range(2):
                            for ls in range(NIT):
                                il = ip * NIT + ls
                                for c in range(2):
                                    m = w1_chunk(th, ip, ls, c)
                                    s_.wait_ge(mm, m + 1)
                                    s_.activation(
                                        h[:, ls * HT + c * CH:
                                          ls * HT + (c + 1) * CH],
                                        psA[m % 4][:, :], AF.Gelu,
                                        bias=b1b[:, il:il + 1],
                                    ).then_inc(ev, 1)
                            for dt in range(ND):
                                s2g = (th * 2 + ip) * ND + dt
                                if s2g >= 2:
                                    s_.wait_ge(trC, 2 * (s2g - 1))
                                for c in range(2):
                                    m = w2_chunk(th, ip, dt, c)
                                    s_.wait_ge(mm, m + 1)
                                    if ip == 1:
                                        s_.activation(
                                            aT[:, (dt % 2) * HT + c * CH:
                                               (dt % 2) * HT + (c + 1) * CH],
                                            psA[m % 4][:, :], AF.Identity,
                                            bias=b2b[:, dt:dt + 1],
                                        ).then_inc(ev, 1)
                                    else:
                                        s_.activation(
                                            aT[:, (dt % 2) * HT + c * CH:
                                               (dt % 2) * HT + (c + 1) * CH],
                                            psA[m % 4][:, :],
                                            AF.Copy).then_inc(ev, 1)
                        for tl in range(8):
                            gt = th * 8 + tl
                            s_.wait_ge(lnv, 2 * (gt + 1))
                            if gt == 0:
                                s_.wait_ge(msc, 1)
                            s_.activation(rstd[:, :], mvb[:, 1:2], AF.Ln,
                                          bias=lnepsb[:, :]).then_inc(lnx, 1)
                            s_.wait_ge(lnx, gt + 1)
                            s_.activation(rstd[:, :], rstd[:, :], AF.Exp,
                                          scale=-0.5).then_inc(lnr, 1)

                @block.vector
                def _(v):
                    for th in range(2):
                        for tl in range(8):
                            tt = th * 8 + tl
                            if tt == 8:
                                v.wait_ge(mm, w1_chunk(0, 1, 31, 1) + 1)
                            for half in range(2):
                                v.wait_ge(trA, 2 * tt + half + 1)
                                v.tensor_copy(
                                    bass.AP(xT, half * 8 * HT + tl * P,
                                            [[ND * HT, P], [HT, 8], [1, P]]),
                                    ptr[half][:, :]).then_inc(evA, 1)
                        v.wait_ge(dC, 16 * 8 * (th + 1))
                        for ip in range(2):
                            for dt in range(ND):
                                s2g = (th * 2 + ip) * ND + dt
                                for pk in range(2):
                                    v.wait_ge(trC, 2 * s2g + pk + 1)
                                    dst = bass.AP(sumh,
                                                  pk * 4 * D + dt * P,
                                                  [[8 * D, P], [D, 4],
                                                   [1, P]])
                                    v.tensor_tensor(
                                        dst, psT[pk][:, :], dst,
                                        op=ALU.add).then_inc(addC, 1)
                        v.wait_ge(addC, 64 * (th + 1))
                        for tl in range(8):
                            gt = th * 8 + tl
                            tile = sumh[:, tl * D:(tl + 1) * D]
                            for c4 in range(4):
                                inst = v.bn_stats(
                                    bn6[:, c4 * 6:(c4 + 1) * 6],
                                    tile[:, c4 * CH:(c4 + 1) * CH])
                            inst.then_inc(lnv, 1)
                            v.wait_ge(lnv, 2 * gt + 1)
                            v.bn_aggr(mvb[:, :], bn6[:, :]).then_inc(lnv, 1)
                            v.wait_ge(lnr, gt + 1)
                            v.tensor_scalar(tile, tile, mvb[:, 0:1],
                                            rstd[:, :], op0=ALU.subtract,
                                            op1=ALU.mult).then_inc(lnw, 1)

                @block.tensor
                def _(t):
                    t.wait_ge(dA, 64)
                    for th in range(2):
                        for tl in range(8):
                            tt = th * 8 + tl
                            t.wait_ge(dAx, 16 * (tt + 1))
                            if tt >= 1:
                                t.wait_ge(evA, 2 * tt)
                            for dt in range(ND):
                                inst = t.transpose(
                                    ptr[dt // 8][:, (dt % 8) * P:
                                                 (dt % 8 + 1) * P],
                                    xtmb[:, dt * P:(dt + 1) * P], eyeb[:, :])
                                if dt % 8 == 7:
                                    inst.then_inc(trA, 1)
                        t.wait_ge(evA, 2 * (th * 8 + 8))
                        for ip in range(2):
                            for ls in range(NIT):
                                gs = (th * 2 + ip) * NIT + ls
                                t.wait_ge(wld1[gs % 2], 16 * (gs // 2 + 1))
                                for c in range(2):
                                    m = w1_chunk(th, ip, ls, c)
                                    if m >= 4:
                                        t.wait_ge(ev, m - 3)
                                    for kt in range(ND):
                                        inst = t.matmul(
                                            psA[m % 4][:, :],
                                            wbuf1[:, (gs % 2) * D + kt * P:
                                                  (gs % 2) * D +
                                                  (kt + 1) * P],
                                            xT[:, kt * HT + c * CH:
                                               kt * HT + (c + 1) * CH],
                                            start=(kt == 0),
                                            stop=(kt == ND - 1))
                                    inst.then_inc(mm, 1)
                            t.wait_ge(ev, th * 192 + ip * 96 + 64)
                            for dt in range(ND):
                                ws = (th * 2 + ip) * ND + dt
                                s2g = ws
                                t.wait_ge(wld2[ws % 2], 16 * (ws // 2 + 1))
                                for c in range(2):
                                    m = w2_chunk(th, ip, dt, c)
                                    t.wait_ge(ev, m - 3)
                                    for kt in range(NIT):
                                        inst = t.matmul(
                                            psA[m % 4][:, :],
                                            wbuf2[:, (ws % 2) * 4096 +
                                                  kt * P:
                                                  (ws % 2) * 4096 +
                                                  (kt + 1) * P],
                                            h[:, kt * HT + c * CH:
                                              kt * HT + (c + 1) * CH],
                                            start=(kt == 0),
                                            stop=(kt == NIT - 1))
                                    inst.then_inc(mm, 1)
                                t.wait_ge(ev, w2_chunk(th, ip, dt, 1) + 1)
                                for pk in range(2):
                                    if s2g >= 1:
                                        t.wait_ge(addC,
                                                  2 * (s2g - 1) + pk + 1)
                                    for tq4 in range(4):
                                        tq = pk * 4 + tq4
                                        inst = t.transpose(
                                            psT[pk][:, tq4 * P:
                                                    (tq4 + 1) * P],
                                            aT[:, (dt % 2) * HT + tq * P:
                                               (dt % 2) * HT +
                                               (tq + 1) * P],
                                            eyef[:, :])
                                    inst.then_inc(trC, 1)
    return nc


def host_prep_p4(w1, b1, w2, b2):
    w1t_s = slab_weights(np.asarray(w1, np.float32).T, osz=P)
    W2T = np.asarray(w2, np.float32).T
    w2t_s = np.stack([slab_weights(W2T[ip * 4096:(ip + 1) * 4096, :], osz=P)
                      for ip in range(2)])
    b1b = np.ascontiguousarray(np.asarray(b1, np.float32).reshape(64, P).T)
    b2b = np.ascontiguousarray(np.asarray(b2, np.float32).reshape(16, P).T)
    return w1t_s, np.ascontiguousarray(w2t_s), b1b, b2b


def _mk_phase4_nc():
    nc = bass.Bass("TRN2", target_bir_lowering=False)
    R3 = nc.dram_tensor("R3", [T, D], F32, kind="ExternalInput")
    w1t = nc.dram_tensor("w1t", [64, P, 2048], F32, kind="ExternalInput")
    w2t = nc.dram_tensor("w2t", [2, 16, P, 4096], F32, kind="ExternalInput")
    b1 = nc.dram_tensor("b1", [P, 64], F32, kind="ExternalInput")
    b2 = nc.dram_tensor("b2", [P, 16], F32, kind="ExternalInput")
    eye32 = nc.dram_tensor("eye32", [P, P], F32, kind="ExternalInput")
    eye16 = nc.dram_tensor("eye16", [P, P], BF16, kind="ExternalInput")
    OUT = nc.dram_tensor("OUT", [T, D], F32, kind="ExternalOutput")
    build_phase4(nc, R3, w1t, w2t, b1, b2, eye32, eye16, OUT)
    return nc


def run_phase4(R3_full, w1, b1, w2, b2, cores=N_CORES):
    w1t, w2t, b1b, b2b = host_prep_p4(w1, b1, w2, b2)
    nc = _mk_phase4_nc()
    in_maps = []
    for c in range(cores):
        in_maps.append(dict(R3=np.ascontiguousarray(R3_full[c], np.float32),
                            w1t=w1t, w2t=w2t, b1=b1b, b2=b2b,
                            eye32=_EYE32, eye16=_EYE16))
    res = run_bass_kernel_spmd(nc, in_maps, core_ids=list(range(cores)))
    return np.stack([r["OUT"] for r in res.results])



def _mk_phase1_nc():
    nc = bass.Bass("TRN2", target_bir_lowering=False)
    x = nc.dram_tensor("x", [T, D], F32, kind="ExternalInput")
    wqkv = nc.dram_tensor("wqkv", [48, P, D], F32, kind="ExternalInput")
    wout = nc.dram_tensor("wout", [16, P, D], F32, kind="ExternalInput")
    eye32 = nc.dram_tensor("eye32", [P, P], F32, kind="ExternalInput")
    eye16 = nc.dram_tensor("eye16", [P, P], BF16, kind="ExternalInput")
    R1 = nc.dram_tensor("R1", [T, D], F32, kind="ExternalOutput")
    build_phase1(nc, x, wqkv, wout, eye32, eye16, R1)
    return nc


_EYE32 = np.eye(P, dtype=np.float32)
_EYE16 = _EYE32.astype(ml_dtypes.bfloat16)


def host_prep_p1(W_qkv, W_out):
    WT = np.asarray(W_qkv, np.float32).T
    cols = []
    for i in range(ND):
        cols.append(WT[:, 4096 + i * P:4096 + (i + 1) * P])
        cols.append(WT[:, 2048 + i * P:2048 + (i + 1) * P])
        cols.append(WT[:, 0 + i * P:0 + (i + 1) * P])
    WTp = np.concatenate(cols, axis=1)
    wqkv_s = slab_weights(WTp, osz=P)
    wout_s = slab_weights(np.asarray(W_out, np.float32).T, osz=P)
    return wqkv_s, wout_s


def run_phase1(x_full, W_qkv, W_out, cores=N_CORES):
    wqkv_s, wout_s = host_prep_p1(W_qkv, W_out)
    nc = _mk_phase1_nc()
    in_maps = []
    for c in range(cores):
        in_maps.append(dict(x=np.ascontiguousarray(x_full[c], np.float32),
                            wqkv=wqkv_s, wout=wout_s,
                            eye32=_EYE32, eye16=_EYE16))
    res = run_bass_kernel_spmd(nc, in_maps, core_ids=list(range(cores)))
    return np.stack([r["R1"] for r in res.results])


def _mk_fused_nc(upto=4):
    nc = bass.Bass("TRN2", target_bir_lowering=False)
    x = nc.dram_tensor("x", [T, D], F32, kind="ExternalInput")
    wqkv = nc.dram_tensor("wqkv", [48, P, D], F32, kind="ExternalInput")
    wout = nc.dram_tensor("wout", [16, P, D], F32, kind="ExternalInput")
    wgu = nc.dram_tensor("wgu", [128, P, D], F32, kind="ExternalInput")
    wd = nc.dram_tensor("wd", [2, 16, P, 4096], F32, kind="ExternalInput")
    wq = nc.dram_tensor("wq", [16, P, D], F32, kind="ExternalInput")
    wk = nc.dram_tensor("wk", [16, P, D], F32, kind="ExternalInput")
    wv = nc.dram_tensor("wv", [16, P, D], F32, kind="ExternalInput")
    wo = nc.dram_tensor("wo", [16, P, D], F32, kind="ExternalInput")
    nw = nc.dram_tensor("nw", [D], F32, kind="ExternalInput")
    bq = nc.dram_tensor("bq", [P, 16], F32, kind="ExternalInput")
    bk = nc.dram_tensor("bk", [P, 16], F32, kind="ExternalInput")
    bv = nc.dram_tensor("bv", [P, 16], F32, kind="ExternalInput")
    bo = nc.dram_tensor("bo", [P, 16], F32, kind="ExternalInput")
    masks = nc.dram_tensor("masks", [4, P, CH], F32, kind="ExternalInput")
    w1t = nc.dram_tensor("w1t", [64, P, 2048], F32, kind="ExternalInput")
    w2t = nc.dram_tensor("w2t", [2, 16, P, 4096], F32, kind="ExternalInput")
    b1 = nc.dram_tensor("b1", [P, 64], F32, kind="ExternalInput")
    b2 = nc.dram_tensor("b2", [P, 16], F32, kind="ExternalInput")
    eye32 = nc.dram_tensor("eye32", [P, P], F32, kind="ExternalInput")
    eye16 = nc.dram_tensor("eye16", [P, P], BF16, kind="ExternalInput")
    R1 = nc.dram_tensor("R1", [T, D], F32)
    R2 = nc.dram_tensor("R2", [T, D], F32)
    R3 = nc.dram_tensor("R3", [T, D], F32)
    OUT = nc.dram_tensor("OUT", [T, D], F32, kind="ExternalOutput")
    w1q = nc.dram_tensor("w1q_s", [64, P, 2048], BF16)
    w2q = nc.dram_tensor("w2q_s", [2, 16, P, 4096], BF16)
    build_phase1(nc, x, wqkv, wout, eye32, eye16, R1)
    if upto >= 2:
        build_phase2(nc, R1, wgu, wd, eye32, eye16, R2)
    if upto >= 3:
        build_phase3(nc, R2, wq, wk, wv, wo, nw, bq, bk, bv, bo, masks,
                     eye32, eye16, R3, qargs=(w1t, w2t, w1q, w2q))
    if upto >= 4:
        build_phase4(nc, R3, w1t, w2t, b1, b2, eye32, eye16, OUT,
                     do_quant=False, w1q=w1q, w2q=w2q)
    return nc


def kernel(**inputs):
    x = np.asarray(inputs["L0_attn_x"], np.float32)
    wqkv_s, wout_s = host_prep_p1(inputs["L0_attn_W_qkv"],
                                  inputs["L0_attn_W_out"])
    wgu_s, wd_s = host_prep_p2(inputs["L0_ffn_w_gate_up"],
                               inputs["L0_ffn_w_down"])
    wqs, wks, wvs, wos, bqb, bkb, bvb, bob, masks = host_prep_p3(
        inputs["L1_attn_wq"], inputs["L1_attn_wk"], inputs["L1_attn_wv"],
        inputs["L1_attn_wo"], inputs["L1_attn_bq"], inputs["L1_attn_bk"],
        inputs["L1_attn_bv"], inputs["L1_attn_bo"])
    w1t, w2t, b1b, b2b = host_prep_p4(inputs["L1_ffn_w1"],
                                      inputs["L1_ffn_b1"],
                                      inputs["L1_ffn_w2"],
                                      inputs["L1_ffn_b2"])
    nw = np.ascontiguousarray(np.asarray(inputs["L1_attn_norm_w"],
                                         np.float32))
    nc = _mk_fused_nc()
    in_maps = []
    for c in range(N_CORES):
        in_maps.append(dict(
            x=np.ascontiguousarray(x[c]),
            wqkv=wqkv_s, wout=wout_s, wgu=wgu_s, wd=wd_s,
            wq=wqs, wk=wks, wv=wvs, wo=wos, nw=nw,
            bq=bqb, bk=bkb, bv=bvb, bo=bob, masks=masks,
            w1t=w1t, w2t=w2t, b1=b1b, b2=b2b,
            eye32=_EYE32, eye16=_EYE16))
    res = run_bass_kernel_spmd(nc, in_maps, core_ids=list(range(N_CORES)))
    return np.stack([r["OUT"] for r in res.results]).astype(np.float32)


def kernel_unfused(**inputs):
    x = np.asarray(inputs["L0_attn_x"], np.float32)
    R1 = run_phase1(x, inputs["L0_attn_W_qkv"], inputs["L0_attn_W_out"])
    R2 = run_phase2(R1, inputs["L0_ffn_w_gate_up"], inputs["L0_ffn_w_down"])
    R3 = run_phase3(R2, inputs["L1_attn_norm_w"],
                    inputs["L1_attn_wq"], inputs["L1_attn_bq"],
                    inputs["L1_attn_wk"], inputs["L1_attn_bk"],
                    inputs["L1_attn_wv"], inputs["L1_attn_bv"],
                    inputs["L1_attn_wo"], inputs["L1_attn_bo"])
    out = run_phase4(R3, inputs["L1_ffn_w1"], inputs["L1_ffn_b1"],
                     inputs["L1_ffn_w2"], inputs["L1_ffn_b2"])
    return out.astype(np.float32)


# version 10
# speedup vs baseline: 1.1338x; 1.1338x over previous
import os
import sys

sys.path.insert(0, "/opt/trn_rl_repo")

import numpy as np
import ml_dtypes
import concourse.bass as bass
import concourse.mybir as mybir
from concourse.bass_utils import run_bass_kernel_spmd
from concourse.alu_op_type import AluOpType as ALU
from contextlib import ExitStack

F32 = mybir.dt.float32
BF16 = mybir.dt.bfloat16
AF = mybir.ActivationFunctionType
AX = mybir.AxisListType

P = 128
T = 2048
D = 2048
NT = T // P
ND = D // P
CH = 512
NCH = T // CH

QK_SCALE = float(P) ** -0.5
L0_SCALE = float(P) ** -0.5
LN_EPS = 1e-5
RMS_EPS = 1e-6
L0_EPS = 1e-9
N_CORES = 8



def slab_weights(WT, osz=P):
    WT = np.ascontiguousarray(WT, dtype=np.float32)
    K, O = WT.shape
    kt = K // P
    A = WT.reshape(kt, P, O // osz, osz).transpose(2, 1, 0, 3)
    return np.ascontiguousarray(A.reshape(O // osz, P, kt * osz))



def emit_xT_build(block, src_dram, row0, nrow_tiles, xT, stage, sems,
                  cast=True):
    raise NotImplementedError



def build_phase1(nc, x, wqkv, wout, eye32, eye16, R1):
    NSLAB_B = 3 * ND

    def cb(s):
        if s < NSLAB_B:
            return 4 * (s + 1)
        return 4 * NSLAB_B + 2 * (s - NSLAB_B + 1)

    with ExitStack() as ctx:
        sb = lambda n_, s_, t_: ctx.enter_context(nc.sbuf_tensor(n_, s_, t_))
        sem = lambda n_: ctx.enter_context(nc.semaphore(n_))

        outT = sb("p1_outT", [P, ND * T], BF16)
        wbuf = sb("p1_wbuf", [P, 3 * D], BF16)
        eyeb = sb("p1_eyeb", [P, P], BF16)
        eyef = sb("p1_eyef", [P, P], F32)
        ones1 = sb("p1_ones1", [P, 1], BF16)
        epsb = sb("p1_epsb", [P, 1], F32)
        lnepsb = sb("p1_lnepsb", [P, 1], F32)

        dA = sem("p1_dA")
        wld = [sem("p1_wld0"), sem("p1_wld1"), sem("p1_wld2")]
        mmB = sem("p1_mmB")
        evB = sem("p1_evB")
        trA = sem("p1_trA")
        evA = sem("p1_evA")
        kst = sem("p1_kst")
        dsem = sem("p1_dsem")
        recS = sem("p1_recS")
        numS = sem("p1_numS")
        vout = sem("p1_vout")
        vch = sem("p1_vch")
        sch = sem("p1_sch")
        trC = sem("p1_trC")
        addC = sem("p1_addC")
        lnv = sem("p1_lnv")
        lnr = sem("p1_lnr")
        lnx = sem("p1_lnx")
        lnw = sem("p1_lnw")
        dC = sem("p1_dC")
        dR = [sem("p1_dR0"), sem("p1_dR1")]
        msc = sem("p1_msc")

        ones_bcast = bass.AP(ones1, 0, [[1, P], [0, T]])

        with ExitStack() as sctx:
            sbl = lambda n_, s_, t_: sctx.enter_context(nc.sbuf_tensor(n_, s_, t_))
            psl = lambda n_, s_, t_: sctx.enter_context(nc.psum_tensor(n_, s_, t_))
            xT = sbl("p1_xT", [P, ND * T], BF16)
            xtmb = sbl("p1_xtmb", [P, D], BF16)
            vbuf = sbl("p1_vbuf", [P, T], BF16)
            kp = sbl("p1_kp", [P, T], BF16)
            kv = sbl("p1_kv", [P, T], BF16)
            qp = sbl("p1_qp", [P, T], F32)
            abuf = sbl("p1_a", [P, T], F32)
            cbuf = sbl("p1_c", [P, T], F32)
            KVc = sbl("p1_KVc", [P, T], F32)
            Kc = sbl("p1_Kc", [P, T], F32)
            psA = [psl(f"p1_psA{i}", [P, CH], F32) for i in range(4)]
            ptr = [psl(f"p1_ptr{i}", [P, 8 * P], BF16) for i in range(2)]

            with nc.Block() as block:
                @block.gpsimd
                def _(g):
                    g.memset(ones1[:, :], 1.0).then_inc(msc, 1)
                    g.memset(epsb[:, :], L0_EPS).then_inc(msc, 1)
                    g.memset(lnepsb[:, :], LN_EPS).then_inc(msc, 1)
                    g.dma_start(eyeb[:, :], eye16[:, :]).then_inc(dA, 16)
                    g.dma_start(eyef[:, :], eye32[:, :]).then_inc(dA, 16)
                    for tt in range(NT):
                        if tt >= 1:
                            g.wait_ge(trA, 2 * tt)
                        g.dma_start(
                            xtmb[:, :], x[tt * P:(tt + 1) * P, :],
                        ).then_inc(dA, 16)
                    for s in range(NSLAB_B):
                        if s >= 3:
                            g.wait_ge(mmB, cb(s - 3))
                        g.dma_start(
                            wbuf[:, (s % 3) * D:(s % 3 + 1) * D],
                            wqkv[s, :, :],
                        ).then_inc(wld[s % 3], 16)

                @block.tensor
                def _(t):
                    for tt in range(NT):
                        t.wait_ge(dA, 32 + 16 * (tt + 1))
                        if tt >= 1:
                            t.wait_ge(evA, 2 * tt)
                        for dt in range(ND):
                            inst = t.transpose(
                                ptr[dt // 8][:, (dt % 8) * P:(dt % 8 + 1) * P],
                                xtmb[:, dt * P:(dt + 1) * P],
                                eyeb[:, :])
                            if dt % 8 == 7:
                                inst.then_inc(trA, 1)
                    t.wait_ge(evA, 2 * NT)
                    for s in range(NSLAB_B):
                        t.wait_ge(wld[s % 3], 16 * (s // 3 + 1))
                        for c in range(NCH):
                            m = 4 * s + c
                            if m >= 4:
                                t.wait_ge(evB, m - 3)
                            for kt in range(ND):
                                inst = t.matmul(
                                    psA[m % 4][:, :],
                                    wbuf[:, (s % 3) * D + kt * P:
                                         (s % 3) * D + (kt + 1) * P],
                                    xT[:, kt * T + c * CH:kt * T + (c + 1) * CH],
                                    start=(kt == 0), stop=(kt == ND - 1))
                            inst.then_inc(mmB, 1)

                @block.vector
                def _(v):
                    for tt in range(NT):
                        for half in range(2):
                            v.wait_ge(trA, 2 * tt + half + 1)
                            v.tensor_copy(
                                bass.AP(xT, half * 8 * T + tt * P,
                                        [[ND * T, P], [T, 8], [1, P]]),
                                ptr[half][:, :]).then_inc(evA, 1)
                    v.wait_ge(msc, 3)
                    for i in range(ND):
                        v.wait_ge(evB, 4 * (3 * i + 1) + 4)
                        v.scalar_tensor_tensor(
                            kp[:, :], cbuf[:, :], 1.0, abuf[:, :],
                            op0=ALU.min, op1=ALU.add).then_inc(kst, 1)
                        v.wait_ge(kst, i + 1)
                        v.tensor_tensor(kv[:, :], kp[:, :], vbuf[:, :],
                                        op=ALU.mult).then_inc(vch, 1)
                        v.wait_ge(vch, 5 * i + 1)
                        v.tensor_tensor_scan(KVc[:, :], ones_bcast, kv[:, :],
                                             0.0, op0=ALU.mult,
                                             op1=ALU.add).then_inc(vch, 1)
                        v.tensor_tensor_scan(Kc[:, :], ones_bcast, kp[:, :],
                                             0.0, op0=ALU.mult,
                                             op1=ALU.add).then_inc(vch, 1)
                        v.wait_ge(evB, 4 * (3 * i + 2) + 4)
                        v.scalar_tensor_tensor(
                            qp[:, :], cbuf[:, :], 1.0, abuf[:, :],
                            op0=ALU.min, op1=ALU.add).then_inc(vch, 1)
                        v.wait_ge(vch, 5 * i + 4)
                        v.tensor_tensor(abuf[:, :], qp[:, :], Kc[:, :],
                                        op=ALU.mult).then_inc(dsem, 1)
                        v.tensor_tensor(KVc[:, :], qp[:, :], KVc[:, :],
                                        op=ALU.mult).then_inc(vch, 1)
                        v.wait_ge(recS, i + 1)
                        v.wait_ge(vch, 5 * i + 5)
                        v.tensor_tensor(outT[:, i * T:(i + 1) * T],
                                        KVc[:, :], abuf[:, :],
                                        op=ALU.mult).then_inc(vout, 1)

                @block.scalar
                def _(s_):
                    for i in range(ND):
                        if i >= 1:
                            s_.wait_ge(vout, i)
                        for c in range(NCH):
                            m = 4 * (3 * i) + c
                            s_.wait_ge(mmB, m + 1)
                            s_.activation(vbuf[:, c * CH:(c + 1) * CH],
                                          psA[m % 4][:, :],
                                          AF.Copy).then_inc(evB, 1)
                        for c in range(NCH):
                            m = 4 * (3 * i + 1) + c
                            s_.wait_ge(mmB, m + 1)
                            s_.activation(abuf[:, c * CH:(c + 1) * CH],
                                          psA[m % 4][:, :], AF.Relu)
                            s_.activation(cbuf[:, c * CH:(c + 1) * CH],
                                          psA[m % 4][:, :],
                                          AF.Exp).then_inc(evB, 1)
                        s_.wait_ge(kst, i + 1)
                        for c in range(NCH):
                            m = 4 * (3 * i + 2) + c
                            s_.wait_ge(mmB, m + 1)
                            s_.activation(abuf[:, c * CH:(c + 1) * CH],
                                          psA[m % 4][:, :], AF.Relu)
                            s_.activation(cbuf[:, c * CH:(c + 1) * CH],
                                          psA[m % 4][:, :],
                                          AF.Exp).then_inc(evB, 1)
                        s_.wait_ge(dsem, i + 1)
                        if i == 0:
                            s_.wait_ge(msc, 3)
                        s_.activation(cbuf[:, :], abuf[:, :], AF.Ln,
                                      bias=epsb[:, :]).then_inc(sch, 1)
                        s_.wait_ge(sch, i + 1)
                        s_.activation(abuf[:, :], cbuf[:, :], AF.Exp,
                                      scale=-1.0).then_inc(recS, 1)


        with ExitStack() as sctx:
            sbl = lambda n_, s_, t_: sctx.enter_context(nc.sbuf_tensor(n_, s_, t_))
            psl = lambda n_, s_, t_: sctx.enter_context(nc.psum_tensor(n_, s_, t_))
            sumh = sbl("p1_sumh", [P, 8 * D], F32)
            aT = sbl("p1_aT", [P, 2 * 1024], F32)
            mvb = sbl("p1_mv", [P, 2], F32)
            bn6 = sbl("p1_bn6", [P, 24], F32)
            rstd = sbl("p1_rstd", [P, 1], F32)
            r1t = sbl("p1_r1t", [P, 2 * D], F32)
            psC = [psl(f"p1_psC{i}", [P, CH], F32) for i in range(4)]
            psT = [psl(f"p1_psT{i}", [P, CH], F32) for i in range(2)]

            with nc.Block() as block:
                @block.sync
                def _(sy):
                    sy.wait_ge(vout, ND)
                    for th in range(2):
                        for tq in range(8):
                            sy.dma_start(
                                sumh[:, tq * D:(tq + 1) * D],
                                x[(th * 8 + tq) * P:(th * 8 + tq + 1) * P, :],
                            ).then_inc(dC, 16)
                        for tq in range(8):
                            gidx = th * 8 + tq
                            sy.wait_ge(lnw, gidx + 1)
                            sy.dma_start(
                                R1[gidx * P:(gidx + 1) * P, :],
                                r1t[:, (tq % 2) * D:(tq % 2 + 1) * D],
                            ).then_inc(dR[gidx % 2], 16)
                    sy.wait_ge(dR[0], 16 * 8)
                    sy.wait_ge(dR[1], 16 * 8)

                @block.gpsimd
                def _(g):
                    for th in range(2):
                        for dt in range(ND):
                            s = NSLAB_B + th * ND + dt
                            g.wait_ge(mmB, cb(s - 3))
                            g.dma_start(
                                wbuf[:, (s % 3) * D:(s % 3 + 1) * D],
                                wout[dt, :, :]).then_inc(wld[s % 3], 16)

                @block.tensor
                def _(t):
                    t.wait_ge(vout, ND)
                    t.wait_ge(evB, 4 * NSLAB_B)
                    for th in range(2):
                        for dt in range(ND):
                            s = NSLAB_B + th * ND + dt
                            idx = th * ND + dt
                            t.wait_ge(wld[s % 3], 16 * (s // 3 + 1))
                            for c in range(2):
                                m = 4 * NSLAB_B + 2 * idx + c
                                t.wait_ge(evB, m - 3)
                                for kt in range(ND):
                                    inst = t.matmul(
                                        psC[m % 4][:, :],
                                        wbuf[:, (s % 3) * D + kt * P:
                                             (s % 3) * D + (kt + 1) * P],
                                        outT[:, kt * T + th * 1024 + c * CH:
                                             kt * T + th * 1024 + (c + 1) * CH],
                                        start=(kt == 0), stop=(kt == ND - 1))
                                inst.then_inc(mmB, 1)
                            t.wait_ge(evB, 4 * NSLAB_B + 2 * idx + 2)
                            for pk in range(2):
                                if idx >= 1:
                                    t.wait_ge(addC, 2 * (idx - 1) + pk + 1)
                                for tq4 in range(4):
                                    tq = pk * 4 + tq4
                                    inst = t.transpose(
                                        psT[pk][:, tq4 * P:(tq4 + 1) * P],
                                        aT[:, (dt % 2) * 1024 + tq * P:
                                           (dt % 2) * 1024 + (tq + 1) * P],
                                        eyef[:, :])
                                inst.then_inc(trC, 1)

                @block.scalar
                def _(s_):
                    for th in range(2):
                        for dt in range(ND):
                            idx = th * ND + dt
                            if idx >= 2:
                                s_.wait_ge(trC, 2 * (idx - 1))
                            for c in range(2):
                                m = 4 * NSLAB_B + 2 * idx + c
                                s_.wait_ge(mmB, m + 1)
                                s_.activation(
                                    aT[:, (dt % 2) * 1024 + c * CH:
                                       (dt % 2) * 1024 + (c + 1) * CH],
                                    psC[m % 4][:, :], AF.Copy,
                                    scale=L0_SCALE).then_inc(evB, 1)
                        for tq in range(8):
                            gidx = th * 8 + tq
                            s_.wait_ge(lnv, 2 * (gidx + 1))
                            s_.activation(rstd[:, :], mvb[:, 1:2], AF.Ln,
                                          bias=lnepsb[:, :]).then_inc(lnx, 1)
                            s_.wait_ge(lnx, gidx + 1)
                            s_.activation(rstd[:, :], rstd[:, :], AF.Exp,
                                          scale=-0.5).then_inc(lnr, 1)

                @block.vector
                def _(v):
                    for th in range(2):
                        v.wait_ge(dC, 16 * 8 * (th + 1))
                        for dt in range(ND):
                            idx = th * ND + dt
                            for pk in range(2):
                                v.wait_ge(trC, 2 * idx + pk + 1)
                                dst = bass.AP(sumh, pk * 4 * D + dt * P,
                                              [[8 * D, P], [D, 4], [1, P]])
                                v.tensor_tensor(dst, psT[pk][:, :], dst,
                                                op=ALU.add).then_inc(addC, 1)
                        v.wait_ge(addC, 32 * (th + 1))
                        for tq in range(8):
                            gidx = th * 8 + tq
                            tile = sumh[:, tq * D:(tq + 1) * D]
                            for c4 in range(4):
                                inst = v.bn_stats(
                                    bn6[:, c4 * 6:(c4 + 1) * 6],
                                    tile[:, c4 * CH:(c4 + 1) * CH])
                            inst.then_inc(lnv, 1)
                            v.wait_ge(lnv, 2 * gidx + 1)
                            v.bn_aggr(mvb[:, :], bn6[:, :]).then_inc(lnv, 1)
                            v.wait_ge(lnr, gidx + 1)
                            if gidx >= 2:
                                v.wait_ge(dR[gidx % 2], 16 * (gidx // 2))
                            v.tensor_scalar(
                                r1t[:, (tq % 2) * D:(tq % 2 + 1) * D],
                                tile, mvb[:, 0:1], rstd[:, :],
                                op0=ALU.subtract,
                                op1=ALU.mult).then_inc(lnw, 1)
    return nc



def _clear_sems(nc):
    with nc.Block() as block:
        @block.sync
        def _(sy):
            sy.sem_clear(range(155, 250))



def build_phase2(nc, R1, wgu, wd, eye32, eye16, R2):
    HT = 1024
    NIT = 32

    _clear_sems(nc)

    def base(th, ip):
        return th * 320 + ip * 160

    gu_done = {}
    dn_done = {}
    for th in range(2):
        for ip in range(2):
            for ls in range(64):
                gu_done[th * 128 + ip * 64 + ls] = base(th, ip) + 2 * (ls + 1)
            for dt in range(ND):
                dn_done[th * 32 + ip * 16 + dt] = \
                    base(th, ip) + 128 + 2 * (dt + 1)

    with ExitStack() as ctx:
        sb = lambda n_, s_, t_: ctx.enter_context(nc.sbuf_tensor(n_, s_, t_))
        psl = lambda n_, s_, t_: ctx.enter_context(nc.psum_tensor(n_, s_, t_))
        sem = lambda n_: ctx.enter_context(nc.semaphore(n_))

        h = sb("p2_h", [P, NIT * HT], BF16)
        xT = sb("p2_xT", [P, ND * HT], BF16)
        sumh = sb("p2_sumh", [P, 8 * D], F32)
        wbufG = sb("p2_wbufG", [P, 2 * D], BF16)
        wbufD = sb("p2_wbufD", [P, 2 * 4096], BF16)
        gbuf = sb("p2_gbuf", [P, 2 * HT], BF16)
        ubuf = sb("p2_ubuf", [P, 2 * HT], BF16)
        aT = sb("p2_aT", [P, 2 * HT], F32)
        xtmb = sb("p2_xtmb", [P, D], BF16)
        eyeb = sb("p2_eyeb", [P, P], BF16)
        eyef = sb("p2_eyef", [P, P], F32)
        mvb = sb("p2_mv", [P, 2], F32)
        bn6 = sb("p2_bn6", [P, 24], F32)
        rstd = sb("p2_rstd", [P, 1], F32)
        lnepsb = sb("p2_lneps", [P, 1], F32)
        psA = [psl(f"p2_psA{i}", [P, CH], F32) for i in range(4)]
        ptrA = [psl(f"p2_ptrA{i}", [P, 8 * P], BF16) for i in range(2)]
        psT = [psl(f"p2_psT{i}", [P, CH], F32) for i in range(2)]

        dA = sem("p2_dA")
        trA = sem("p2_trA")
        evA = sem("p2_evA")
        wldG = [sem("p2_wldG0"), sem("p2_wldG1")]
        wldD = [sem("p2_wldD0"), sem("p2_wldD1")]
        mm = sem("p2_mm")
        ev = sem("p2_ev")
        hm = sem("p2_hm")
        dC = sem("p2_dC")
        trC = sem("p2_trC")
        addC = sem("p2_addC")
        lnv = sem("p2_lnv")
        lnr = sem("p2_lnr")
        lnx = sem("p2_lnx")
        lnw = sem("p2_lnw")
        dR = sem("p2_dR")
        msc = sem("p2_msc")

        with nc.Block() as block:
            @block.gpsimd
            def _(g):
                g.memset(lnepsb[:, :], LN_EPS).then_inc(msc, 1)
                g.dma_start(eyeb[:, :], eye16[:, :]).then_inc(dA, 16)
                g.dma_start(eyef[:, :], eye32[:, :]).then_inc(dA, 16)
                for th in range(2):
                    for tl in range(8):
                        tt = th * 8 + tl
                        if tt >= 1:
                            g.wait_ge(trA, 2 * tt)
                        g.dma_start(
                            xtmb[:, :], R1[tt * P:(tt + 1) * P, :],
                        ).then_inc(dA, 16)
                    for ip in range(2):
                        for ls in range(64):
                            gs = th * 128 + ip * 64 + ls
                            if gs >= 2:
                                g.wait_ge(mm, gu_done[gs - 2])
                            g.dma_start(
                                wbufG[:, (gs % 2) * D:(gs % 2 + 1) * D],
                                wgu[ip * 64 + ls, :, :]).then_inc(wldG[gs % 2], 16)
                        for dt in range(ND):
                            ds = th * 32 + ip * 16 + dt
                            if ds >= 2:
                                g.wait_ge(mm, dn_done[ds - 2])
                            g.dma_start(
                                wbufD[:, (ds % 2) * 4096:(ds % 2 + 1) * 4096],
                                wd[ip, dt, :, :]).then_inc(wldD[ds % 2], 16)

            @block.tensor
            def _(t):
                for th in range(2):
                    for tl in range(8):
                        tt = th * 8 + tl
                        t.wait_ge(dA, 32 + 16 * (tt + 1))
                        if tt >= 1:
                            t.wait_ge(evA, 2 * tt)
                        for dt in range(ND):
                            inst = t.transpose(
                                ptrA[dt // 8][:, (dt % 8) * P:(dt % 8 + 1) * P],
                                xtmb[:, dt * P:(dt + 1) * P], eyeb[:, :])
                            if dt % 8 == 7:
                                inst.then_inc(trA, 1)
                    t.wait_ge(evA, 2 * (th * 8 + 8))
                    for ip in range(2):
                        for ls in range(64):
                            gs = th * 128 + ip * 64 + ls
                            t.wait_ge(wldG[gs % 2], 16 * (gs // 2 + 1))
                            for c in range(2):
                                m = base(th, ip) + 2 * ls + c
                                if m >= 4:
                                    t.wait_ge(ev, m - 3)
                                for kt in range(ND):
                                    inst = t.matmul(
                                        psA[m % 4][:, :],
                                        wbufG[:, (gs % 2) * D + kt * P:
                                              (gs % 2) * D + (kt + 1) * P],
                                        xT[:, kt * HT + c * CH:
                                           kt * HT + (c + 1) * CH],
                                        start=(kt == 0), stop=(kt == ND - 1))
                                inst.then_inc(mm, 1)
                        t.wait_ge(hm, 32 * (th * 2 + ip + 1))
                        for dt in range(ND):
                            ds = th * 32 + ip * 16 + dt
                            idxg = th * 32 + ip * 16 + dt
                            t.wait_ge(wldD[ds % 2], 16 * (ds // 2 + 1))
                            for c in range(2):
                                m = base(th, ip) + 128 + 2 * dt + c
                                t.wait_ge(ev, m - 3)
                                for kt in range(NIT):
                                    inst = t.matmul(
                                        psA[m % 4][:, :],
                                        wbufD[:, (ds % 2) * 4096 + kt * P:
                                              (ds % 2) * 4096 + (kt + 1) * P],
                                        h[:, kt * HT + c * CH:
                                          kt * HT + (c + 1) * CH],
                                        start=(kt == 0), stop=(kt == NIT - 1))
                                inst.then_inc(mm, 1)
                            t.wait_ge(ev, base(th, ip) + 128 + 2 * dt + 2)
                            for pk in range(2):
                                if idxg >= 1:
                                    t.wait_ge(addC, 2 * (idxg - 1) + pk + 1)
                                for tq4 in range(4):
                                    tq = pk * 4 + tq4
                                    inst = t.transpose(
                                        psT[pk][:, tq4 * P:(tq4 + 1) * P],
                                        aT[:, (dt % 2) * HT + tq * P:
                                           (dt % 2) * HT + (tq + 1) * P],
                                        eyef[:, :])
                                inst.then_inc(trC, 1)

            @block.scalar
            def _(s_):
                for th in range(2):
                    for ip in range(2):
                        for ls in range(64):
                            il = ls // 2
                            is_g = (ls % 2 == 0)
                            gidx = (th * 2 + ip) * 32 + il
                            if gidx >= 2 and is_g:
                                s_.wait_ge(hm, gidx - 1)
                            for c in range(2):
                                m = base(th, ip) + 2 * ls + c
                                s_.wait_ge(mm, m + 1)
                                dst = (gbuf if is_g else ubuf)
                                s_.activation(
                                    dst[:, (il % 2) * HT + c * CH:
                                        (il % 2) * HT + (c + 1) * CH],
                                    psA[m % 4][:, :],
                                    AF.Silu if is_g else AF.Copy,
                                ).then_inc(ev, 1)
                        for dt in range(ND):
                            idxg = th * 32 + ip * 16 + dt
                            if idxg >= 2:
                                s_.wait_ge(trC, 2 * (idxg - 1))
                            for c in range(2):
                                m = base(th, ip) + 128 + 2 * dt + c
                                s_.wait_ge(mm, m + 1)
                                s_.activation(
                                    aT[:, (dt % 2) * HT + c * CH:
                                       (dt % 2) * HT + (c + 1) * CH],
                                    psA[m % 4][:, :], AF.Copy).then_inc(ev, 1)
                    for tl in range(8):
                        gt = th * 8 + tl
                        s_.wait_ge(lnv, 2 * (gt + 1))
                        if gt == 0:
                            s_.wait_ge(msc, 1)
                        s_.activation(rstd[:, :], mvb[:, 1:2], AF.Ln,
                                      bias=lnepsb[:, :]).then_inc(lnx, 1)
                        s_.wait_ge(lnx, gt + 1)
                        s_.activation(rstd[:, :], rstd[:, :], AF.Exp,
                                      scale=-0.5).then_inc(lnr, 1)

            @block.vector
            def _(v):
                for th in range(2):
                    for tl in range(8):
                        tt = th * 8 + tl
                        for half in range(2):
                            v.wait_ge(trA, 2 * tt + half + 1)
                            if th == 1 and tl == 0 and half == 0:
                                v.wait_ge(mm, base(0, 1) + 128)
                            v.tensor_copy(
                                bass.AP(xT, half * 8 * HT + tl * P,
                                        [[ND * HT, P], [HT, 8], [1, P]]),
                                ptrA[half][:, :]).then_inc(evA, 1)
                    for ip in range(2):
                        for il in range(NIT):
                            gidx = (th * 2 + ip) * 32 + il
                            if il == 0 and gidx >= 32:
                                v.wait_ge(mm, base(th, ip - 1) + 160
                                          if ip == 1 else base(th - 1, 1) + 160)
                            v.wait_ge(ev, base(th, ip) + 4 * il + 4)
                            v.tensor_tensor(
                                h[:, il * HT:(il + 1) * HT],
                                gbuf[:, (il % 2) * HT:(il % 2 + 1) * HT],
                                ubuf[:, (il % 2) * HT:(il % 2 + 1) * HT],
                                op=ALU.mult).then_inc(hm, 1)
                        for dt in range(ND):
                            idxg = th * 32 + ip * 16 + dt
                            if ip == 0 and dt == 0:
                                v.wait_ge(dC, 16 * 8 * (th + 1))
                            for pk in range(2):
                                v.wait_ge(trC, 2 * idxg + pk + 1)
                                dst = bass.AP(sumh, pk * 4 * D + dt * P,
                                              [[8 * D, P], [D, 4], [1, P]])
                                v.tensor_tensor(dst, psT[pk][:, :], dst,
                                                op=ALU.add).then_inc(addC, 1)
                    v.wait_ge(addC, 32 * (th + 1))
                    for tl in range(8):
                        gt = th * 8 + tl
                        tile = sumh[:, tl * D:(tl + 1) * D]
                        for c4 in range(4):
                            inst = v.bn_stats(bn6[:, c4 * 6:(c4 + 1) * 6],
                                              tile[:, c4 * CH:(c4 + 1) * CH])
                        inst.then_inc(lnv, 1)
                        v.wait_ge(lnv, 2 * gt + 1)
                        v.bn_aggr(mvb[:, :], bn6[:, :]).then_inc(lnv, 1)
                        v.wait_ge(lnr, gt + 1)
                        v.tensor_scalar(tile, tile, mvb[:, 0:1], rstd[:, :],
                                        op0=ALU.subtract,
                                        op1=ALU.mult).then_inc(lnw, 1)

            @block.sync
            def _(sy):
                for th in range(2):
                    if th == 1:
                        sy.wait_ge(dR, 16 * 8)
                    for tl in range(8):
                        sy.dma_start(
                            sumh[:, tl * D:(tl + 1) * D],
                            R1[(th * 8 + tl) * P:(th * 8 + tl + 1) * P, :],
                        ).then_inc(dC, 16)
                    for tl in range(8):
                        gt = th * 8 + tl
                        sy.wait_ge(lnw, gt + 1)
                        sy.dma_start(
                            R2[gt * P:(gt + 1) * P, :],
                            sumh[:, tl * D:(tl + 1) * D]).then_inc(dR, 16)
                sy.wait_ge(dR, 16 * NT)
    return nc


def host_prep_p2(w_gate_up, w_down):
    WT = np.asarray(w_gate_up, np.float32).T
    cols = []
    for ip in range(2):
        for il in range(32):
            i = ip * 32 + il
            cols.append(WT[:, i * P:(i + 1) * P])
            cols.append(WT[:, 8192 + i * P:8192 + (i + 1) * P])
    wgu_s = slab_weights(np.concatenate(cols, axis=1), osz=P)
    WdT = np.asarray(w_down, np.float32).T
    wd_s = np.zeros((2, 16, P, 4096), np.float32)
    for ip in range(2):
        sl = slab_weights(WdT[ip * 4096:(ip + 1) * 4096, :], osz=P)
        wd_s[ip] = sl
    return wgu_s, np.ascontiguousarray(wd_s)


def _mk_phase2_nc():
    nc = bass.Bass("TRN2", target_bir_lowering=False)
    R1 = nc.dram_tensor("R1", [T, D], F32, kind="ExternalInput")
    wgu = nc.dram_tensor("wgu", [128, P, D], F32, kind="ExternalInput")
    wd = nc.dram_tensor("wd", [2, 16, P, 4096], F32, kind="ExternalInput")
    eye32 = nc.dram_tensor("eye32", [P, P], F32, kind="ExternalInput")
    eye16 = nc.dram_tensor("eye16", [P, P], BF16, kind="ExternalInput")
    R2 = nc.dram_tensor("R2", [T, D], F32, kind="ExternalOutput")
    build_phase2(nc, R1, wgu, wd, eye32, eye16, R2)
    return nc


def run_phase2(R1_full, w_gate_up, w_down, cores=N_CORES):
    wgu_s, wd_s = host_prep_p2(w_gate_up, w_down)
    nc = _mk_phase2_nc()
    in_maps = []
    for c in range(cores):
        in_maps.append(dict(R1=np.ascontiguousarray(R1_full[c], np.float32),
                            wgu=wgu_s, wd=wd_s,
                            eye32=_EYE32, eye16=_EYE16))
    res = run_bass_kernel_spmd(nc, in_maps, core_ids=list(range(cores)))
    return np.stack([r["R2"] for r in res.results])




def build_phase3(nc, R2, wq, wk, wv, wo, nw, bq, bk, bv, bo, masks,
                 eye32, eye16, R3, upto=3, att="full", d_lvl="full",
                 qargs=None):
    _clear_sems(nc)
    qs = nc.dram_tensor("p3_qs", [16, P, T], BF16)
    vs = nc.dram_tensor("p3_vs", [16, P, T], BF16)
    os_ = nc.dram_tensor("p3_os", [16, P, D], BF16)
    NB = 48
    QPT = 256
    QH = 1024

    def q_src(i):
        w1t, w2t, _, _ = qargs
        if i < 128:
            return w1t[i // 2, :, (i % 2) * QH:(i % 2 + 1) * QH]
        j = i - 128
        ip, dt, qq_ = (j // 4) // 16, (j // 4) % 16, j % 4
        return w2t[ip, dt, :, qq_ * QH:(qq_ + 1) * QH]

    def q_dst(i):
        _, _, w1q, w2q = qargs
        if i < 128:
            return w1q[i // 2, :, (i % 2) * QH:(i % 2 + 1) * QH]
        j = i - 128
        ip, dt, qq_ = (j // 4) // 16, (j // 4) % 16, j % 4
        return w2q[ip, dt, :, qq_ * QH:(qq_ + 1) * QH]

    def q_batch(k, n_batches, total=QPT):
        a = (total * k) // n_batches
        b = (total * (k + 1)) // n_batches
        return range(a, b)

    with ExitStack() as ctx:
        sb = lambda n_, s_, t_: ctx.enter_context(nc.sbuf_tensor(n_, s_, t_))
        psl = lambda n_, s_, t_: ctx.enter_context(nc.psum_tensor(n_, s_, t_))
        sem = lambda n_: ctx.enter_context(nc.semaphore(n_))

        big = sb("p3_big", [P, ND * T], BF16)
        wbuf = sb("p3_wbuf", [P, 2 * D], BF16)
        qtile = sb("p3_qtile", [P, 2 * T], BF16)
        eyeb = sb("p3_eyeb", [P, P], BF16)
        eyef = sb("p3_eyef", [P, P], F32)
        lnepsb = sb("p3_lneps", [P, 1], F32)
        mvb = sb("p3_mv", [P, 2], F32)
        bn6 = sb("p3_bn6", [P, 24], F32)
        lrstd = sb("p3_lrstd", [P, 1], F32)
        bqb = sb("p3_bqb", [P, 16], F32)
        bkb = sb("p3_bkb", [P, 16], F32)
        bvb = sb("p3_bvb", [P, 16], F32)
        bob = sb("p3_bob", [P, 16], F32)

        psA = [psl(f"p3_psA{i}", [P, CH], F32) for i in range(2)]
        ptrB = [psl(f"p3_ptrB{i}", [P, 8 * P], BF16) for i in range(2)]
        psS = psl("p3_psS", [P, T], F32)

        dA = sem("p3_dA")
        dX = [sem("p3_dX0"), sem("p3_dX1")]
        rm1 = sem("p3_rm1")
        rm1b = sem("p3_rm1b")
        rm2 = sem("p3_rm2")
        xnv = sem("p3_xnv")
        trA = sem("p3_trA")
        evA = sem("p3_evA")
        wld = [sem("p3_wld0"), sem("p3_wld1")]
        mm = sem("p3_mm")
        ev = sem("p3_ev")
        dQ = [sem("p3_dQ0"), sem("p3_dQ1")]
        dVs = [sem("p3_dVs0"), sem("p3_dVs1")]
        dVr = [sem("p3_dVr0"), sem("p3_dVr1")]
        trV = sem("p3_trV")
        evV = sem("p3_evV")
        qld = [sem("p3_qld0"), sem("p3_qld1")]
        mmS = sem("p3_mmS")
        aux1 = sem("p3_aux1")
        sS = sem("p3_sS")
        sE = sem("p3_sE")
        trP = sem("p3_trP")
        evP = sem("p3_evP")
        rc = sem("p3_rc")
        mmO = sem("p3_mmO")
        evO = sem("p3_evO")
        dOs = [sem("p3_dOs0"), sem("p3_dOs1")]
        dOl = [sem("p3_dOl0"), sem("p3_dOl1")]
        trD = sem("p3_trD")
        evD = sem("p3_evD")
        wld2 = [sem("p3_wld20"), sem("p3_wld21")]
        mmD = sem("p3_mmD")
        evE = sem("p3_evE")
        trC = sem("p3_trC")
        addC = sem("p3_addC")
        lnv = sem("p3_lnv")
        lnr = sem("p3_lnr")
        lnx = sem("p3_lnx")
        lnw = sem("p3_lnw")
        dC = sem("p3_dC")
        dR = sem("p3_dR")
        msc = sem("p3_msc")
        qdS = [sem("p3_qdS0"), sem("p3_qdS1")]
        qdS2 = [sem("p3_qdS20"), sem("p3_qdS21")]
        qsc = sem("p3_qsc")
        qv = sem("p3_qv")
        qdW = [sem("p3_qdW0"), sem("p3_qdW1")]
        qthr = sem("p3_qthr")

        with ExitStack() as bctx:
            sbb = lambda n_, s_, t_: bctx.enter_context(nc.sbuf_tensor(n_, s_, t_))
            Kreg = sbb("p3_K", [P, ND * T], BF16)
            vtile = sbb("p3_vtile", [P, 2 * T], BF16)
            if qargs is not None:
                qstrip = sbb("p3_qstrip", [P, 2 * QH], F32)
                qpb = sbb("p3_qpb", [P, QPT], F32)

            with ExitStack() as actx:
                sba = lambda n_, s_, t_: actx.enter_context(
                    nc.sbuf_tensor(n_, s_, t_))
                xtm2 = sba("p3_xtm2", [P, 2 * D], F32)
                junk = sba("p3_junk", [P, D], F32)
                xtmb = sba("p3_xtmb", [P, D], BF16)
                nwb = sba("p3_nwb", [P, D], F32)
                rmsepsb = sba("p3_rmseps", [P, 1], F32)
                ssq = sba("p3_ssq", [P, 1], F32)
                rtmp = sba("p3_rtmp", [P, 1], F32)
                rstd = sba("p3_rstd", [P, 1], F32)

                with nc.Block() as block:
                    @block.sync
                    def _(sy):
                        sy.dma_start(eyef[:, :], eye32[:, :]).then_inc(dA, 16)
                        sy.dma_start(eyeb[:, :], eye16[:, :]).then_inc(dA, 16)
                        sy.dma_start(
                            nwb[:, :],
                            bass.AP(nw, 0, [[0, P], [1, D]])).then_inc(dA, 16)
                        sy.dma_start(bqb[:, :], bq[:, :]).then_inc(dA, 16)
                        sy.dma_start(bkb[:, :], bk[:, :]).then_inc(dA, 16)
                        sy.dma_start(bvb[:, :], bv[:, :]).then_inc(dA, 16)
                        sy.dma_start(bob[:, :], bo[:, :]).then_inc(dA, 16)
                        for tt in range(NT):
                            if tt >= 2:
                                sy.wait_ge(xnv, tt - 1)
                            sy.dma_start(
                                xtm2[:, (tt % 2) * D:(tt % 2 + 1) * D],
                                R2[tt * P:(tt + 1) * P, :],
                            ).then_inc(dX[tt % 2], 16)
                        for dt in range(ND):
                            sy.wait_ge(ev, 4 * dt + 4)
                            sy.dma_start(
                                qs[dt, :, :],
                                qtile[:, (dt % 2) * T:(dt % 2 + 1) * T],
                            ).then_inc(dQ[dt % 2], 16)
                        for dt in range(ND):
                            sy.wait_ge(ev, 4 * (32 + dt) + 4)
                            sy.dma_start(
                                vs[dt, :, :],
                                vtile[:, (dt % 2) * T:(dt % 2 + 1) * T],
                            ).then_inc(dVs[dt % 2], 16)
                        sy.wait_ge(dQ[0], 16 * 8)
                        sy.wait_ge(dQ[1], 16 * 8)
                        sy.wait_ge(dVs[0], 16 * 8)
                        sy.wait_ge(dVs[1], 16 * 8)

                    @block.gpsimd
                    def _(g):
                        g.memset(rmsepsb[:, :], RMS_EPS).then_inc(msc, 1)
                        g.memset(lnepsb[:, :], LN_EPS).then_inc(msc, 1)
                        for grp, wsrc in ((0, wq), (1, wk), (2, wv)):
                            for dt in range(ND):
                                s = grp * ND + dt
                                if s >= 2:
                                    g.wait_ge(mm, 4 * (s - 1))
                                g.dma_start(
                                    wbuf[:, (s % 2) * D:(s % 2 + 1) * D],
                                    wsrc[dt, :, :]).then_inc(wld[s % 2], 16)
                                if qargs is not None:
                                    for i in q_batch(s, NB):
                                        if i >= 2:
                                            g.wait_ge(qsc, i - 1)
                                        g.dma_start(
                                            qstrip[:, (i % 2) * QH:
                                                   (i % 2 + 1) * QH],
                                            q_src(i)).then_inc(qdS[i % 2], 16)

                    @block.tensor
                    def _(t):
                        t.wait_ge(dA, 16 * 7)
                        for tt in range(NT):
                            t.wait_ge(xnv, tt + 1)
                            if tt >= 1:
                                t.wait_ge(evA, 2 * tt)
                            for dt in range(ND):
                                inst = t.transpose(
                                    ptrB[dt // 8][:, (dt % 8) * P:
                                                  (dt % 8 + 1) * P],
                                    xtmb[:, dt * P:(dt + 1) * P], eyeb[:, :])
                                if dt % 8 == 7:
                                    inst.then_inc(trA, 1)
                        t.wait_ge(evA, 2 * NT)
                        for s in range(NB):
                            t.wait_ge(wld[s % 2], 16 * (s // 2 + 1))
                            for c in range(NCH):
                                m = 4 * s + c
                                if m >= 2:
                                    t.wait_ge(ev, m - 1)
                                for kt in range(ND):
                                    inst = t.matmul(
                                        psA[m % 2][:, :],
                                        wbuf[:, (s % 2) * D + kt * P:
                                             (s % 2) * D + (kt + 1) * P],
                                        big[:, kt * T + c * CH:
                                            kt * T + (c + 1) * CH],
                                        start=(kt == 0), stop=(kt == ND - 1))
                                inst.then_inc(mm, 1)

                    @block.scalar
                    def _(s_):
                        s_.wait_ge(msc, 2)
                        for tt in range(NT):
                            s_.wait_ge(dX[tt % 2], 16 * (tt // 2 + 1))
                            if tt >= 1:
                                s_.wait_ge(xnv, tt)
                            s_.activation(
                                junk[:, :],
                                xtm2[:, (tt % 2) * D:(tt % 2 + 1) * D],
                                AF.Square,
                                accum_out=ssq[:, :]).then_inc(rm1, 1)
                            s_.wait_ge(rm1, tt + 1)
                            s_.activation(rtmp[:, :], ssq[:, :], AF.Ln,
                                          bias=rmsepsb[:, :],
                                          scale=1.0 / D).then_inc(rm1b, 1)
                            s_.wait_ge(rm1b, tt + 1)
                            s_.activation(rstd[:, :], rtmp[:, :], AF.Exp,
                                          scale=-0.5).then_inc(rm2, 1)
                        for grp in range(3):
                            for dt in range(ND):
                                s = grp * ND + dt
                                if grp == 0 and dt >= 2:
                                    s_.wait_ge(dQ[dt % 2], 16 * (dt // 2))
                                if grp == 2 and dt >= 2:
                                    s_.wait_ge(dVs[dt % 2], 16 * (dt // 2))
                                bias = (bqb, bkb, bvb)[grp]
                                for c in range(NCH):
                                    m = 4 * s + c
                                    s_.wait_ge(mm, m + 1)
                                    if grp == 0:
                                        dst = qtile[:, (dt % 2) * T + c * CH:
                                                    (dt % 2) * T + (c + 1) * CH]
                                    elif grp == 1:
                                        dst = Kreg[:, dt * T + c * CH:
                                                   dt * T + (c + 1) * CH]
                                    else:
                                        dst = vtile[:, (dt % 2) * T + c * CH:
                                                    (dt % 2) * T + (c + 1) * CH]
                                    s_.activation(
                                        dst, psA[m % 2][:, :], AF.Identity,
                                        bias=bias[:, dt:dt + 1]).then_inc(ev, 1)
                                if qargs is not None:
                                    for i in q_batch(s, NB):
                                        s_.wait_ge(qdS[i % 2],
                                                   16 * (i // 2 + 1))
                                        s_.activation(
                                            qstrip[:, (i % 2) * QH:
                                                   (i % 2 + 1) * QH],
                                            qstrip[:, (i % 2) * QH:
                                                   (i % 2 + 1) * QH],
                                            AF.Abs,
                                            accum_out=qpb[:, i:i + 1],
                                        ).then_inc(qsc, 1)

                    @block.vector
                    def _(v):
                        v.wait_ge(dA, 16 * 7)
                        for tt in range(NT):
                            v.wait_ge(rm2, tt + 1)
                            if tt >= 1:
                                v.wait_ge(trA, 2 * tt)
                            v.scalar_tensor_tensor(
                                xtmb[:, :],
                                xtm2[:, (tt % 2) * D:(tt % 2 + 1) * D],
                                rstd[:, :], nwb[:, :],
                                op0=ALU.mult, op1=ALU.mult).then_inc(xnv, 1)
                            for half in range(2):
                                v.wait_ge(trA, 2 * tt + half + 1)
                                v.tensor_copy(
                                    bass.AP(big, half * 8 * T + tt * P,
                                            [[ND * T, P], [T, 8], [1, P]]),
                                    ptrB[half][:, :]).then_inc(evA, 1)

            if upto < 2:
                return nc
            Psb = sbb("p3_Psb", [P, T], BF16)
            PTsb = sbb("p3_PTsb", [P, T], BF16)
            qblk = sbb("p3_qblk", [P, 2 * T], BF16)
            Otile = sbb("p3_Otile", [P, 2 * D], BF16)
            maskb = sbb("p3_maskb", [P, 4 * CH], F32)
            if qargs is not None:
                qred = sbb("p3_qred", [P, 2], F32)
                qtpq = sbb("p3_qtpq", [1, 2], F32)
                qth1 = sbb("p3_qth1", [P, 2], F32)
                qth2 = sbb("p3_qth2", [P, 2], F32)
                qab = sbb("p3_qab", [P, QH], BF16)
                qbb = sbb("p3_qbb", [P, QH], BF16)
                qqb = sbb("p3_qqb", [P, 2 * QH], BF16)
                qonesf = sbb("p3_qonesf", [P, 1], F32)
                qonesr = sbb("p3_qonesr", [1, P], F32)
            rm = sbb("p3_rm", [P, 1], F32)
            nm = sbb("p3_nm", [P, 1], F32)
            lsum = sbb("p3_lsum", [P, 1], F32)
            linv = sbb("p3_linv", [P, 1], F32)
            psO = psA[0][:, 0:P]

            with nc.Block() as block:
                @block.sync
                def _(sy):
                    for v_ in range(4):
                        sy.dma_start(maskb[:, v_ * CH:(v_ + 1) * CH],
                                     masks[v_, :, :]).then_inc(dA, 16)
                    for dt in range(ND):
                        if dt >= 2:
                            sy.wait_ge(trV, 2 * (dt - 1))
                        sy.dma_start(
                            vtile[:, (dt % 2) * T:(dt % 2 + 1) * T],
                            vs[dt, :, :]).then_inc(dVr[dt % 2], 16)
                    if att != "none":
                        for qb in range(NT):
                            if qb >= 2:
                                sy.wait_ge(mmS, 16 * (qb - 1))
                            sy.dma_start(
                                qblk[:, (qb % 2) * T:(qb % 2 + 1) * T],
                                bass.AP(qs, qb * P,
                                        [[T, P], [P * T, 16], [1, P]]),
                            ).then_inc(qld[qb % 2], 16)
                    if att == "full":
                        sy.wait_ge(dOs[0], 16 * 8)
                        sy.wait_ge(dOs[1], 16 * 8)

                @block.gpsimd
                def _(g):
                    if qargs is not None:
                        g.memset(qonesf[:, :], 1.0).then_inc(qthr, 1)
                        g.memset(qonesr[:, :], 1.0).then_inc(qthr, 1)
                        for i in range(QPT):
                            if i >= 2:
                                g.wait_ge(qv, 3 * (i - 1))
                            g.dma_start(
                                qstrip[:, (i % 2) * QH:(i % 2 + 1) * QH],
                                q_src(i)).then_inc(qdS2[i % 2], 16)
                            g.wait_ge(qv, 3 * i + 3)
                            g.dma_start(
                                q_dst(i),
                                qqb[:, (i % 2) * QH:(i % 2 + 1) * QH],
                            ).then_inc(qdW[i % 2], 16)
                        g.wait_ge(qdW[0], 16 * 128)
                        g.wait_ge(qdW[1], 16 * 128)

                @block.tensor
                def _(t):
                    t.wait_ge(dA, 16 * 11)
                    if qargs is not None:
                        t.wait_ge(qthr, 4)
                        t.matmul(psA[1][0:1, 0:1], qonesf[:, :],
                                 qred[:, 0:1], start=True, stop=True)
                        t.matmul(psA[1][0:1, 1:2], qonesf[:, :],
                                 qred[:, 1:2], start=True,
                                 stop=True).then_inc(qthr, 1)
                        t.wait_ge(qthr, 6)
                        t.matmul(psA[1][:, 4:6], qonesr[:, :], qtpq[:, :],
                                 start=True, stop=True).then_inc(qthr, 1)
                    for dt in range(ND):
                        t.wait_ge(dVr[dt % 2], 16 * (dt // 2 + 1))
                        if dt >= 1:
                            t.wait_ge(evV, 2 * dt)
                        for tt in range(NT):
                            inst = t.transpose(
                                ptrB[tt // 8][:, (tt % 8) * P:(tt % 8 + 1) * P],
                                vtile[:, (dt % 2) * T + tt * P:
                                      (dt % 2) * T + (tt + 1) * P],
                                eyeb[:, :])
                            if tt % 8 == 7:
                                inst.then_inc(trV, 1)
                    for qb in (range(NT) if att != "none" else []):
                        t.wait_ge(qld[qb % 2], 16 * (qb // 2 + 1))
                        if qb == 0:
                            t.wait_ge(evV, 2 * NT)
                        nchk = qb // 4 + 1
                        row = (qb + 1) * P
                        for h in range(ND):
                            gidx = qb * ND + h
                            t.wait_ge(sE, gidx)
                            for c in range(nchk):
                                w = min(CH, row - c * CH)
                                inst = t.matmul(
                                    psS[:, c * CH:c * CH + w],
                                    qblk[:, (qb % 2) * T + h * P:
                                         (qb % 2) * T + (h + 1) * P],
                                    Kreg[:, h * T + c * CH:h * T + c * CH + w],
                                    start=True, stop=True)
                            inst.then_inc(mmS, 1)
                            if att == "s":
                                continue
                            t.wait_ge(sE, gidx + 1)
                            t.wait_ge(evP, gidx)
                            for j in range(qb + 1):
                                inst = t.transpose(
                                    ptrB[j // 8][:, (j % 8) * P:(j % 8 + 1) * P],
                                    Psb[:, j * P:(j + 1) * P], eyeb[:, :])
                            inst.then_inc(trP, 1)
                            if att == "pt":
                                continue
                            t.wait_ge(evP, gidx + 1)
                            if att == "full":
                                t.wait_ge(evO, gidx)
                            for j in range(qb + 1):
                                inst = t.matmul(
                                    psO,
                                    PTsb[:, j * P:(j + 1) * P],
                                    big[:, j * T + h * P:j * T + (h + 1) * P],
                                    start=(j == 0), stop=(j == qb))
                            inst.then_inc(mmO, 1)

                @block.scalar
                def _(s_):
                    if qargs is not None:
                        s_.wait_ge(qthr, 5)
                        s_.activation(qtpq[:, :], psA[1][0:1, 0:2], AF.Copy,
                                      scale=0.7 / 16777216.0).then_inc(qthr, 1)
                        s_.wait_ge(qthr, 7)
                        s_.activation(qth1[:, 0:1], psA[1][:, 4:5], AF.Copy)
                        s_.activation(qth2[:, 0:1], psA[1][:, 5:6], AF.Copy)
                        s_.activation(qth1[:, 1:2], psA[1][:, 4:5], AF.Copy,
                                      scale=-1.0)
                        s_.activation(qth2[:, 1:2], psA[1][:, 5:6], AF.Copy,
                                      scale=-1.0).then_inc(qthr, 1)
                    for qb in (range(NT) if att != "none" else []):
                        row = (qb + 1) * P
                        for h in range(ND):
                            gidx = qb * ND + h
                            s_.wait_ge(sS, gidx + 1)
                            if att == "full":
                                s_.wait_ge(rc, gidx)
                            s_.activation(Psb[:, 0:row], psS[:, 0:row],
                                          AF.Exp, bias=nm[:, :],
                                          scale=QK_SCALE,
                                          accum_out=lsum[:, :]).then_inc(sE, 1)
                            if att == "s":
                                continue
                            s_.wait_ge(trP, gidx + 1)
                            nbk = 1 if qb < 8 else 2
                            for bk_ in range(nbk):
                                wpt = min(8 * P, row - bk_ * 8 * P)
                                inst = s_.activation(
                                    PTsb[:, bk_ * 8 * P:bk_ * 8 * P + wpt],
                                    ptrB[bk_][:, 0:wpt], AF.Copy)
                            inst.then_inc(evP, 1)
                            if att in ("pt", "pv"):
                                continue
                            s_.wait_ge(mmO, gidx + 1)
                            s_.wait_ge(rc, gidx + 1)
                            if h == 0 and qb >= 2:
                                s_.wait_ge(dOs[qb % 2], 16 * (qb // 2))
                            s_.activation(
                                Otile[:, (qb % 2) * D + h * P:
                                      (qb % 2) * D + (h + 1) * P],
                                psO, AF.Identity,
                                scale=linv[:, :]).then_inc(evO, 1)
                            if h == ND - 1:
                                s_.wait_ge(evO, 16 * (qb + 1))
                                s_.dma_start(
                                    os_[qb, :, :],
                                    Otile[:, (qb % 2) * D:(qb % 2 + 1) * D],
                                ).then_inc(dOs[qb % 2], 16)

                @block.vector
                def _(v):
                    if qargs is not None:
                        v.wait_ge(qsc, 128)
                        v.reduce_sum(qred[:, 0:1], qpb[:, 0:128],
                                     axis=AX.X).then_inc(qthr, 1)
                        v.wait_ge(qsc, QPT)
                        v.reduce_sum(qred[:, 1:2], qpb[:, 128:QPT],
                                     axis=AX.X).then_inc(qthr, 1)
                    for dt in range(ND):
                        for half in range(2):
                            v.wait_ge(trV, 2 * dt + half + 1)
                            v.tensor_copy(
                                bass.AP(big, half * 8 * T + dt * P,
                                        [[ND * T, P], [T, 8], [1, P]]),
                                ptrB[half][:, :]).then_inc(evV, 1)
                    for qb in (range(NT) if att != "none" else []):
                        ci = qb // 4
                        vvar = qb % 4
                        wd_ = (vvar + 1) * P
                        row = (qb + 1) * P
                        for h in range(ND):
                            gidx = qb * ND + h
                            v.wait_ge(mmS, gidx + 1)
                            dst = psS[:, ci * CH:ci * CH + wd_]
                            v.tensor_tensor(
                                dst, dst,
                                maskb[:, vvar * CH:vvar * CH + wd_],
                                op=ALU.add).then_inc(aux1, 1)
                            v.wait_ge(aux1, 2 * gidx + 1)
                            v.reduce_max(rm[:, :], psS[:, 0:row], axis=AX.X,
                                         negate=True).then_inc(aux1, 1)
                            v.wait_ge(aux1, 2 * gidx + 2)
                            v.tensor_scalar(nm[:, :], rm[:, :], QK_SCALE,
                                            None, op0=ALU.mult).then_inc(sS, 1)
                            if att != "full":
                                continue
                            v.wait_ge(sE, gidx + 1)
                            if gidx >= 1:
                                v.wait_ge(evO, gidx)
                            v.reciprocal(linv[:, :],
                                         lsum[:, :]).then_inc(rc, 1)
                        if qargs is not None:
                            if qb == 0:
                                v.wait_ge(qthr, 8)
                            for i in q_batch(qb, NT):
                                tcol = qth1 if i < 128 else qth2
                                v.wait_ge(qdS2[i % 2], 16 * (i // 2 + 1))
                                if i >= 2:
                                    v.wait_ge(qdW[i % 2], 16 * (i // 2))
                                v.tensor_scalar(
                                    qab[:, :],
                                    qstrip[:, (i % 2) * QH:(i % 2 + 1) * QH],
                                    tcol[:, 0:1], None,
                                    op0=ALU.is_ge).then_inc(qv, 1)
                                v.tensor_scalar(
                                    qbb[:, :],
                                    qstrip[:, (i % 2) * QH:(i % 2 + 1) * QH],
                                    tcol[:, 1:2], None,
                                    op0=ALU.is_le).then_inc(qv, 1)
                                v.wait_ge(qv, 3 * i + 2)
                                v.tensor_tensor(
                                    qqb[:, (i % 2) * QH:(i % 2 + 1) * QH],
                                    qab[:, :], qbb[:, :],
                                    op=ALU.subtract).then_inc(qv, 1)

        if upto < 3:
            return nc
        with ExitStack() as dctx:
            sbd = lambda n_, s_, t_: dctx.enter_context(nc.sbuf_tensor(n_, s_, t_))
            sumh = sbd("p3_sumh", [P, 8 * D], F32)
            aT = sbd("p3_aT", [P, 2 * 1024], F32)
            psD = psA
            psT = [psS[:, 0:CH], psS[:, CH:2 * CH]]

            with nc.Block() as block:
                @block.sync
                def _(sy):
                    for tt in range(NT):
                        if tt >= 2:
                            sy.wait_ge(trD, 2 * (tt - 1))
                        sy.dma_start(
                            qtile[:, (tt % 2) * T:(tt % 2 + 1) * T],
                            os_[tt, :, :]).then_inc(dOl[tt % 2], 16)
                    if d_lvl == "full":
                        for th in range(2):
                            if th == 1:
                                sy.wait_ge(dR, 16 * 8)
                            for tl in range(8):
                                sy.dma_start(
                                    sumh[:, tl * D:(tl + 1) * D],
                                    R2[(th * 8 + tl) * P:
                                       (th * 8 + tl + 1) * P, :],
                                ).then_inc(dC, 16)
                            for tl in range(8):
                                gt = th * 8 + tl
                                sy.wait_ge(lnw, gt + 1)
                                sy.dma_start(
                                    R3[gt * P:(gt + 1) * P, :],
                                    sumh[:, tl * D:(tl + 1) * D],
                                ).then_inc(dR, 16)
                        sy.wait_ge(dR, 16 * NT)

                @block.gpsimd
                def _(g):
                    for th in (range(2) if d_lvl != "ot" else []):
                        for dt in range(ND):
                            s2 = th * ND + dt
                            if s2 >= 2:
                                g.wait_ge(mmD, 2 * (s2 - 1))
                            g.dma_start(
                                wbuf[:, (s2 % 2) * D:(s2 % 2 + 1) * D],
                                wo[dt, :, :]).then_inc(wld2[s2 % 2], 16)

                @block.tensor
                def _(t):
                    for tt in range(NT):
                        t.wait_ge(dOl[tt % 2], 16 * (tt // 2 + 1))
                        if tt >= 1:
                            t.wait_ge(evD, 2 * tt)
                        for dt in range(ND):
                            inst = t.transpose(
                                ptrB[dt // 8][:, (dt % 8) * P:(dt % 8 + 1) * P],
                                qtile[:, (tt % 2) * T + dt * P:
                                      (tt % 2) * T + (dt + 1) * P],
                                eyeb[:, :])
                            if dt % 8 == 7:
                                inst.then_inc(trD, 1)
                    if d_lvl != "ot":
                        t.wait_ge(evD, 2 * NT)
                        for th in range(2):
                            for dt in range(ND):
                                s2 = th * ND + dt
                                t.wait_ge(wld2[s2 % 2], 16 * (s2 // 2 + 1))
                                for c in range(2):
                                    m = 2 * s2 + c
                                    if m >= 2:
                                        t.wait_ge(evE, m - 1)
                                    for kt in range(ND):
                                        inst = t.matmul(
                                            psD[m % 2][:, :],
                                            wbuf[:, (s2 % 2) * D + kt * P:
                                                 (s2 % 2) * D + (kt + 1) * P],
                                            big[:, kt * T + th * 1024 + c * CH:
                                                kt * T + th * 1024 +
                                                (c + 1) * CH],
                                            start=(kt == 0),
                                            stop=(kt == ND - 1))
                                    inst.then_inc(mmD, 1)
                                if d_lvl == "mm":
                                    continue
                                t.wait_ge(evE, 2 * s2 + 2)
                                for pk in range(2):
                                    if s2 >= 1:
                                        t.wait_ge(addC, 2 * (s2 - 1) + pk + 1)
                                    for tq4 in range(4):
                                        tq = pk * 4 + tq4
                                        inst = t.transpose(
                                            psT[pk][:, tq4 * P:(tq4 + 1) * P],
                                            aT[:, (dt % 2) * 1024 + tq * P:
                                               (dt % 2) * 1024 + (tq + 1) * P],
                                            eyef[:, :])
                                    inst.then_inc(trC, 1)

                @block.scalar
                def _(s_):
                    for th in (range(2) if d_lvl != "ot" else []):
                        for dt in range(ND):
                            s2 = th * ND + dt
                            if s2 >= 2 and d_lvl == "full":
                                s_.wait_ge(trC, 2 * (s2 - 1))
                            for c in range(2):
                                m = 2 * s2 + c
                                s_.wait_ge(mmD, m + 1)
                                s_.activation(
                                    aT[:, (dt % 2) * 1024 + c * CH:
                                       (dt % 2) * 1024 + (c + 1) * CH],
                                    psD[m % 2][:, :], AF.Identity,
                                    bias=bob[:, dt:dt + 1]).then_inc(evE, 1)
                        if d_lvl == "mm":
                            continue
                        for tl in range(8):
                            gt = th * 8 + tl
                            s_.wait_ge(lnv, 2 * (gt + 1))
                            s_.activation(lrstd[:, :], mvb[:, 1:2], AF.Ln,
                                          bias=lnepsb[:, :]).then_inc(lnx, 1)
                            s_.wait_ge(lnx, gt + 1)
                            s_.activation(lrstd[:, :], lrstd[:, :], AF.Exp,
                                          scale=-0.5).then_inc(lnr, 1)

                @block.vector
                def _(v):
                    for tt in range(NT):
                        for half in range(2):
                            v.wait_ge(trD, 2 * tt + half + 1)
                            v.tensor_copy(
                                bass.AP(big, half * 8 * T + tt * P,
                                        [[ND * T, P], [T, 8], [1, P]]),
                                ptrB[half][:, :]).then_inc(evD, 1)
                    for th in (range(2) if d_lvl == "full" else []):
                        v.wait_ge(dC, 16 * 8 * (th + 1))
                        for dt in range(ND):
                            s2 = th * ND + dt
                            for pk in range(2):
                                v.wait_ge(trC, 2 * s2 + pk + 1)
                                dst = bass.AP(sumh, pk * 4 * D + dt * P,
                                              [[8 * D, P], [D, 4], [1, P]])
                                v.tensor_tensor(dst, psT[pk][:, :], dst,
                                                op=ALU.add).then_inc(addC, 1)
                        v.wait_ge(addC, 32 * (th + 1))
                        for tl in range(8):
                            gt = th * 8 + tl
                            tile = sumh[:, tl * D:(tl + 1) * D]
                            for c4 in range(4):
                                inst = v.bn_stats(
                                    bn6[:, c4 * 6:(c4 + 1) * 6],
                                    tile[:, c4 * CH:(c4 + 1) * CH])
                            inst.then_inc(lnv, 1)
                            v.wait_ge(lnv, 2 * gt + 1)
                            v.bn_aggr(mvb[:, :], bn6[:, :]).then_inc(lnv, 1)
                            v.wait_ge(lnr, gt + 1)
                            v.tensor_scalar(tile, tile, mvb[:, 0:1],
                                            lrstd[:, :], op0=ALU.subtract,
                                            op1=ALU.mult).then_inc(lnw, 1)
    return nc


def host_prep_p3(wq, wk, wv, wo, bq, bk, bv, bo):
    sl = lambda W: slab_weights(np.asarray(W, np.float32).T, osz=P)
    rb = lambda b: np.ascontiguousarray(
        np.asarray(b, np.float32).reshape(16, P).T)
    masks = np.zeros((4, P, CH), np.float32)
    for v_ in range(4):
        r = np.arange(P)[:, None]
        c = np.arange(CH)[None, :]
        masks[v_] = np.where(c <= v_ * P + r, 0.0, -1e30)
    return (sl(wq), sl(wk), sl(wv), sl(wo),
            rb(bq), rb(bk), rb(bv), rb(bo), masks)


def _mk_phase3_nc(upto=3, att="full", d_lvl="full"):
    nc = bass.Bass("TRN2", target_bir_lowering=False)
    R2 = nc.dram_tensor("R2", [T, D], F32, kind="ExternalInput")
    wq = nc.dram_tensor("wq", [16, P, D], F32, kind="ExternalInput")
    wk = nc.dram_tensor("wk", [16, P, D], F32, kind="ExternalInput")
    wv = nc.dram_tensor("wv", [16, P, D], F32, kind="ExternalInput")
    wo = nc.dram_tensor("wo", [16, P, D], F32, kind="ExternalInput")
    nw = nc.dram_tensor("nw", [D], F32, kind="ExternalInput")
    bq = nc.dram_tensor("bq", [P, 16], F32, kind="ExternalInput")
    bk = nc.dram_tensor("bk", [P, 16], F32, kind="ExternalInput")
    bv = nc.dram_tensor("bv", [P, 16], F32, kind="ExternalInput")
    bo = nc.dram_tensor("bo", [P, 16], F32, kind="ExternalInput")
    masks = nc.dram_tensor("masks", [4, P, CH], F32, kind="ExternalInput")
    eye32 = nc.dram_tensor("eye32", [P, P], F32, kind="ExternalInput")
    eye16 = nc.dram_tensor("eye16", [P, P], BF16, kind="ExternalInput")
    R3 = nc.dram_tensor("R3", [T, D], F32, kind="ExternalOutput")
    build_phase3(nc, R2, wq, wk, wv, wo, nw, bq, bk, bv, bo, masks,
                 eye32, eye16, R3, upto=upto, att=att, d_lvl=d_lvl)
    return nc


def run_phase3(R2_full, norm_w, wq, bq, wk, bk, wv, bv, wo, bo,
               cores=N_CORES, upto=3, att="full", d_lvl="full"):
    wqs, wks, wvs, wos, bqb, bkb, bvb, bob, masks = host_prep_p3(
        wq, wk, wv, wo, bq, bk, bv, bo)
    nc = _mk_phase3_nc(upto=upto, att=att, d_lvl=d_lvl)
    in_maps = []
    for c in range(cores):
        in_maps.append(dict(
            R2=np.ascontiguousarray(R2_full[c], np.float32),
            wq=wqs, wk=wks, wv=wvs, wo=wos,
            nw=np.ascontiguousarray(norm_w, np.float32),
            bq=bqb, bk=bkb, bv=bvb, bo=bob, masks=masks,
            eye32=_EYE32, eye16=_EYE16))
    res = run_bass_kernel_spmd(nc, in_maps, core_ids=list(range(cores)))
    return np.stack([r["R3"] for r in res.results])




def build_phase4(nc, R3, w1t, w2t, b1, b2, eye32, eye16, OUT,
                 do_quant=True, w1q=None, w2q=None):
    _clear_sems(nc)
    if w1q is None:
        w1q = nc.dram_tensor("p4_w1q", [64, P, 2048], BF16)
        w2q = nc.dram_tensor("p4_w2q", [2, 16, P, 4096], BF16)
    HT = 1024
    NIT = 32
    NP1 = 64
    NPT = 128
    NW1 = 16777216.0
    NW2 = 16777216.0

    def w1_chunk(th, ip, ls, c):
        return th * 192 + ip * 96 + 2 * ls + c

    def w2_chunk(th, ip, dt, c):
        return th * 192 + ip * 96 + 64 + 2 * dt + c

    with ExitStack() as ctx:
        sb = lambda n_, s_, t_: ctx.enter_context(nc.sbuf_tensor(n_, s_, t_))
        psl = lambda n_, s_, t_: ctx.enter_context(nc.psum_tensor(n_, s_, t_))
        sem = lambda n_: ctx.enter_context(nc.semaphore(n_))

        eyeb = sb("p4_eyeb", [P, P], BF16)
        eyef = sb("p4_eyef", [P, P], F32)
        onesf = sb("p4_onesf", [P, 1], F32)
        onesr = sb("p4_onesr", [1, P], F32)
        mvb = sb("p4_mv", [P, 2], F32)
        bn6 = sb("p4_bn6", [P, 24], F32)
        rstd = sb("p4_rstd", [P, 1], F32)
        lnepsb = sb("p4_lneps", [P, 1], F32)
        b1b = sb("p4_b1b", [P, 64], F32)
        b2b = sb("p4_b2b", [P, 16], F32)
        psA = [psl(f"p4_psA{i}", [P, CH], F32) for i in range(4)]
        ptr = [psl(f"p4_ptr{i}", [P, 8 * P], BF16) for i in range(2)]
        psT = [psl(f"p4_psT{i}", [P, CH], F32) for i in range(2)]

        dA = sem("p4_dA")
        dAx = sem("p4_dAx")
        dS = [sem("p4_dS0"), sem("p4_dS1")]
        qsc = sem("p4_qsc")
        qv = sem("p4_qv")
        qg = sem("p4_qg")
        dQw = [sem("p4_dQw0"), sem("p4_dQw1")]
        thr = sem("p4_thr")
        trA = sem("p4_trA")
        evA = sem("p4_evA")
        wld1 = [sem("p4_wld10"), sem("p4_wld11")]
        wld2 = [sem("p4_wld20"), sem("p4_wld21")]
        mm = sem("p4_mm")
        ev = sem("p4_ev")
        trC = sem("p4_trC")
        addC = sem("p4_addC")
        lnv = sem("p4_lnv")
        lnr = sem("p4_lnr")
        lnx = sem("p4_lnx")
        lnw = sem("p4_lnw")
        dC = sem("p4_dC")
        dR = sem("p4_dR")
        msc = sem("p4_msc")

        def src_ap(i):
            if i < NP1:
                return w1t[i, :, :]
            j = i - NP1
            ip, dt, hf = j // 32, (j % 32) // 2, j % 2
            return w2t[ip, dt, :, hf * D:(hf + 1) * D]

        def dst_ap(i):
            if i < NP1:
                return w1q[i, :, :]
            j = i - NP1
            ip, dt, hf = j // 32, (j % 32) // 2, j % 2
            return w2q[ip, dt, :, hf * D:(hf + 1) * D]

        qctx_mgr = ExitStack() if do_quant else None
        if do_quant:
          with ExitStack() as qctx:
            sbq = lambda n_, s_, t_: qctx.enter_context(nc.sbuf_tensor(n_, s_, t_))
            strip = sbq("p4_strip", [P, 2 * D], F32)
            qa = sbq("p4_qa", [P, D], BF16)
            qb_ = sbq("p4_qb", [P, D], BF16)
            qq = sbq("p4_qq", [P, 2 * D], BF16)
            pb = sbq("p4_pb", [P, NPT], F32)
            red = sbq("p4_red", [P, 2], F32)
            tpq = sbq("p4_tpq", [1, 2], F32)
            th1 = sbq("p4_th1", [P, 2], F32)
            th2 = sbq("p4_th2", [P, 2], F32)

            with nc.Block() as block:
                @block.sync
                def _(sy):
                    sy.dma_start(eyef[:, :], eye32[:, :]).then_inc(dA, 16)
                    sy.dma_start(eyeb[:, :], eye16[:, :]).then_inc(dA, 16)
                    sy.dma_start(b1b[:, :], b1[:, :]).then_inc(dA, 16)
                    sy.dma_start(b2b[:, :], b2[:, :]).then_inc(dA, 16)
                    for i in range(NPT):
                        if i >= 2:
                            sy.wait_ge(qsc, i - 1)
                        sy.dma_start(strip[:, (i % 2) * D:(i % 2 + 1) * D],
                                     src_ap(i)).then_inc(dS[i % 2], 16)
                    for i in range(NPT):
                        if i < 2:
                            sy.wait_ge(qsc, NPT)
                        else:
                            sy.wait_ge(qv, 3 * (i - 1))
                        sy.dma_start(strip[:, (i % 2) * D:(i % 2 + 1) * D],
                                     src_ap(i)).then_inc(dS[i % 2], 16)
                        sy.wait_ge(qv, 3 * i + 3)
                        sy.dma_start(dst_ap(i),
                                     qq[:, (i % 2) * D:(i % 2 + 1) * D],
                                     ).then_inc(dQw[i % 2], 16)
                    sy.wait_ge(dQw[0], 16 * 64)
                    sy.wait_ge(dQw[1], 16 * 64)

                @block.gpsimd
                def _(g):
                    g.memset(lnepsb[:, :], LN_EPS).then_inc(msc, 1)
                    g.memset(onesf[:, :], 1.0).then_inc(msc, 1)
                    g.memset(onesr[:, :], 1.0).then_inc(msc, 1)
                @block.scalar
                def _(s_):
                    for i in range(NPT):
                        s_.wait_ge(dS[i % 2], 16 * (i // 2 + 1))
                        s_.activation(
                            strip[:, (i % 2) * D:(i % 2 + 1) * D],
                            strip[:, (i % 2) * D:(i % 2 + 1) * D],
                            AF.Abs,
                            accum_out=pb[:, i:i + 1]).then_inc(qsc, 1)
                    s_.wait_ge(thr, 3)
                    s_.activation(tpq[:, :], psT[0][0:1, 0:2], AF.Copy,
                                  scale=0.7 / NW1).then_inc(thr, 1)
                    s_.wait_ge(thr, 5)
                    s_.activation(th1[:, 0:1], psT[1][:, 0:1], AF.Copy)
                    s_.activation(th2[:, 0:1], psT[1][:, 1:2], AF.Copy)
                    s_.activation(th1[:, 1:2], psT[1][:, 0:1], AF.Copy,
                                  scale=-1.0)
                    s_.activation(th2[:, 1:2], psT[1][:, 1:2], AF.Copy,
                                  scale=-1.0).then_inc(thr, 1)

                @block.vector
                def _(v):
                    v.wait_ge(qsc, NP1)
                    v.reduce_sum(red[:, 0:1], pb[:, 0:NP1],
                                 axis=AX.X).then_inc(thr, 1)
                    v.wait_ge(qsc, NPT)
                    v.reduce_sum(red[:, 1:2], pb[:, NP1:NPT],
                                 axis=AX.X).then_inc(thr, 1)
                    v.wait_ge(thr, 6)
                    for i in range(NPT):
                        tcol = th1 if i < NP1 else th2
                        v.wait_ge(dS[i % 2], 16 * (NPT // 2 + i // 2 + 1))
                        if i >= 2:
                            v.wait_ge(dQw[i % 2], 16 * (i // 2))
                        v.tensor_scalar(
                            qa[:, :],
                            strip[:, (i % 2) * D:(i % 2 + 1) * D],
                            tcol[:, 0:1], None,
                            op0=ALU.is_ge).then_inc(qv, 1)
                        v.tensor_scalar(
                            qb_[:, :],
                            strip[:, (i % 2) * D:(i % 2 + 1) * D],
                            tcol[:, 1:2], None,
                            op0=ALU.is_le).then_inc(qv, 1)
                        v.wait_ge(qv, 3 * i + 2)
                        v.tensor_tensor(qq[:, (i % 2) * D:(i % 2 + 1) * D],
                                        qa[:, :], qb_[:, :],
                                        op=ALU.subtract).then_inc(qv, 1)

                @block.tensor
                def _(t):
                    t.wait_ge(dA, 64)
                    t.wait_ge(msc, 3)
                    t.wait_ge(thr, 1)
                    t.matmul(psT[0][0:1, 0:1], onesf[:, :], red[:, 0:1],
                             start=True, stop=True)
                    t.wait_ge(thr, 2)
                    t.matmul(psT[0][0:1, 1:2], onesf[:, :], red[:, 1:2],
                             start=True, stop=True).then_inc(thr, 1)
                    t.wait_ge(thr, 4)
                    t.matmul(psT[1][:, 0:2], onesr[:, :], tpq[:, :],
                             start=True, stop=True).then_inc(thr, 1)

        with ExitStack() as mctx:
            sbm = lambda n_, s_, t_: mctx.enter_context(nc.sbuf_tensor(n_, s_, t_))
            h = sbm("p4_h", [P, NIT * HT], BF16)
            xT = sbm("p4_xT", [P, ND * HT], BF16)
            sumh = sbm("p4_sumh", [P, 8 * D], F32)
            wbuf1 = sbm("p4_wbuf1", [P, 2 * D], BF16)
            wbuf2 = sbm("p4_wbuf2", [P, 2 * 4096], BF16)
            aT = sbm("p4_aT", [P, 2 * HT], F32)
            xtmb = sbm("p4_xtmb", [P, D], BF16)
            th1b = sbm("p4_th1b", [P, 2], F32)

            with nc.Block() as block:
                @block.sync
                def _(sy):
                    if not do_quant:
                        sy.dma_start(eyef[:, :], eye32[:, :]).then_inc(dA, 16)
                        sy.dma_start(eyeb[:, :], eye16[:, :]).then_inc(dA, 16)
                        sy.dma_start(b1b[:, :], b1[:, :]).then_inc(dA, 16)
                        sy.dma_start(b2b[:, :], b2[:, :]).then_inc(dA, 16)
                    for th in range(2):
                        if th == 1:
                            sy.wait_ge(dR, 16 * 8)
                        for tl in range(8):
                            sy.dma_start(
                                sumh[:, tl * D:(tl + 1) * D],
                                R3[(th * 8 + tl) * P:
                                   (th * 8 + tl + 1) * P, :],
                            ).then_inc(dC, 16)
                        for ip in range(2):
                            for ls in range(NIT):
                                gs = (th * 2 + ip) * NIT + ls
                                if gs >= 2:
                                    g2 = gs - 2
                                    sy.wait_ge(mm, w1_chunk(
                                        g2 // 64, (g2 % 64) // 32,
                                        g2 % 32, 1) + 1)
                                it = ip * NIT + ls
                                sy.dma_start(
                                    wbuf1[:, (gs % 2) * D:(gs % 2 + 1) * D],
                                    w1q[it, :, :],
                                ).then_inc(wld1[gs % 2], 16)
                            for dt in range(ND):
                                ws = (th * 2 + ip) * ND + dt
                                if ws >= 2:
                                    w2_ = ws - 2
                                    sy.wait_ge(mm, w2_chunk(
                                        w2_ // 32, (w2_ % 32) // 16,
                                        w2_ % 16, 1) + 1)
                                sy.dma_start(
                                    wbuf2[:, (ws % 2) * 4096:
                                          (ws % 2 + 1) * 4096],
                                    w2q[ip, dt, :, :],
                                ).then_inc(wld2[ws % 2], 16)
                        for tl in range(8):
                            gt = th * 8 + tl
                            sy.wait_ge(lnw, gt + 1)
                            sy.dma_start(
                                OUT[gt * P:(gt + 1) * P, :],
                                sumh[:, tl * D:(tl + 1) * D],
                            ).then_inc(dR, 16)
                    sy.wait_ge(dR, 16 * NT)

                @block.gpsimd
                def _(g):
                    g.memset(lnepsb[:, :], LN_EPS).then_inc(msc, 1)
                    for tt in range(NT):
                        if tt >= 1:
                            g.wait_ge(trA, 2 * tt)
                        g.dma_start(
                            xtmb[:, :],
                            R3[tt * P:(tt + 1) * P, :]).then_inc(dAx, 16)

                @block.scalar
                def _(s_):
                    s_.wait_ge(dA, 64)
                    for th in range(2):
                        for ip in range(2):
                            for ls in range(NIT):
                                il = ip * NIT + ls
                                for c in range(2):
                                    m = w1_chunk(th, ip, ls, c)
                                    s_.wait_ge(mm, m + 1)
                                    s_.activation(
                                        h[:, ls * HT + c * CH:
                                          ls * HT + (c + 1) * CH],
                                        psA[m % 4][:, :], AF.Gelu,
                                        bias=b1b[:, il:il + 1],
                                    ).then_inc(ev, 1)
                            for dt in range(ND):
                                s2g = (th * 2 + ip) * ND + dt
                                if s2g >= 2:
                                    s_.wait_ge(trC, 2 * (s2g - 1))
                                for c in range(2):
                                    m = w2_chunk(th, ip, dt, c)
                                    s_.wait_ge(mm, m + 1)
                                    if ip == 1:
                                        s_.activation(
                                            aT[:, (dt % 2) * HT + c * CH:
                                               (dt % 2) * HT + (c + 1) * CH],
                                            psA[m % 4][:, :], AF.Identity,
                                            bias=b2b[:, dt:dt + 1],
                                        ).then_inc(ev, 1)
                                    else:
                                        s_.activation(
                                            aT[:, (dt % 2) * HT + c * CH:
                                               (dt % 2) * HT + (c + 1) * CH],
                                            psA[m % 4][:, :],
                                            AF.Copy).then_inc(ev, 1)
                        for tl in range(8):
                            gt = th * 8 + tl
                            s_.wait_ge(lnv, 2 * (gt + 1))
                            if gt == 0:
                                s_.wait_ge(msc, 1)
                            s_.activation(rstd[:, :], mvb[:, 1:2], AF.Ln,
                                          bias=lnepsb[:, :]).then_inc(lnx, 1)
                            s_.wait_ge(lnx, gt + 1)
                            s_.activation(rstd[:, :], rstd[:, :], AF.Exp,
                                          scale=-0.5).then_inc(lnr, 1)

                @block.vector
                def _(v):
                    for th in range(2):
                        for tl in range(8):
                            tt = th * 8 + tl
                            if tt == 8:
                                v.wait_ge(mm, w1_chunk(0, 1, 31, 1) + 1)
                            for half in range(2):
                                v.wait_ge(trA, 2 * tt + half + 1)
                                v.tensor_copy(
                                    bass.AP(xT, half * 8 * HT + tl * P,
                                            [[ND * HT, P], [HT, 8], [1, P]]),
                                    ptr[half][:, :]).then_inc(evA, 1)
                        v.wait_ge(dC, 16 * 8 * (th + 1))
                        for ip in range(2):
                            for dt in range(ND):
                                s2g = (th * 2 + ip) * ND + dt
                                for pk in range(2):
                                    v.wait_ge(trC, 2 * s2g + pk + 1)
                                    dst = bass.AP(sumh,
                                                  pk * 4 * D + dt * P,
                                                  [[8 * D, P], [D, 4],
                                                   [1, P]])
                                    v.tensor_tensor(
                                        dst, psT[pk][:, :], dst,
                                        op=ALU.add).then_inc(addC, 1)
                        v.wait_ge(addC, 64 * (th + 1))
                        for tl in range(8):
                            gt = th * 8 + tl
                            tile = sumh[:, tl * D:(tl + 1) * D]
                            for c4 in range(4):
                                inst = v.bn_stats(
                                    bn6[:, c4 * 6:(c4 + 1) * 6],
                                    tile[:, c4 * CH:(c4 + 1) * CH])
                            inst.then_inc(lnv, 1)
                            v.wait_ge(lnv, 2 * gt + 1)
                            v.bn_aggr(mvb[:, :], bn6[:, :]).then_inc(lnv, 1)
                            v.wait_ge(lnr, gt + 1)
                            v.tensor_scalar(tile, tile, mvb[:, 0:1],
                                            rstd[:, :], op0=ALU.subtract,
                                            op1=ALU.mult).then_inc(lnw, 1)

                @block.tensor
                def _(t):
                    t.wait_ge(dA, 64)
                    for th in range(2):
                        for tl in range(8):
                            tt = th * 8 + tl
                            t.wait_ge(dAx, 16 * (tt + 1))
                            if tt >= 1:
                                t.wait_ge(evA, 2 * tt)
                            for dt in range(ND):
                                inst = t.transpose(
                                    ptr[dt // 8][:, (dt % 8) * P:
                                                 (dt % 8 + 1) * P],
                                    xtmb[:, dt * P:(dt + 1) * P], eyeb[:, :])
                                if dt % 8 == 7:
                                    inst.then_inc(trA, 1)
                        t.wait_ge(evA, 2 * (th * 8 + 8))
                        for ip in range(2):
                            for ls in range(NIT):
                                gs = (th * 2 + ip) * NIT + ls
                                t.wait_ge(wld1[gs % 2], 16 * (gs // 2 + 1))
                                for c in range(2):
                                    m = w1_chunk(th, ip, ls, c)
                                    if m >= 4:
                                        t.wait_ge(ev, m - 3)
                                    for kt in range(ND):
                                        inst = t.matmul(
                                            psA[m % 4][:, :],
                                            wbuf1[:, (gs % 2) * D + kt * P:
                                                  (gs % 2) * D +
                                                  (kt + 1) * P],
                                            xT[:, kt * HT + c * CH:
                                               kt * HT + (c + 1) * CH],
                                            start=(kt == 0),
                                            stop=(kt == ND - 1))
                                    inst.then_inc(mm, 1)
                            t.wait_ge(ev, th * 192 + ip * 96 + 64)
                            for dt in range(ND):
                                ws = (th * 2 + ip) * ND + dt
                                s2g = ws
                                t.wait_ge(wld2[ws % 2], 16 * (ws // 2 + 1))
                                for c in range(2):
                                    m = w2_chunk(th, ip, dt, c)
                                    t.wait_ge(ev, m - 3)
                                    for kt in range(NIT):
                                        inst = t.matmul(
                                            psA[m % 4][:, :],
                                            wbuf2[:, (ws % 2) * 4096 +
                                                  kt * P:
                                                  (ws % 2) * 4096 +
                                                  (kt + 1) * P],
                                            h[:, kt * HT + c * CH:
                                              kt * HT + (c + 1) * CH],
                                            start=(kt == 0),
                                            stop=(kt == NIT - 1))
                                    inst.then_inc(mm, 1)
                                t.wait_ge(ev, w2_chunk(th, ip, dt, 1) + 1)
                                for pk in range(2):
                                    if s2g >= 1:
                                        t.wait_ge(addC,
                                                  2 * (s2g - 1) + pk + 1)
                                    for tq4 in range(4):
                                        tq = pk * 4 + tq4
                                        inst = t.transpose(
                                            psT[pk][:, tq4 * P:
                                                    (tq4 + 1) * P],
                                            aT[:, (dt % 2) * HT + tq * P:
                                               (dt % 2) * HT +
                                               (tq + 1) * P],
                                            eyef[:, :])
                                    inst.then_inc(trC, 1)
    return nc


def host_prep_p4(w1, b1, w2, b2):
    w1t_s = slab_weights(np.asarray(w1, np.float32).T, osz=P)
    W2T = np.asarray(w2, np.float32).T
    w2t_s = np.stack([slab_weights(W2T[ip * 4096:(ip + 1) * 4096, :], osz=P)
                      for ip in range(2)])
    b1b = np.ascontiguousarray(np.asarray(b1, np.float32).reshape(64, P).T)
    b2b = np.ascontiguousarray(np.asarray(b2, np.float32).reshape(16, P).T)
    return w1t_s, np.ascontiguousarray(w2t_s), b1b, b2b


def _mk_phase4_nc():
    nc = bass.Bass("TRN2", target_bir_lowering=False)
    R3 = nc.dram_tensor("R3", [T, D], F32, kind="ExternalInput")
    w1t = nc.dram_tensor("w1t", [64, P, 2048], F32, kind="ExternalInput")
    w2t = nc.dram_tensor("w2t", [2, 16, P, 4096], F32, kind="ExternalInput")
    b1 = nc.dram_tensor("b1", [P, 64], F32, kind="ExternalInput")
    b2 = nc.dram_tensor("b2", [P, 16], F32, kind="ExternalInput")
    eye32 = nc.dram_tensor("eye32", [P, P], F32, kind="ExternalInput")
    eye16 = nc.dram_tensor("eye16", [P, P], BF16, kind="ExternalInput")
    OUT = nc.dram_tensor("OUT", [T, D], F32, kind="ExternalOutput")
    build_phase4(nc, R3, w1t, w2t, b1, b2, eye32, eye16, OUT)
    return nc


def run_phase4(R3_full, w1, b1, w2, b2, cores=N_CORES):
    w1t, w2t, b1b, b2b = host_prep_p4(w1, b1, w2, b2)
    nc = _mk_phase4_nc()
    in_maps = []
    for c in range(cores):
        in_maps.append(dict(R3=np.ascontiguousarray(R3_full[c], np.float32),
                            w1t=w1t, w2t=w2t, b1=b1b, b2=b2b,
                            eye32=_EYE32, eye16=_EYE16))
    res = run_bass_kernel_spmd(nc, in_maps, core_ids=list(range(cores)))
    return np.stack([r["OUT"] for r in res.results])



def _mk_phase1_nc():
    nc = bass.Bass("TRN2", target_bir_lowering=False)
    x = nc.dram_tensor("x", [T, D], F32, kind="ExternalInput")
    wqkv = nc.dram_tensor("wqkv", [48, P, D], F32, kind="ExternalInput")
    wout = nc.dram_tensor("wout", [16, P, D], F32, kind="ExternalInput")
    eye32 = nc.dram_tensor("eye32", [P, P], F32, kind="ExternalInput")
    eye16 = nc.dram_tensor("eye16", [P, P], BF16, kind="ExternalInput")
    R1 = nc.dram_tensor("R1", [T, D], F32, kind="ExternalOutput")
    build_phase1(nc, x, wqkv, wout, eye32, eye16, R1)
    return nc


_EYE32 = np.eye(P, dtype=np.float32)
_EYE16 = _EYE32.astype(ml_dtypes.bfloat16)


def host_prep_p1(W_qkv, W_out):
    WT = np.asarray(W_qkv, np.float32).T
    cols = []
    for i in range(ND):
        cols.append(WT[:, 4096 + i * P:4096 + (i + 1) * P])
        cols.append(WT[:, 2048 + i * P:2048 + (i + 1) * P])
        cols.append(WT[:, 0 + i * P:0 + (i + 1) * P])
    WTp = np.concatenate(cols, axis=1)
    wqkv_s = slab_weights(WTp, osz=P)
    wout_s = slab_weights(np.asarray(W_out, np.float32).T, osz=P)
    return wqkv_s, wout_s


def run_phase1(x_full, W_qkv, W_out, cores=N_CORES):
    wqkv_s, wout_s = host_prep_p1(W_qkv, W_out)
    nc = _mk_phase1_nc()
    in_maps = []
    for c in range(cores):
        in_maps.append(dict(x=np.ascontiguousarray(x_full[c], np.float32),
                            wqkv=wqkv_s, wout=wout_s,
                            eye32=_EYE32, eye16=_EYE16))
    res = run_bass_kernel_spmd(nc, in_maps, core_ids=list(range(cores)))
    return np.stack([r["R1"] for r in res.results])


def _mk_fused_nc(upto=4):
    nc = bass.Bass("TRN2", target_bir_lowering=False)
    x = nc.dram_tensor("x", [T, D], F32, kind="ExternalInput")
    wqkv = nc.dram_tensor("wqkv", [48, P, D], F32, kind="ExternalInput")
    wout = nc.dram_tensor("wout", [16, P, D], F32, kind="ExternalInput")
    wgu = nc.dram_tensor("wgu", [128, P, D], F32, kind="ExternalInput")
    wd = nc.dram_tensor("wd", [2, 16, P, 4096], F32, kind="ExternalInput")
    wq = nc.dram_tensor("wq", [16, P, D], F32, kind="ExternalInput")
    wk = nc.dram_tensor("wk", [16, P, D], F32, kind="ExternalInput")
    wv = nc.dram_tensor("wv", [16, P, D], F32, kind="ExternalInput")
    wo = nc.dram_tensor("wo", [16, P, D], F32, kind="ExternalInput")
    nw = nc.dram_tensor("nw", [D], F32, kind="ExternalInput")
    bq = nc.dram_tensor("bq", [P, 16], F32, kind="ExternalInput")
    bk = nc.dram_tensor("bk", [P, 16], F32, kind="ExternalInput")
    bv = nc.dram_tensor("bv", [P, 16], F32, kind="ExternalInput")
    bo = nc.dram_tensor("bo", [P, 16], F32, kind="ExternalInput")
    masks = nc.dram_tensor("masks", [4, P, CH], F32, kind="ExternalInput")
    w1t = nc.dram_tensor("w1t", [64, P, 2048], F32, kind="ExternalInput")
    w2t = nc.dram_tensor("w2t", [2, 16, P, 4096], F32, kind="ExternalInput")
    b1 = nc.dram_tensor("b1", [P, 64], F32, kind="ExternalInput")
    b2 = nc.dram_tensor("b2", [P, 16], F32, kind="ExternalInput")
    eye32 = nc.dram_tensor("eye32", [P, P], F32, kind="ExternalInput")
    eye16 = nc.dram_tensor("eye16", [P, P], BF16, kind="ExternalInput")
    R1 = nc.dram_tensor("R1", [T, D], F32)
    R2 = nc.dram_tensor("R2", [T, D], F32)
    R3 = nc.dram_tensor("R3", [T, D], F32)
    OUT = nc.dram_tensor("OUT", [T, D], F32, kind="ExternalOutput")
    w1q = nc.dram_tensor("w1q_s", [64, P, 2048], BF16)
    w2q = nc.dram_tensor("w2q_s", [2, 16, P, 4096], BF16)
    build_phase1(nc, x, wqkv, wout, eye32, eye16, R1)
    if upto >= 2:
        build_phase2(nc, R1, wgu, wd, eye32, eye16, R2)
    if upto >= 3:
        build_phase3(nc, R2, wq, wk, wv, wo, nw, bq, bk, bv, bo, masks,
                     eye32, eye16, R3, qargs=(w1t, w2t, w1q, w2q))
    if upto >= 4:
        build_phase4(nc, R3, w1t, w2t, b1, b2, eye32, eye16, OUT,
                     do_quant=False, w1q=w1q, w2q=w2q)
    return nc


def kernel(**inputs):
    x = np.asarray(inputs["L0_attn_x"], np.float32)
    wqkv_s, wout_s = host_prep_p1(inputs["L0_attn_W_qkv"],
                                  inputs["L0_attn_W_out"])
    wgu_s, wd_s = host_prep_p2(inputs["L0_ffn_w_gate_up"],
                               inputs["L0_ffn_w_down"])
    wqs, wks, wvs, wos, bqb, bkb, bvb, bob, masks = host_prep_p3(
        inputs["L1_attn_wq"], inputs["L1_attn_wk"], inputs["L1_attn_wv"],
        inputs["L1_attn_wo"], inputs["L1_attn_bq"], inputs["L1_attn_bk"],
        inputs["L1_attn_bv"], inputs["L1_attn_bo"])
    w1t, w2t, b1b, b2b = host_prep_p4(inputs["L1_ffn_w1"],
                                      inputs["L1_ffn_b1"],
                                      inputs["L1_ffn_w2"],
                                      inputs["L1_ffn_b2"])
    nw = np.ascontiguousarray(np.asarray(inputs["L1_attn_norm_w"],
                                         np.float32))
    nc = _mk_fused_nc()
    in_maps = []
    for c in range(N_CORES):
        in_maps.append(dict(
            x=np.ascontiguousarray(x[c]),
            wqkv=wqkv_s, wout=wout_s, wgu=wgu_s, wd=wd_s,
            wq=wqs, wk=wks, wv=wvs, wo=wos, nw=nw,
            bq=bqb, bk=bkb, bv=bvb, bo=bob, masks=masks,
            w1t=w1t, w2t=w2t, b1=b1b, b2=b2b,
            eye32=_EYE32, eye16=_EYE16))
    res = run_bass_kernel_spmd(nc, in_maps, core_ids=list(range(N_CORES)))
    return np.stack([r["OUT"] for r in res.results]).astype(np.float32)


def kernel_unfused(**inputs):
    x = np.asarray(inputs["L0_attn_x"], np.float32)
    R1 = run_phase1(x, inputs["L0_attn_W_qkv"], inputs["L0_attn_W_out"])
    R2 = run_phase2(R1, inputs["L0_ffn_w_gate_up"], inputs["L0_ffn_w_down"])
    R3 = run_phase3(R2, inputs["L1_attn_norm_w"],
                    inputs["L1_attn_wq"], inputs["L1_attn_bq"],
                    inputs["L1_attn_wk"], inputs["L1_attn_bk"],
                    inputs["L1_attn_wv"], inputs["L1_attn_bv"],
                    inputs["L1_attn_wo"], inputs["L1_attn_bo"])
    out = run_phase4(R3, inputs["L1_ffn_w1"], inputs["L1_ffn_b1"],
                     inputs["L1_ffn_w2"], inputs["L1_ffn_b2"])
    return out.astype(np.float32)


# version 11 (final)
# speedup vs baseline: 1.2027x; 1.0608x over previous
import os
import sys

sys.path.insert(0, "/opt/trn_rl_repo")

import numpy as np
import ml_dtypes
import concourse.bass as bass
import concourse.mybir as mybir
from concourse.bass_utils import run_bass_kernel_spmd
from concourse.alu_op_type import AluOpType as ALU
from contextlib import ExitStack

F32 = mybir.dt.float32
BF16 = mybir.dt.bfloat16
AF = mybir.ActivationFunctionType
AX = mybir.AxisListType

P = 128
T = 2048
D = 2048
NT = T // P
ND = D // P
CH = 512
NCH = T // CH

QK_SCALE = float(P) ** -0.5
L0_SCALE = float(P) ** -0.5
LN_EPS = 1e-5
RMS_EPS = 1e-6
L0_EPS = 1e-9
N_CORES = 8



def slab_weights(WT, osz=P):
    WT = np.ascontiguousarray(WT, dtype=np.float32)
    K, O = WT.shape
    kt = K // P
    A = WT.reshape(kt, P, O // osz, osz).transpose(2, 1, 0, 3)
    return np.ascontiguousarray(A.reshape(O // osz, P, kt * osz))



def emit_xT_build(block, src_dram, row0, nrow_tiles, xT, stage, sems,
                  cast=True):
    raise NotImplementedError



def build_phase1(nc, x, wqkv, wout, eye32, eye16, R1):
    NSLAB_B = 3 * ND

    def cb(s):
        if s < NSLAB_B:
            return 4 * (s + 1)
        return 4 * NSLAB_B + 2 * (s - NSLAB_B + 1)

    with ExitStack() as ctx:
        sb = lambda n_, s_, t_: ctx.enter_context(nc.sbuf_tensor(n_, s_, t_))
        sem = lambda n_: ctx.enter_context(nc.semaphore(n_))

        outT = sb("p1_outT", [P, ND * T], BF16)
        wbuf = sb("p1_wbuf", [P, 3 * D], BF16)
        eyeb = sb("p1_eyeb", [P, P], BF16)
        eyef = sb("p1_eyef", [P, P], F32)
        ones1 = sb("p1_ones1", [P, 1], BF16)
        epsb = sb("p1_epsb", [P, 1], F32)
        lnepsb = sb("p1_lnepsb", [P, 1], F32)

        dA = sem("p1_dA")
        wld = [sem("p1_wld0"), sem("p1_wld1"), sem("p1_wld2")]
        mmB = sem("p1_mmB")
        evB = sem("p1_evB")
        trA = sem("p1_trA")
        evA = sem("p1_evA")
        kst = sem("p1_kst")
        dsem = sem("p1_dsem")
        recS = sem("p1_recS")
        numS = sem("p1_numS")
        vout = sem("p1_vout")
        vch = sem("p1_vch")
        sch = sem("p1_sch")
        trC = sem("p1_trC")
        addC = sem("p1_addC")
        lnv = sem("p1_lnv")
        lnr = sem("p1_lnr")
        lnx = sem("p1_lnx")
        lnw = sem("p1_lnw")
        dC = sem("p1_dC")
        dR = [sem("p1_dR0"), sem("p1_dR1")]
        msc = sem("p1_msc")

        ones_bcast = bass.AP(ones1, 0, [[1, P], [0, T]])

        with ExitStack() as sctx:
            sbl = lambda n_, s_, t_: sctx.enter_context(nc.sbuf_tensor(n_, s_, t_))
            psl = lambda n_, s_, t_: sctx.enter_context(nc.psum_tensor(n_, s_, t_))
            xT = sbl("p1_xT", [P, ND * T], BF16)
            xtmb = sbl("p1_xtmb", [P, D], BF16)
            vbuf = sbl("p1_vbuf", [P, T], BF16)
            kp = sbl("p1_kp", [P, T], BF16)
            kv = sbl("p1_kv", [P, T], BF16)
            qp = sbl("p1_qp", [P, T], F32)
            abuf = sbl("p1_a", [P, T], F32)
            cbuf = sbl("p1_c", [P, T], F32)
            KVc = sbl("p1_KVc", [P, T], F32)
            Kc = sbl("p1_Kc", [P, T], F32)
            psA = [psl(f"p1_psA{i}", [P, CH], F32) for i in range(4)]
            ptr = [psl(f"p1_ptr{i}", [P, 8 * P], BF16) for i in range(2)]

            with nc.Block() as block:
                @block.gpsimd
                def _(g):
                    g.memset(ones1[:, :], 1.0).then_inc(msc, 1)
                    g.memset(epsb[:, :], L0_EPS).then_inc(msc, 1)
                    g.memset(lnepsb[:, :], LN_EPS).then_inc(msc, 1)
                    g.dma_start(eyeb[:, :], eye16[:, :]).then_inc(dA, 16)
                    g.dma_start(eyef[:, :], eye32[:, :]).then_inc(dA, 16)
                    for tt in range(NT):
                        if tt >= 1:
                            g.wait_ge(trA, 2 * tt)
                        g.dma_start(
                            xtmb[:, :], x[tt * P:(tt + 1) * P, :],
                        ).then_inc(dA, 16)
                    for s in range(NSLAB_B):
                        if s >= 3:
                            g.wait_ge(mmB, cb(s - 3))
                        g.dma_start(
                            wbuf[:, (s % 3) * D:(s % 3 + 1) * D],
                            wqkv[s, :, :],
                        ).then_inc(wld[s % 3], 16)

                @block.tensor
                def _(t):
                    for tt in range(NT):
                        t.wait_ge(dA, 32 + 16 * (tt + 1))
                        if tt >= 1:
                            t.wait_ge(evA, 2 * tt)
                        for dt in range(ND):
                            inst = t.transpose(
                                ptr[dt // 8][:, (dt % 8) * P:(dt % 8 + 1) * P],
                                xtmb[:, dt * P:(dt + 1) * P],
                                eyeb[:, :])
                            if dt % 8 == 7:
                                inst.then_inc(trA, 1)
                    t.wait_ge(evA, 2 * NT)
                    for s in range(NSLAB_B):
                        t.wait_ge(wld[s % 3], 16 * (s // 3 + 1))
                        for c in range(NCH):
                            m = 4 * s + c
                            if m >= 4:
                                t.wait_ge(evB, m - 3)
                            for kt in range(ND):
                                inst = t.matmul(
                                    psA[m % 4][:, :],
                                    wbuf[:, (s % 3) * D + kt * P:
                                         (s % 3) * D + (kt + 1) * P],
                                    xT[:, kt * T + c * CH:kt * T + (c + 1) * CH],
                                    start=(kt == 0), stop=(kt == ND - 1))
                            inst.then_inc(mmB, 1)

                @block.vector
                def _(v):
                    for tt in range(NT):
                        for half in range(2):
                            v.wait_ge(trA, 2 * tt + half + 1)
                            v.tensor_copy(
                                bass.AP(xT, half * 8 * T + tt * P,
                                        [[ND * T, P], [T, 8], [1, P]]),
                                ptr[half][:, :]).then_inc(evA, 1)
                    v.wait_ge(msc, 3)
                    for i in range(ND):
                        v.wait_ge(evB, 4 * (3 * i + 1) + 4)
                        v.scalar_tensor_tensor(
                            kp[:, :], cbuf[:, :], 1.0, abuf[:, :],
                            op0=ALU.min, op1=ALU.add).then_inc(kst, 1)
                        v.wait_ge(kst, i + 1)
                        v.tensor_tensor(kv[:, :], kp[:, :], vbuf[:, :],
                                        op=ALU.mult).then_inc(vch, 1)
                        v.wait_ge(vch, 5 * i + 1)
                        v.tensor_tensor_scan(KVc[:, :], ones_bcast, kv[:, :],
                                             0.0, op0=ALU.mult,
                                             op1=ALU.add).then_inc(vch, 1)
                        v.tensor_tensor_scan(Kc[:, :], ones_bcast, kp[:, :],
                                             0.0, op0=ALU.mult,
                                             op1=ALU.add).then_inc(vch, 1)
                        v.wait_ge(evB, 4 * (3 * i + 2) + 4)
                        v.scalar_tensor_tensor(
                            qp[:, :], cbuf[:, :], 1.0, abuf[:, :],
                            op0=ALU.min, op1=ALU.add).then_inc(vch, 1)
                        v.wait_ge(vch, 5 * i + 4)
                        v.tensor_tensor(abuf[:, :], qp[:, :], Kc[:, :],
                                        op=ALU.mult).then_inc(dsem, 1)
                        v.tensor_tensor(KVc[:, :], qp[:, :], KVc[:, :],
                                        op=ALU.mult).then_inc(vch, 1)
                        v.wait_ge(recS, i + 1)
                        v.wait_ge(vch, 5 * i + 5)
                        v.tensor_tensor(outT[:, i * T:(i + 1) * T],
                                        KVc[:, :], abuf[:, :],
                                        op=ALU.mult).then_inc(vout, 1)

                @block.scalar
                def _(s_):
                    for i in range(ND):
                        if i >= 1:
                            s_.wait_ge(vout, i)
                        for c in range(NCH):
                            m = 4 * (3 * i) + c
                            s_.wait_ge(mmB, m + 1)
                            s_.activation(vbuf[:, c * CH:(c + 1) * CH],
                                          psA[m % 4][:, :],
                                          AF.Copy).then_inc(evB, 1)
                        for c in range(NCH):
                            m = 4 * (3 * i + 1) + c
                            s_.wait_ge(mmB, m + 1)
                            s_.activation(abuf[:, c * CH:(c + 1) * CH],
                                          psA[m % 4][:, :], AF.Relu)
                            s_.activation(cbuf[:, c * CH:(c + 1) * CH],
                                          psA[m % 4][:, :],
                                          AF.Exp).then_inc(evB, 1)
                        s_.wait_ge(kst, i + 1)
                        for c in range(NCH):
                            m = 4 * (3 * i + 2) + c
                            s_.wait_ge(mmB, m + 1)
                            s_.activation(abuf[:, c * CH:(c + 1) * CH],
                                          psA[m % 4][:, :], AF.Relu)
                            s_.activation(cbuf[:, c * CH:(c + 1) * CH],
                                          psA[m % 4][:, :],
                                          AF.Exp).then_inc(evB, 1)
                        s_.wait_ge(dsem, i + 1)
                        if i == 0:
                            s_.wait_ge(msc, 3)
                        s_.activation(cbuf[:, :], abuf[:, :], AF.Ln,
                                      bias=epsb[:, :]).then_inc(sch, 1)
                        s_.wait_ge(sch, i + 1)
                        s_.activation(abuf[:, :], cbuf[:, :], AF.Exp,
                                      scale=-1.0).then_inc(recS, 1)


        with ExitStack() as sctx:
            sbl = lambda n_, s_, t_: sctx.enter_context(nc.sbuf_tensor(n_, s_, t_))
            psl = lambda n_, s_, t_: sctx.enter_context(nc.psum_tensor(n_, s_, t_))
            sumh = sbl("p1_sumh", [P, 8 * D], F32)
            aT = sbl("p1_aT", [P, 2 * 1024], F32)
            mvb = sbl("p1_mv", [P, 2], F32)
            bn6 = sbl("p1_bn6", [P, 24], F32)
            rstd = sbl("p1_rstd", [P, 1], F32)
            r1t = sbl("p1_r1t", [P, 2 * D], F32)
            psC = [psl(f"p1_psC{i}", [P, CH], F32) for i in range(4)]
            psT = [psl(f"p1_psT{i}", [P, CH], F32) for i in range(2)]

            with nc.Block() as block:
                @block.sync
                def _(sy):
                    sy.wait_ge(vout, ND)
                    for th in range(2):
                        for tq in range(8):
                            sy.dma_start(
                                sumh[:, tq * D:(tq + 1) * D],
                                x[(th * 8 + tq) * P:(th * 8 + tq + 1) * P, :],
                            ).then_inc(dC, 16)
                        for tq in range(8):
                            gidx = th * 8 + tq
                            sy.wait_ge(lnw, gidx + 1)
                            sy.dma_start(
                                R1[gidx * P:(gidx + 1) * P, :],
                                r1t[:, (tq % 2) * D:(tq % 2 + 1) * D],
                            ).then_inc(dR[gidx % 2], 16)
                    sy.wait_ge(dR[0], 16 * 8)
                    sy.wait_ge(dR[1], 16 * 8)

                @block.gpsimd
                def _(g):
                    for th in range(2):
                        for dt in range(ND):
                            s = NSLAB_B + th * ND + dt
                            g.wait_ge(mmB, cb(s - 3))
                            g.dma_start(
                                wbuf[:, (s % 3) * D:(s % 3 + 1) * D],
                                wout[dt, :, :]).then_inc(wld[s % 3], 16)

                @block.tensor
                def _(t):
                    t.wait_ge(vout, ND)
                    t.wait_ge(evB, 4 * NSLAB_B)
                    for th in range(2):
                        for dt in range(ND):
                            s = NSLAB_B + th * ND + dt
                            idx = th * ND + dt
                            t.wait_ge(wld[s % 3], 16 * (s // 3 + 1))
                            for c in range(2):
                                m = 4 * NSLAB_B + 2 * idx + c
                                t.wait_ge(evB, m - 3)
                                for kt in range(ND):
                                    inst = t.matmul(
                                        psC[m % 4][:, :],
                                        wbuf[:, (s % 3) * D + kt * P:
                                             (s % 3) * D + (kt + 1) * P],
                                        outT[:, kt * T + th * 1024 + c * CH:
                                             kt * T + th * 1024 + (c + 1) * CH],
                                        start=(kt == 0), stop=(kt == ND - 1))
                                inst.then_inc(mmB, 1)
                            t.wait_ge(evB, 4 * NSLAB_B + 2 * idx + 2)
                            for pk in range(2):
                                if idx >= 1:
                                    t.wait_ge(addC, 2 * (idx - 1) + pk + 1)
                                for tq4 in range(4):
                                    tq = pk * 4 + tq4
                                    inst = t.transpose(
                                        psT[pk][:, tq4 * P:(tq4 + 1) * P],
                                        aT[:, (dt % 2) * 1024 + tq * P:
                                           (dt % 2) * 1024 + (tq + 1) * P],
                                        eyef[:, :])
                                inst.then_inc(trC, 1)

                @block.scalar
                def _(s_):
                    for th in range(2):
                        for dt in range(ND):
                            idx = th * ND + dt
                            if idx >= 2:
                                s_.wait_ge(trC, 2 * (idx - 1))
                            for c in range(2):
                                m = 4 * NSLAB_B + 2 * idx + c
                                s_.wait_ge(mmB, m + 1)
                                s_.activation(
                                    aT[:, (dt % 2) * 1024 + c * CH:
                                       (dt % 2) * 1024 + (c + 1) * CH],
                                    psC[m % 4][:, :], AF.Copy,
                                    scale=L0_SCALE).then_inc(evB, 1)
                        for tq in range(8):
                            gidx = th * 8 + tq
                            s_.wait_ge(lnv, 2 * (gidx + 1))
                            s_.activation(rstd[:, :], mvb[:, 1:2], AF.Ln,
                                          bias=lnepsb[:, :]).then_inc(lnx, 1)
                            s_.wait_ge(lnx, gidx + 1)
                            s_.activation(rstd[:, :], rstd[:, :], AF.Exp,
                                          scale=-0.5).then_inc(lnr, 1)

                @block.vector
                def _(v):
                    for th in range(2):
                        v.wait_ge(dC, 16 * 8 * (th + 1))
                        for dt in range(ND):
                            idx = th * ND + dt
                            for pk in range(2):
                                v.wait_ge(trC, 2 * idx + pk + 1)
                                dst = bass.AP(sumh, pk * 4 * D + dt * P,
                                              [[8 * D, P], [D, 4], [1, P]])
                                v.tensor_tensor(dst, psT[pk][:, :], dst,
                                                op=ALU.add).then_inc(addC, 1)
                        v.wait_ge(addC, 32 * (th + 1))
                        for tq in range(8):
                            gidx = th * 8 + tq
                            tile = sumh[:, tq * D:(tq + 1) * D]
                            for c4 in range(4):
                                inst = v.bn_stats(
                                    bn6[:, c4 * 6:(c4 + 1) * 6],
                                    tile[:, c4 * CH:(c4 + 1) * CH])
                            inst.then_inc(lnv, 1)
                            v.wait_ge(lnv, 2 * gidx + 1)
                            v.bn_aggr(mvb[:, :], bn6[:, :]).then_inc(lnv, 1)
                            v.wait_ge(lnr, gidx + 1)
                            if gidx >= 2:
                                v.wait_ge(dR[gidx % 2], 16 * (gidx // 2))
                            v.tensor_scalar(
                                r1t[:, (tq % 2) * D:(tq % 2 + 1) * D],
                                tile, mvb[:, 0:1], rstd[:, :],
                                op0=ALU.subtract,
                                op1=ALU.mult).then_inc(lnw, 1)
    return nc



def _clear_sems(nc):
    with nc.Block() as block:
        @block.sync
        def _(sy):
            sy.sem_clear(range(155, 250))



def build_phase2(nc, R1, wgu, wd, eye32, eye16, R2):
    HT = 1024
    NIT = 32

    _clear_sems(nc)

    def base(th, ip):
        return th * 320 + ip * 160

    gu_done = {}
    dn_done = {}
    for th in range(2):
        for ip in range(2):
            for ls in range(64):
                gu_done[th * 128 + ip * 64 + ls] = base(th, ip) + 2 * (ls + 1)
            for dt in range(ND):
                dn_done[th * 32 + ip * 16 + dt] = \
                    base(th, ip) + 128 + 2 * (dt + 1)

    with ExitStack() as ctx:
        sb = lambda n_, s_, t_: ctx.enter_context(nc.sbuf_tensor(n_, s_, t_))
        psl = lambda n_, s_, t_: ctx.enter_context(nc.psum_tensor(n_, s_, t_))
        sem = lambda n_: ctx.enter_context(nc.semaphore(n_))

        h = sb("p2_h", [P, NIT * HT], BF16)
        xT = sb("p2_xT", [P, ND * HT], BF16)
        sumh = sb("p2_sumh", [P, 8 * D], F32)
        wbufG = sb("p2_wbufG", [P, 2 * D], BF16)
        wbufD = sb("p2_wbufD", [P, 2 * 4096], BF16)
        gbuf = sb("p2_gbuf", [P, 2 * HT], BF16)
        ubuf = sb("p2_ubuf", [P, 2 * HT], BF16)
        aT = sb("p2_aT", [P, 2 * HT], F32)
        xtmb = sb("p2_xtmb", [P, D], BF16)
        eyeb = sb("p2_eyeb", [P, P], BF16)
        eyef = sb("p2_eyef", [P, P], F32)
        mvb = sb("p2_mv", [P, 2], F32)
        bn6 = sb("p2_bn6", [P, 24], F32)
        rstd = sb("p2_rstd", [P, 1], F32)
        lnepsb = sb("p2_lneps", [P, 1], F32)
        psA = [psl(f"p2_psA{i}", [P, CH], F32) for i in range(4)]
        ptrA = [psl(f"p2_ptrA{i}", [P, 8 * P], BF16) for i in range(2)]
        psT = [psl(f"p2_psT{i}", [P, CH], F32) for i in range(2)]

        dA = sem("p2_dA")
        trA = sem("p2_trA")
        evA = sem("p2_evA")
        wldG = [sem("p2_wldG0"), sem("p2_wldG1")]
        wldD = [sem("p2_wldD0"), sem("p2_wldD1")]
        mm = sem("p2_mm")
        ev = sem("p2_ev")
        hm = sem("p2_hm")
        dC = sem("p2_dC")
        trC = sem("p2_trC")
        addC = sem("p2_addC")
        lnv = sem("p2_lnv")
        lnr = sem("p2_lnr")
        lnx = sem("p2_lnx")
        lnw = sem("p2_lnw")
        dR = sem("p2_dR")
        msc = sem("p2_msc")

        with nc.Block() as block:
            @block.gpsimd
            def _(g):
                g.memset(lnepsb[:, :], LN_EPS).then_inc(msc, 1)
                g.dma_start(eyeb[:, :], eye16[:, :]).then_inc(dA, 16)
                g.dma_start(eyef[:, :], eye32[:, :]).then_inc(dA, 16)
                for th in range(2):
                    for tl in range(8):
                        tt = th * 8 + tl
                        if tt >= 1:
                            g.wait_ge(trA, 2 * tt)
                        g.dma_start(
                            xtmb[:, :], R1[tt * P:(tt + 1) * P, :],
                        ).then_inc(dA, 16)
                    for ip in range(2):
                        for ls in range(64):
                            gs = th * 128 + ip * 64 + ls
                            if gs >= 2:
                                g.wait_ge(mm, gu_done[gs - 2])
                            g.dma_start(
                                wbufG[:, (gs % 2) * D:(gs % 2 + 1) * D],
                                wgu[ip * 64 + ls, :, :]).then_inc(wldG[gs % 2], 16)
                        for dt in range(ND):
                            ds = th * 32 + ip * 16 + dt
                            if ds >= 2:
                                g.wait_ge(mm, dn_done[ds - 2])
                            g.dma_start(
                                wbufD[:, (ds % 2) * 4096:(ds % 2 + 1) * 4096],
                                wd[ip, dt, :, :]).then_inc(wldD[ds % 2], 16)

            @block.tensor
            def _(t):
                for th in range(2):
                    for tl in range(8):
                        tt = th * 8 + tl
                        t.wait_ge(dA, 32 + 16 * (tt + 1))
                        if tt >= 1:
                            t.wait_ge(evA, 2 * tt)
                        for dt in range(ND):
                            inst = t.transpose(
                                ptrA[dt // 8][:, (dt % 8) * P:(dt % 8 + 1) * P],
                                xtmb[:, dt * P:(dt + 1) * P], eyeb[:, :])
                            if dt % 8 == 7:
                                inst.then_inc(trA, 1)
                    t.wait_ge(evA, 2 * (th * 8 + 8))
                    for ip in range(2):
                        for ls in range(64):
                            gs = th * 128 + ip * 64 + ls
                            t.wait_ge(wldG[gs % 2], 16 * (gs // 2 + 1))
                            for c in range(2):
                                m = base(th, ip) + 2 * ls + c
                                if m >= 4:
                                    t.wait_ge(ev, m - 3)
                                for kt in range(ND):
                                    inst = t.matmul(
                                        psA[m % 4][:, :],
                                        wbufG[:, (gs % 2) * D + kt * P:
                                              (gs % 2) * D + (kt + 1) * P],
                                        xT[:, kt * HT + c * CH:
                                           kt * HT + (c + 1) * CH],
                                        start=(kt == 0), stop=(kt == ND - 1))
                                inst.then_inc(mm, 1)
                        t.wait_ge(hm, 32 * (th * 2 + ip + 1))
                        for dt in range(ND):
                            ds = th * 32 + ip * 16 + dt
                            idxg = th * 32 + ip * 16 + dt
                            t.wait_ge(wldD[ds % 2], 16 * (ds // 2 + 1))
                            for c in range(2):
                                m = base(th, ip) + 128 + 2 * dt + c
                                t.wait_ge(ev, m - 3)
                                for kt in range(NIT):
                                    inst = t.matmul(
                                        psA[m % 4][:, :],
                                        wbufD[:, (ds % 2) * 4096 + kt * P:
                                              (ds % 2) * 4096 + (kt + 1) * P],
                                        h[:, kt * HT + c * CH:
                                          kt * HT + (c + 1) * CH],
                                        start=(kt == 0), stop=(kt == NIT - 1))
                                inst.then_inc(mm, 1)
                            t.wait_ge(ev, base(th, ip) + 128 + 2 * dt + 2)
                            for pk in range(2):
                                if idxg >= 1:
                                    t.wait_ge(addC, 2 * (idxg - 1) + pk + 1)
                                for tq4 in range(4):
                                    tq = pk * 4 + tq4
                                    inst = t.transpose(
                                        psT[pk][:, tq4 * P:(tq4 + 1) * P],
                                        aT[:, (dt % 2) * HT + tq * P:
                                           (dt % 2) * HT + (tq + 1) * P],
                                        eyef[:, :])
                                inst.then_inc(trC, 1)

            @block.scalar
            def _(s_):
                for th in range(2):
                    for ip in range(2):
                        for ls in range(64):
                            il = ls // 2
                            is_g = (ls % 2 == 0)
                            gidx = (th * 2 + ip) * 32 + il
                            if gidx >= 2 and is_g:
                                s_.wait_ge(hm, gidx - 1)
                            for c in range(2):
                                m = base(th, ip) + 2 * ls + c
                                s_.wait_ge(mm, m + 1)
                                dst = (gbuf if is_g else ubuf)
                                s_.activation(
                                    dst[:, (il % 2) * HT + c * CH:
                                        (il % 2) * HT + (c + 1) * CH],
                                    psA[m % 4][:, :],
                                    AF.Silu if is_g else AF.Copy,
                                ).then_inc(ev, 1)
                        for dt in range(ND):
                            idxg = th * 32 + ip * 16 + dt
                            if idxg >= 2:
                                s_.wait_ge(trC, 2 * (idxg - 1))
                            for c in range(2):
                                m = base(th, ip) + 128 + 2 * dt + c
                                s_.wait_ge(mm, m + 1)
                                s_.activation(
                                    aT[:, (dt % 2) * HT + c * CH:
                                       (dt % 2) * HT + (c + 1) * CH],
                                    psA[m % 4][:, :], AF.Copy).then_inc(ev, 1)
                    for tl in range(8):
                        gt = th * 8 + tl
                        s_.wait_ge(lnv, 2 * (gt + 1))
                        if gt == 0:
                            s_.wait_ge(msc, 1)
                        s_.activation(rstd[:, :], mvb[:, 1:2], AF.Ln,
                                      bias=lnepsb[:, :]).then_inc(lnx, 1)
                        s_.wait_ge(lnx, gt + 1)
                        s_.activation(rstd[:, :], rstd[:, :], AF.Exp,
                                      scale=-0.5).then_inc(lnr, 1)

            @block.vector
            def _(v):
                for th in range(2):
                    for tl in range(8):
                        tt = th * 8 + tl
                        for half in range(2):
                            v.wait_ge(trA, 2 * tt + half + 1)
                            if th == 1 and tl == 0 and half == 0:
                                v.wait_ge(mm, base(0, 1) + 128)
                            v.tensor_copy(
                                bass.AP(xT, half * 8 * HT + tl * P,
                                        [[ND * HT, P], [HT, 8], [1, P]]),
                                ptrA[half][:, :]).then_inc(evA, 1)
                    for ip in range(2):
                        for il in range(NIT):
                            gidx = (th * 2 + ip) * 32 + il
                            if il == 0 and gidx >= 32:
                                v.wait_ge(mm, base(th, ip - 1) + 160
                                          if ip == 1 else base(th - 1, 1) + 160)
                            v.wait_ge(ev, base(th, ip) + 4 * il + 4)
                            v.tensor_tensor(
                                h[:, il * HT:(il + 1) * HT],
                                gbuf[:, (il % 2) * HT:(il % 2 + 1) * HT],
                                ubuf[:, (il % 2) * HT:(il % 2 + 1) * HT],
                                op=ALU.mult).then_inc(hm, 1)
                        for dt in range(ND):
                            idxg = th * 32 + ip * 16 + dt
                            if ip == 0 and dt == 0:
                                v.wait_ge(dC, 16 * 8 * (th + 1))
                            for pk in range(2):
                                v.wait_ge(trC, 2 * idxg + pk + 1)
                                dst = bass.AP(sumh, pk * 4 * D + dt * P,
                                              [[8 * D, P], [D, 4], [1, P]])
                                v.tensor_tensor(dst, psT[pk][:, :], dst,
                                                op=ALU.add).then_inc(addC, 1)
                    v.wait_ge(addC, 64 * (th + 1))
                    for tl in range(8):
                        gt = th * 8 + tl
                        tile = sumh[:, tl * D:(tl + 1) * D]
                        for c4 in range(4):
                            inst = v.bn_stats(bn6[:, c4 * 6:(c4 + 1) * 6],
                                              tile[:, c4 * CH:(c4 + 1) * CH])
                        inst.then_inc(lnv, 1)
                        v.wait_ge(lnv, 2 * gt + 1)
                        v.bn_aggr(mvb[:, :], bn6[:, :]).then_inc(lnv, 1)
                        v.wait_ge(lnr, gt + 1)
                        v.tensor_scalar(tile, tile, mvb[:, 0:1], rstd[:, :],
                                        op0=ALU.subtract,
                                        op1=ALU.mult).then_inc(lnw, 1)

            @block.sync
            def _(sy):
                for th in range(2):
                    if th == 1:
                        sy.wait_ge(dR, 16 * 8)
                    for tl in range(8):
                        sy.dma_start(
                            sumh[:, tl * D:(tl + 1) * D],
                            R1[(th * 8 + tl) * P:(th * 8 + tl + 1) * P, :],
                        ).then_inc(dC, 16)
                    for tl in range(8):
                        gt = th * 8 + tl
                        sy.wait_ge(lnw, gt + 1)
                        sy.dma_start(
                            R2[gt * P:(gt + 1) * P, :],
                            sumh[:, tl * D:(tl + 1) * D]).then_inc(dR, 16)
                sy.wait_ge(dR, 16 * NT)
    return nc


def host_prep_p2(w_gate_up, w_down):
    WT = np.asarray(w_gate_up, np.float32).T
    cols = []
    for ip in range(2):
        for il in range(32):
            i = ip * 32 + il
            cols.append(WT[:, i * P:(i + 1) * P])
            cols.append(WT[:, 8192 + i * P:8192 + (i + 1) * P])
    wgu_s = slab_weights(np.concatenate(cols, axis=1), osz=P)
    WdT = np.asarray(w_down, np.float32).T
    wd_s = np.zeros((2, 16, P, 4096), np.float32)
    for ip in range(2):
        sl = slab_weights(WdT[ip * 4096:(ip + 1) * 4096, :], osz=P)
        wd_s[ip] = sl
    return wgu_s, np.ascontiguousarray(wd_s)


def _mk_phase2_nc():
    nc = bass.Bass("TRN2", target_bir_lowering=False)
    R1 = nc.dram_tensor("R1", [T, D], F32, kind="ExternalInput")
    wgu = nc.dram_tensor("wgu", [128, P, D], F32, kind="ExternalInput")
    wd = nc.dram_tensor("wd", [2, 16, P, 4096], F32, kind="ExternalInput")
    eye32 = nc.dram_tensor("eye32", [P, P], F32, kind="ExternalInput")
    eye16 = nc.dram_tensor("eye16", [P, P], BF16, kind="ExternalInput")
    R2 = nc.dram_tensor("R2", [T, D], F32, kind="ExternalOutput")
    build_phase2(nc, R1, wgu, wd, eye32, eye16, R2)
    return nc


def run_phase2(R1_full, w_gate_up, w_down, cores=N_CORES):
    wgu_s, wd_s = host_prep_p2(w_gate_up, w_down)
    nc = _mk_phase2_nc()
    in_maps = []
    for c in range(cores):
        in_maps.append(dict(R1=np.ascontiguousarray(R1_full[c], np.float32),
                            wgu=wgu_s, wd=wd_s,
                            eye32=_EYE32, eye16=_EYE16))
    res = run_bass_kernel_spmd(nc, in_maps, core_ids=list(range(cores)))
    return np.stack([r["R2"] for r in res.results])




def build_phase3(nc, R2, wq, wk, wv, wo, nw, bq, bk, bv, bo, masks,
                 eye32, eye16, R3, upto=3, att="full", d_lvl="full"):
    _clear_sems(nc)
    qs = nc.dram_tensor("p3_qs", [16, P, T], BF16)
    vs = nc.dram_tensor("p3_vs", [16, P, T], BF16)
    os_ = nc.dram_tensor("p3_os", [16, P, D], BF16)
    NB = 48

    with ExitStack() as ctx:
        sb = lambda n_, s_, t_: ctx.enter_context(nc.sbuf_tensor(n_, s_, t_))
        psl = lambda n_, s_, t_: ctx.enter_context(nc.psum_tensor(n_, s_, t_))
        sem = lambda n_: ctx.enter_context(nc.semaphore(n_))

        big = sb("p3_big", [P, ND * T], BF16)
        wbuf = sb("p3_wbuf", [P, 2 * D], BF16)
        qtile = sb("p3_qtile", [P, 2 * T], BF16)
        eyeb = sb("p3_eyeb", [P, P], BF16)
        eyef = sb("p3_eyef", [P, P], F32)
        lnepsb = sb("p3_lneps", [P, 1], F32)
        mvb = sb("p3_mv", [P, 2], F32)
        bn6 = sb("p3_bn6", [P, 24], F32)
        lrstd = sb("p3_lrstd", [P, 1], F32)
        bqb = sb("p3_bqb", [P, 16], F32)
        bkb = sb("p3_bkb", [P, 16], F32)
        bvb = sb("p3_bvb", [P, 16], F32)
        bob = sb("p3_bob", [P, 16], F32)

        psA = [psl(f"p3_psA{i}", [P, CH], F32) for i in range(2)]
        ptrB = [psl(f"p3_ptrB{i}", [P, 8 * P], BF16) for i in range(2)]
        psS = psl("p3_psS", [P, T], F32)

        dA = sem("p3_dA")
        dX = [sem("p3_dX0"), sem("p3_dX1")]
        rm1 = sem("p3_rm1")
        rm1b = sem("p3_rm1b")
        rm2 = sem("p3_rm2")
        xnv = sem("p3_xnv")
        trA = sem("p3_trA")
        evA = sem("p3_evA")
        wld = [sem("p3_wld0"), sem("p3_wld1")]
        mm = sem("p3_mm")
        ev = sem("p3_ev")
        dQ = [sem("p3_dQ0"), sem("p3_dQ1")]
        dVs = [sem("p3_dVs0"), sem("p3_dVs1")]
        dVr = [sem("p3_dVr0"), sem("p3_dVr1")]
        trV = sem("p3_trV")
        evV = sem("p3_evV")
        qld = [sem("p3_qld0"), sem("p3_qld1")]
        mmS = sem("p3_mmS")
        aux1 = sem("p3_aux1")
        sS = sem("p3_sS")
        sE = sem("p3_sE")
        trP = sem("p3_trP")
        evP = sem("p3_evP")
        rc = sem("p3_rc")
        mmO = sem("p3_mmO")
        evO = sem("p3_evO")
        dOs = [sem("p3_dOs0"), sem("p3_dOs1")]
        dOl = [sem("p3_dOl0"), sem("p3_dOl1")]
        trD = sem("p3_trD")
        evD = sem("p3_evD")
        wld2 = [sem("p3_wld20"), sem("p3_wld21")]
        mmD = sem("p3_mmD")
        evE = sem("p3_evE")
        trC = sem("p3_trC")
        addC = sem("p3_addC")
        lnv = sem("p3_lnv")
        lnr = sem("p3_lnr")
        lnx = sem("p3_lnx")
        lnw = sem("p3_lnw")
        dC = sem("p3_dC")
        dR = sem("p3_dR")
        msc = sem("p3_msc")

        with ExitStack() as bctx:
            sbb = lambda n_, s_, t_: bctx.enter_context(nc.sbuf_tensor(n_, s_, t_))
            Kreg = sbb("p3_K", [P, ND * T], BF16)
            vtile = sbb("p3_vtile", [P, 2 * T], BF16)

            with ExitStack() as actx:
                sba = lambda n_, s_, t_: actx.enter_context(
                    nc.sbuf_tensor(n_, s_, t_))
                xtm2 = sba("p3_xtm2", [P, 2 * D], F32)
                junk = sba("p3_junk", [P, D], F32)
                xtmb = sba("p3_xtmb", [P, D], BF16)
                nwb = sba("p3_nwb", [P, D], F32)
                rmsepsb = sba("p3_rmseps", [P, 1], F32)
                ssq = sba("p3_ssq", [P, 1], F32)
                rtmp = sba("p3_rtmp", [P, 1], F32)
                rstd = sba("p3_rstd", [P, 1], F32)

                with nc.Block() as block:
                    @block.sync
                    def _(sy):
                        sy.dma_start(eyef[:, :], eye32[:, :]).then_inc(dA, 16)
                        sy.dma_start(eyeb[:, :], eye16[:, :]).then_inc(dA, 16)
                        sy.dma_start(
                            nwb[:, :],
                            bass.AP(nw, 0, [[0, P], [1, D]])).then_inc(dA, 16)
                        sy.dma_start(bqb[:, :], bq[:, :]).then_inc(dA, 16)
                        sy.dma_start(bkb[:, :], bk[:, :]).then_inc(dA, 16)
                        sy.dma_start(bvb[:, :], bv[:, :]).then_inc(dA, 16)
                        sy.dma_start(bob[:, :], bo[:, :]).then_inc(dA, 16)
                        for tt in range(NT):
                            if tt >= 2:
                                sy.wait_ge(xnv, tt - 1)
                            sy.dma_start(
                                xtm2[:, (tt % 2) * D:(tt % 2 + 1) * D],
                                R2[tt * P:(tt + 1) * P, :],
                            ).then_inc(dX[tt % 2], 16)
                        for dt in range(ND):
                            sy.wait_ge(ev, 4 * dt + 4)
                            sy.dma_start(
                                qs[dt, :, :],
                                qtile[:, (dt % 2) * T:(dt % 2 + 1) * T],
                            ).then_inc(dQ[dt % 2], 16)
                        for dt in range(ND):
                            sy.wait_ge(ev, 4 * (32 + dt) + 4)
                            sy.dma_start(
                                vs[dt, :, :],
                                vtile[:, (dt % 2) * T:(dt % 2 + 1) * T],
                            ).then_inc(dVs[dt % 2], 16)
                        sy.wait_ge(dQ[0], 16 * 8)
                        sy.wait_ge(dQ[1], 16 * 8)
                        sy.wait_ge(dVs[0], 16 * 8)
                        sy.wait_ge(dVs[1], 16 * 8)

                    @block.gpsimd
                    def _(g):
                        g.memset(rmsepsb[:, :], RMS_EPS).then_inc(msc, 1)
                        g.memset(lnepsb[:, :], LN_EPS).then_inc(msc, 1)
                        for grp, wsrc in ((0, wq), (1, wk), (2, wv)):
                            for dt in range(ND):
                                s = grp * ND + dt
                                if s >= 2:
                                    g.wait_ge(mm, 4 * (s - 1))
                                g.dma_start(
                                    wbuf[:, (s % 2) * D:(s % 2 + 1) * D],
                                    wsrc[dt, :, :]).then_inc(wld[s % 2], 16)

                    @block.tensor
                    def _(t):
                        t.wait_ge(dA, 16 * 7)
                        for tt in range(NT):
                            t.wait_ge(xnv, tt + 1)
                            if tt >= 1:
                                t.wait_ge(evA, 2 * tt)
                            for dt in range(ND):
                                inst = t.transpose(
                                    ptrB[dt // 8][:, (dt % 8) * P:
                                                  (dt % 8 + 1) * P],
                                    xtmb[:, dt * P:(dt + 1) * P], eyeb[:, :])
                                if dt % 8 == 7:
                                    inst.then_inc(trA, 1)
                        t.wait_ge(evA, 2 * NT)
                        for s in range(NB):
                            t.wait_ge(wld[s % 2], 16 * (s // 2 + 1))
                            for c in range(NCH):
                                m = 4 * s + c
                                if m >= 2:
                                    t.wait_ge(ev, m - 1)
                                for kt in range(ND):
                                    inst = t.matmul(
                                        psA[m % 2][:, :],
                                        wbuf[:, (s % 2) * D + kt * P:
                                             (s % 2) * D + (kt + 1) * P],
                                        big[:, kt * T + c * CH:
                                            kt * T + (c + 1) * CH],
                                        start=(kt == 0), stop=(kt == ND - 1))
                                inst.then_inc(mm, 1)

                    @block.scalar
                    def _(s_):
                        s_.wait_ge(msc, 2)
                        for tt in range(NT):
                            s_.wait_ge(dX[tt % 2], 16 * (tt // 2 + 1))
                            if tt >= 1:
                                s_.wait_ge(xnv, tt)
                            s_.activation(
                                junk[:, :],
                                xtm2[:, (tt % 2) * D:(tt % 2 + 1) * D],
                                AF.Square,
                                accum_out=ssq[:, :]).then_inc(rm1, 1)
                            s_.wait_ge(rm1, tt + 1)
                            s_.activation(rtmp[:, :], ssq[:, :], AF.Ln,
                                          bias=rmsepsb[:, :],
                                          scale=1.0 / D).then_inc(rm1b, 1)
                            s_.wait_ge(rm1b, tt + 1)
                            s_.activation(rstd[:, :], rtmp[:, :], AF.Exp,
                                          scale=-0.5).then_inc(rm2, 1)
                        for grp in range(3):
                            for dt in range(ND):
                                s = grp * ND + dt
                                if grp == 0 and dt >= 2:
                                    s_.wait_ge(dQ[dt % 2], 16 * (dt // 2))
                                if grp == 2 and dt >= 2:
                                    s_.wait_ge(dVs[dt % 2], 16 * (dt // 2))
                                bias = (bqb, bkb, bvb)[grp]
                                for c in range(NCH):
                                    m = 4 * s + c
                                    s_.wait_ge(mm, m + 1)
                                    if grp == 0:
                                        dst = qtile[:, (dt % 2) * T + c * CH:
                                                    (dt % 2) * T + (c + 1) * CH]
                                    elif grp == 1:
                                        dst = Kreg[:, dt * T + c * CH:
                                                   dt * T + (c + 1) * CH]
                                    else:
                                        dst = vtile[:, (dt % 2) * T + c * CH:
                                                    (dt % 2) * T + (c + 1) * CH]
                                    s_.activation(
                                        dst, psA[m % 2][:, :], AF.Identity,
                                        bias=bias[:, dt:dt + 1]).then_inc(ev, 1)

                    @block.vector
                    def _(v):
                        v.wait_ge(dA, 16 * 7)
                        for tt in range(NT):
                            v.wait_ge(rm2, tt + 1)
                            if tt >= 1:
                                v.wait_ge(trA, 2 * tt)
                            v.scalar_tensor_tensor(
                                xtmb[:, :],
                                xtm2[:, (tt % 2) * D:(tt % 2 + 1) * D],
                                rstd[:, :], nwb[:, :],
                                op0=ALU.mult, op1=ALU.mult).then_inc(xnv, 1)
                            for half in range(2):
                                v.wait_ge(trA, 2 * tt + half + 1)
                                v.tensor_copy(
                                    bass.AP(big, half * 8 * T + tt * P,
                                            [[ND * T, P], [T, 8], [1, P]]),
                                    ptrB[half][:, :]).then_inc(evA, 1)

            if upto < 2:
                return nc
            Psb = sbb("p3_Psb", [P, T], BF16)
            PTsb = sbb("p3_PTsb", [P, T], BF16)
            qblk = sbb("p3_qblk", [P, 2 * T], BF16)
            Otile = sbb("p3_Otile", [P, 2 * D], BF16)
            maskb = sbb("p3_maskb", [P, 4 * CH], F32)
            rm = sbb("p3_rm", [P, 1], F32)
            nm = sbb("p3_nm", [P, 1], F32)
            lsum = sbb("p3_lsum", [P, 1], F32)
            linv = sbb("p3_linv", [P, 1], F32)
            psO = psA[0][:, 0:P]

            with nc.Block() as block:
                @block.sync
                def _(sy):
                    for v_ in range(4):
                        sy.dma_start(maskb[:, v_ * CH:(v_ + 1) * CH],
                                     masks[v_, :, :]).then_inc(dA, 16)
                    for dt in range(ND):
                        if dt >= 2:
                            sy.wait_ge(trV, 2 * (dt - 1))
                        sy.dma_start(
                            vtile[:, (dt % 2) * T:(dt % 2 + 1) * T],
                            vs[dt, :, :]).then_inc(dVr[dt % 2], 16)
                    if att != "none":
                        for qb in range(NT):
                            if qb >= 2:
                                sy.wait_ge(mmS, 16 * (qb - 1))
                            sy.dma_start(
                                qblk[:, (qb % 2) * T:(qb % 2 + 1) * T],
                                bass.AP(qs, qb * P,
                                        [[T, P], [P * T, 16], [1, P]]),
                            ).then_inc(qld[qb % 2], 16)
                    if att == "full":
                        sy.wait_ge(dOs[0], 16 * 8)
                        sy.wait_ge(dOs[1], 16 * 8)

                @block.tensor
                def _(t):
                    t.wait_ge(dA, 16 * 11)
                    for dt in range(ND):
                        t.wait_ge(dVr[dt % 2], 16 * (dt // 2 + 1))
                        if dt >= 1:
                            t.wait_ge(evV, 2 * dt)
                        for tt in range(NT):
                            inst = t.transpose(
                                ptrB[tt // 8][:, (tt % 8) * P:(tt % 8 + 1) * P],
                                vtile[:, (dt % 2) * T + tt * P:
                                      (dt % 2) * T + (tt + 1) * P],
                                eyeb[:, :])
                            if tt % 8 == 7:
                                inst.then_inc(trV, 1)
                    for qb in (range(NT) if att != "none" else []):
                        t.wait_ge(qld[qb % 2], 16 * (qb // 2 + 1))
                        if qb == 0:
                            t.wait_ge(evV, 2 * NT)
                        nchk = qb // 4 + 1
                        row = (qb + 1) * P
                        for h in range(ND):
                            gidx = qb * ND + h
                            t.wait_ge(sE, gidx)
                            for c in range(nchk):
                                w = min(CH, row - c * CH)
                                inst = t.matmul(
                                    psS[:, c * CH:c * CH + w],
                                    qblk[:, (qb % 2) * T + h * P:
                                         (qb % 2) * T + (h + 1) * P],
                                    Kreg[:, h * T + c * CH:h * T + c * CH + w],
                                    start=True, stop=True)
                            inst.then_inc(mmS, 1)
                            if att == "s":
                                continue
                            t.wait_ge(sE, gidx + 1)
                            t.wait_ge(evP, gidx)
                            for j in range(qb + 1):
                                inst = t.transpose(
                                    ptrB[j // 8][:, (j % 8) * P:(j % 8 + 1) * P],
                                    Psb[:, j * P:(j + 1) * P], eyeb[:, :])
                            inst.then_inc(trP, 1)
                            if att == "pt":
                                continue
                            t.wait_ge(evP, gidx + 1)
                            if att == "full":
                                t.wait_ge(evO, gidx)
                            for j in range(qb + 1):
                                inst = t.matmul(
                                    psO,
                                    PTsb[:, j * P:(j + 1) * P],
                                    big[:, j * T + h * P:j * T + (h + 1) * P],
                                    start=(j == 0), stop=(j == qb))
                            inst.then_inc(mmO, 1)

                @block.scalar
                def _(s_):
                    for qb in (range(NT) if att != "none" else []):
                        row = (qb + 1) * P
                        for h in range(ND):
                            gidx = qb * ND + h
                            s_.wait_ge(sS, gidx + 1)
                            if att == "full":
                                s_.wait_ge(rc, gidx)
                            s_.activation(Psb[:, 0:row], psS[:, 0:row],
                                          AF.Exp, bias=nm[:, :],
                                          scale=QK_SCALE,
                                          accum_out=lsum[:, :]).then_inc(sE, 1)
                            if att == "s":
                                continue
                            s_.wait_ge(trP, gidx + 1)
                            nbk = 1 if qb < 8 else 2
                            for bk_ in range(nbk):
                                wpt = min(8 * P, row - bk_ * 8 * P)
                                inst = s_.activation(
                                    PTsb[:, bk_ * 8 * P:bk_ * 8 * P + wpt],
                                    ptrB[bk_][:, 0:wpt], AF.Copy)
                            inst.then_inc(evP, 1)
                            if att in ("pt", "pv"):
                                continue
                            s_.wait_ge(mmO, gidx + 1)
                            s_.wait_ge(rc, gidx + 1)
                            if h == 0 and qb >= 2:
                                s_.wait_ge(dOs[qb % 2], 16 * (qb // 2))
                            s_.activation(
                                Otile[:, (qb % 2) * D + h * P:
                                      (qb % 2) * D + (h + 1) * P],
                                psO, AF.Identity,
                                scale=linv[:, :]).then_inc(evO, 1)
                            if h == ND - 1:
                                s_.wait_ge(evO, 16 * (qb + 1))
                                s_.dma_start(
                                    os_[qb, :, :],
                                    Otile[:, (qb % 2) * D:(qb % 2 + 1) * D],
                                ).then_inc(dOs[qb % 2], 16)

                @block.vector
                def _(v):
                    for dt in range(ND):
                        for half in range(2):
                            v.wait_ge(trV, 2 * dt + half + 1)
                            v.tensor_copy(
                                bass.AP(big, half * 8 * T + dt * P,
                                        [[ND * T, P], [T, 8], [1, P]]),
                                ptrB[half][:, :]).then_inc(evV, 1)
                    for qb in (range(NT) if att != "none" else []):
                        ci = qb // 4
                        vvar = qb % 4
                        wd_ = (vvar + 1) * P
                        row = (qb + 1) * P
                        for h in range(ND):
                            gidx = qb * ND + h
                            v.wait_ge(mmS, gidx + 1)
                            dst = psS[:, ci * CH:ci * CH + wd_]
                            v.tensor_tensor(
                                dst, dst,
                                maskb[:, vvar * CH:vvar * CH + wd_],
                                op=ALU.add).then_inc(aux1, 1)
                            v.wait_ge(aux1, 2 * gidx + 1)
                            v.reduce_max(rm[:, :], psS[:, 0:row], axis=AX.X,
                                         negate=True).then_inc(aux1, 1)
                            v.wait_ge(aux1, 2 * gidx + 2)
                            v.tensor_scalar(nm[:, :], rm[:, :], QK_SCALE,
                                            None, op0=ALU.mult).then_inc(sS, 1)
                            if att != "full":
                                continue
                            v.wait_ge(sE, gidx + 1)
                            if gidx >= 1:
                                v.wait_ge(evO, gidx)
                            v.reciprocal(linv[:, :],
                                         lsum[:, :]).then_inc(rc, 1)

        if upto < 3:
            return nc
        with ExitStack() as dctx:
            sbd = lambda n_, s_, t_: dctx.enter_context(nc.sbuf_tensor(n_, s_, t_))
            sumh = sbd("p3_sumh", [P, 8 * D], F32)
            aT = sbd("p3_aT", [P, 2 * 1024], F32)
            psD = psA
            psT = [psS[:, 0:CH], psS[:, CH:2 * CH]]

            with nc.Block() as block:
                @block.sync
                def _(sy):
                    for tt in range(NT):
                        if tt >= 2:
                            sy.wait_ge(trD, 2 * (tt - 1))
                        sy.dma_start(
                            qtile[:, (tt % 2) * T:(tt % 2 + 1) * T],
                            os_[tt, :, :]).then_inc(dOl[tt % 2], 16)
                    if d_lvl == "full":
                        for th in range(2):
                            if th == 1:
                                sy.wait_ge(dR, 16 * 8)
                            for tl in range(8):
                                sy.dma_start(
                                    sumh[:, tl * D:(tl + 1) * D],
                                    R2[(th * 8 + tl) * P:
                                       (th * 8 + tl + 1) * P, :],
                                ).then_inc(dC, 16)
                            for tl in range(8):
                                gt = th * 8 + tl
                                sy.wait_ge(lnw, gt + 1)
                                sy.dma_start(
                                    R3[gt * P:(gt + 1) * P, :],
                                    sumh[:, tl * D:(tl + 1) * D],
                                ).then_inc(dR, 16)
                        sy.wait_ge(dR, 16 * NT)

                @block.gpsimd
                def _(g):
                    for th in (range(2) if d_lvl != "ot" else []):
                        for dt in range(ND):
                            s2 = th * ND + dt
                            if s2 >= 2:
                                g.wait_ge(mmD, 2 * (s2 - 1))
                            g.dma_start(
                                wbuf[:, (s2 % 2) * D:(s2 % 2 + 1) * D],
                                wo[dt, :, :]).then_inc(wld2[s2 % 2], 16)

                @block.tensor
                def _(t):
                    for tt in range(NT):
                        t.wait_ge(dOl[tt % 2], 16 * (tt // 2 + 1))
                        if tt >= 1:
                            t.wait_ge(evD, 2 * tt)
                        for dt in range(ND):
                            inst = t.transpose(
                                ptrB[dt // 8][:, (dt % 8) * P:(dt % 8 + 1) * P],
                                qtile[:, (tt % 2) * T + dt * P:
                                      (tt % 2) * T + (dt + 1) * P],
                                eyeb[:, :])
                            if dt % 8 == 7:
                                inst.then_inc(trD, 1)
                    if d_lvl != "ot":
                        t.wait_ge(evD, 2 * NT)
                        for th in range(2):
                            for dt in range(ND):
                                s2 = th * ND + dt
                                t.wait_ge(wld2[s2 % 2], 16 * (s2 // 2 + 1))
                                for c in range(2):
                                    m = 2 * s2 + c
                                    if m >= 2:
                                        t.wait_ge(evE, m - 1)
                                    for kt in range(ND):
                                        inst = t.matmul(
                                            psD[m % 2][:, :],
                                            wbuf[:, (s2 % 2) * D + kt * P:
                                                 (s2 % 2) * D + (kt + 1) * P],
                                            big[:, kt * T + th * 1024 + c * CH:
                                                kt * T + th * 1024 +
                                                (c + 1) * CH],
                                            start=(kt == 0),
                                            stop=(kt == ND - 1))
                                    inst.then_inc(mmD, 1)
                                if d_lvl == "mm":
                                    continue
                                t.wait_ge(evE, 2 * s2 + 2)
                                for pk in range(2):
                                    if s2 >= 1:
                                        t.wait_ge(addC, 2 * (s2 - 1) + pk + 1)
                                    for tq4 in range(4):
                                        tq = pk * 4 + tq4
                                        inst = t.transpose(
                                            psT[pk][:, tq4 * P:(tq4 + 1) * P],
                                            aT[:, (dt % 2) * 1024 + tq * P:
                                               (dt % 2) * 1024 + (tq + 1) * P],
                                            eyef[:, :])
                                    inst.then_inc(trC, 1)

                @block.scalar
                def _(s_):
                    for th in (range(2) if d_lvl != "ot" else []):
                        for dt in range(ND):
                            s2 = th * ND + dt
                            if s2 >= 2 and d_lvl == "full":
                                s_.wait_ge(trC, 2 * (s2 - 1))
                            for c in range(2):
                                m = 2 * s2 + c
                                s_.wait_ge(mmD, m + 1)
                                s_.activation(
                                    aT[:, (dt % 2) * 1024 + c * CH:
                                       (dt % 2) * 1024 + (c + 1) * CH],
                                    psD[m % 2][:, :], AF.Identity,
                                    bias=bob[:, dt:dt + 1]).then_inc(evE, 1)
                        if d_lvl == "mm":
                            continue
                        for tl in range(8):
                            gt = th * 8 + tl
                            s_.wait_ge(lnv, 2 * (gt + 1))
                            s_.activation(lrstd[:, :], mvb[:, 1:2], AF.Ln,
                                          bias=lnepsb[:, :]).then_inc(lnx, 1)
                            s_.wait_ge(lnx, gt + 1)
                            s_.activation(lrstd[:, :], lrstd[:, :], AF.Exp,
                                          scale=-0.5).then_inc(lnr, 1)

                @block.vector
                def _(v):
                    for tt in range(NT):
                        for half in range(2):
                            v.wait_ge(trD, 2 * tt + half + 1)
                            v.tensor_copy(
                                bass.AP(big, half * 8 * T + tt * P,
                                        [[ND * T, P], [T, 8], [1, P]]),
                                ptrB[half][:, :]).then_inc(evD, 1)
                    for th in (range(2) if d_lvl == "full" else []):
                        v.wait_ge(dC, 16 * 8 * (th + 1))
                        for dt in range(ND):
                            s2 = th * ND + dt
                            for pk in range(2):
                                v.wait_ge(trC, 2 * s2 + pk + 1)
                                dst = bass.AP(sumh, pk * 4 * D + dt * P,
                                              [[8 * D, P], [D, 4], [1, P]])
                                v.tensor_tensor(dst, psT[pk][:, :], dst,
                                                op=ALU.add).then_inc(addC, 1)
                        v.wait_ge(addC, 32 * (th + 1))
                        for tl in range(8):
                            gt = th * 8 + tl
                            tile = sumh[:, tl * D:(tl + 1) * D]
                            for c4 in range(4):
                                inst = v.bn_stats(
                                    bn6[:, c4 * 6:(c4 + 1) * 6],
                                    tile[:, c4 * CH:(c4 + 1) * CH])
                            inst.then_inc(lnv, 1)
                            v.wait_ge(lnv, 2 * gt + 1)
                            v.bn_aggr(mvb[:, :], bn6[:, :]).then_inc(lnv, 1)
                            v.wait_ge(lnr, gt + 1)
                            v.tensor_scalar(tile, tile, mvb[:, 0:1],
                                            lrstd[:, :], op0=ALU.subtract,
                                            op1=ALU.mult).then_inc(lnw, 1)
    return nc


def host_prep_p3(wq, wk, wv, wo, bq, bk, bv, bo):
    sl = lambda W: slab_weights(np.asarray(W, np.float32).T, osz=P)
    rb = lambda b: np.ascontiguousarray(
        np.asarray(b, np.float32).reshape(16, P).T)
    masks = np.zeros((4, P, CH), np.float32)
    for v_ in range(4):
        r = np.arange(P)[:, None]
        c = np.arange(CH)[None, :]
        masks[v_] = np.where(c <= v_ * P + r, 0.0, -1e30)
    return (sl(wq), sl(wk), sl(wv), sl(wo),
            rb(bq), rb(bk), rb(bv), rb(bo), masks)


def _mk_phase3_nc(upto=3, att="full", d_lvl="full"):
    nc = bass.Bass("TRN2", target_bir_lowering=False)
    R2 = nc.dram_tensor("R2", [T, D], F32, kind="ExternalInput")
    wq = nc.dram_tensor("wq", [16, P, D], F32, kind="ExternalInput")
    wk = nc.dram_tensor("wk", [16, P, D], F32, kind="ExternalInput")
    wv = nc.dram_tensor("wv", [16, P, D], F32, kind="ExternalInput")
    wo = nc.dram_tensor("wo", [16, P, D], F32, kind="ExternalInput")
    nw = nc.dram_tensor("nw", [D], F32, kind="ExternalInput")
    bq = nc.dram_tensor("bq", [P, 16], F32, kind="ExternalInput")
    bk = nc.dram_tensor("bk", [P, 16], F32, kind="ExternalInput")
    bv = nc.dram_tensor("bv", [P, 16], F32, kind="ExternalInput")
    bo = nc.dram_tensor("bo", [P, 16], F32, kind="ExternalInput")
    masks = nc.dram_tensor("masks", [4, P, CH], F32, kind="ExternalInput")
    eye32 = nc.dram_tensor("eye32", [P, P], F32, kind="ExternalInput")
    eye16 = nc.dram_tensor("eye16", [P, P], BF16, kind="ExternalInput")
    R3 = nc.dram_tensor("R3", [T, D], F32, kind="ExternalOutput")
    build_phase3(nc, R2, wq, wk, wv, wo, nw, bq, bk, bv, bo, masks,
                 eye32, eye16, R3, upto=upto, att=att, d_lvl=d_lvl)
    return nc


def run_phase3(R2_full, norm_w, wq, bq, wk, bk, wv, bv, wo, bo,
               cores=N_CORES, upto=3, att="full", d_lvl="full"):
    wqs, wks, wvs, wos, bqb, bkb, bvb, bob, masks = host_prep_p3(
        wq, wk, wv, wo, bq, bk, bv, bo)
    nc = _mk_phase3_nc(upto=upto, att=att, d_lvl=d_lvl)
    in_maps = []
    for c in range(cores):
        in_maps.append(dict(
            R2=np.ascontiguousarray(R2_full[c], np.float32),
            wq=wqs, wk=wks, wv=wvs, wo=wos,
            nw=np.ascontiguousarray(norm_w, np.float32),
            bq=bqb, bk=bkb, bv=bvb, bo=bob, masks=masks,
            eye32=_EYE32, eye16=_EYE16))
    res = run_bass_kernel_spmd(nc, in_maps, core_ids=list(range(cores)))
    return np.stack([r["R3"] for r in res.results])




def build_phase4(nc, R3, w1t, w2t, b1, b2, eye32, eye16, OUT):
    _clear_sems(nc)
    w1q = nc.dram_tensor("p4_w1q", [64, P, 2048], BF16)
    w2q = nc.dram_tensor("p4_w2q", [2, 16, P, 4096], BF16)
    HT = 1024
    NIT = 32
    NP1 = 64
    NPT = 128
    NW1 = 16777216.0
    NW2 = 16777216.0

    def w1_chunk(th, ip, ls, c):
        return th * 192 + ip * 96 + 2 * ls + c

    def w2_chunk(th, ip, dt, c):
        return th * 192 + ip * 96 + 64 + 2 * dt + c

    with ExitStack() as ctx:
        sb = lambda n_, s_, t_: ctx.enter_context(nc.sbuf_tensor(n_, s_, t_))
        psl = lambda n_, s_, t_: ctx.enter_context(nc.psum_tensor(n_, s_, t_))
        sem = lambda n_: ctx.enter_context(nc.semaphore(n_))

        eyeb = sb("p4_eyeb", [P, P], BF16)
        eyef = sb("p4_eyef", [P, P], F32)
        onesf = sb("p4_onesf", [P, 1], F32)
        onesr = sb("p4_onesr", [1, P], F32)
        mvb = sb("p4_mv", [P, 2], F32)
        bn6 = sb("p4_bn6", [P, 24], F32)
        rstd = sb("p4_rstd", [P, 1], F32)
        lnepsb = sb("p4_lneps", [P, 1], F32)
        b1b = sb("p4_b1b", [P, 64], F32)
        b2b = sb("p4_b2b", [P, 16], F32)
        psA = [psl(f"p4_psA{i}", [P, CH], F32) for i in range(4)]
        ptr = [psl(f"p4_ptr{i}", [P, 8 * P], BF16) for i in range(2)]
        psT = [psl(f"p4_psT{i}", [P, CH], F32) for i in range(2)]

        dA = sem("p4_dA")
        dAx = sem("p4_dAx")
        dS = [sem("p4_dS0"), sem("p4_dS1")]
        qsc = sem("p4_qsc")
        qv = sem("p4_qv")
        qg = sem("p4_qg")
        dQw = [sem("p4_dQw0"), sem("p4_dQw1")]
        thr = sem("p4_thr")
        trA = sem("p4_trA")
        evA = sem("p4_evA")
        wld1 = [sem("p4_wld10"), sem("p4_wld11")]
        wld2 = [sem("p4_wld20"), sem("p4_wld21")]
        mm = sem("p4_mm")
        ev = sem("p4_ev")
        trC = sem("p4_trC")
        addC = sem("p4_addC")
        lnv = sem("p4_lnv")
        lnr = sem("p4_lnr")
        lnx = sem("p4_lnx")
        lnw = sem("p4_lnw")
        dC = sem("p4_dC")
        dR = sem("p4_dR")
        msc = sem("p4_msc")

        def src_ap(i):
            if i < NP1:
                return w1t[i, :, :]
            j = i - NP1
            ip, dt, hf = j // 32, (j % 32) // 2, j % 2
            return w2t[ip, dt, :, hf * D:(hf + 1) * D]

        def dst_ap(i):
            if i < NP1:
                return w1q[i, :, :]
            j = i - NP1
            ip, dt, hf = j // 32, (j % 32) // 2, j % 2
            return w2q[ip, dt, :, hf * D:(hf + 1) * D]

        with ExitStack() as qctx:
            sbq = lambda n_, s_, t_: qctx.enter_context(nc.sbuf_tensor(n_, s_, t_))
            strip = sbq("p4_strip", [P, 2 * D], F32)
            qa = sbq("p4_qa", [P, D], BF16)
            qb_ = sbq("p4_qb", [P, D], BF16)
            qq = sbq("p4_qq", [P, 2 * D], BF16)
            pb = sbq("p4_pb", [P, NPT], F32)
            red = sbq("p4_red", [P, 2], F32)
            tpq = sbq("p4_tpq", [1, 2], F32)
            th1 = sbq("p4_th1", [P, 2], F32)
            th2 = sbq("p4_th2", [P, 2], F32)

            with nc.Block() as block:
                @block.sync
                def _(sy):
                    sy.dma_start(eyef[:, :], eye32[:, :]).then_inc(dA, 16)
                    sy.dma_start(eyeb[:, :], eye16[:, :]).then_inc(dA, 16)
                    sy.dma_start(b1b[:, :], b1[:, :]).then_inc(dA, 16)
                    sy.dma_start(b2b[:, :], b2[:, :]).then_inc(dA, 16)
                    for i in range(NPT):
                        if i >= 2:
                            sy.wait_ge(qsc, i - 1)
                        sy.dma_start(strip[:, (i % 2) * D:(i % 2 + 1) * D],
                                     src_ap(i)).then_inc(dS[i % 2], 16)
                    for i in range(NPT):
                        if i < 2:
                            sy.wait_ge(qsc, NPT)
                        else:
                            sy.wait_ge(qv, 3 * (i - 1))
                        sy.dma_start(strip[:, (i % 2) * D:(i % 2 + 1) * D],
                                     src_ap(i)).then_inc(dS[i % 2], 16)
                        sy.wait_ge(qv, 3 * i + 3)
                        sy.dma_start(dst_ap(i),
                                     qq[:, (i % 2) * D:(i % 2 + 1) * D],
                                     ).then_inc(dQw[i % 2], 16)
                    sy.wait_ge(dQw[0], 16 * 64)
                    sy.wait_ge(dQw[1], 16 * 64)

                @block.gpsimd
                def _(g):
                    g.memset(lnepsb[:, :], LN_EPS).then_inc(msc, 1)
                    g.memset(onesf[:, :], 1.0).then_inc(msc, 1)
                    g.memset(onesr[:, :], 1.0).then_inc(msc, 1)
                @block.scalar
                def _(s_):
                    for i in range(NPT):
                        s_.wait_ge(dS[i % 2], 16 * (i // 2 + 1))
                        s_.activation(
                            strip[:, (i % 2) * D:(i % 2 + 1) * D],
                            strip[:, (i % 2) * D:(i % 2 + 1) * D],
                            AF.Abs,
                            accum_out=pb[:, i:i + 1]).then_inc(qsc, 1)
                    s_.wait_ge(thr, 3)
                    s_.activation(tpq[:, :], psT[0][0:1, 0:2], AF.Copy,
                                  scale=0.7 / NW1).then_inc(thr, 1)
                    s_.wait_ge(thr, 5)
                    s_.activation(th1[:, 0:1], psT[1][:, 0:1], AF.Copy)
                    s_.activation(th2[:, 0:1], psT[1][:, 1:2], AF.Copy)
                    s_.activation(th1[:, 1:2], psT[1][:, 0:1], AF.Copy,
                                  scale=-1.0)
                    s_.activation(th2[:, 1:2], psT[1][:, 1:2], AF.Copy,
                                  scale=-1.0).then_inc(thr, 1)

                @block.vector
                def _(v):
                    v.wait_ge(qsc, NP1)
                    v.reduce_sum(red[:, 0:1], pb[:, 0:NP1],
                                 axis=AX.X).then_inc(thr, 1)
                    v.wait_ge(qsc, NPT)
                    v.reduce_sum(red[:, 1:2], pb[:, NP1:NPT],
                                 axis=AX.X).then_inc(thr, 1)
                    v.wait_ge(thr, 6)
                    for i in range(NPT):
                        tcol = th1 if i < NP1 else th2
                        v.wait_ge(dS[i % 2], 16 * (NPT // 2 + i // 2 + 1))
                        if i >= 2:
                            v.wait_ge(dQw[i % 2], 16 * (i // 2))
                        v.tensor_scalar(
                            qa[:, :],
                            strip[:, (i % 2) * D:(i % 2 + 1) * D],
                            tcol[:, 0:1], None,
                            op0=ALU.is_ge).then_inc(qv, 1)
                        v.tensor_scalar(
                            qb_[:, :],
                            strip[:, (i % 2) * D:(i % 2 + 1) * D],
                            tcol[:, 1:2], None,
                            op0=ALU.is_le).then_inc(qv, 1)
                        v.wait_ge(qv, 3 * i + 2)
                        v.tensor_tensor(qq[:, (i % 2) * D:(i % 2 + 1) * D],
                                        qa[:, :], qb_[:, :],
                                        op=ALU.subtract).then_inc(qv, 1)

                @block.tensor
                def _(t):
                    t.wait_ge(dA, 64)
                    t.wait_ge(msc, 3)
                    t.wait_ge(thr, 1)
                    t.matmul(psT[0][0:1, 0:1], onesf[:, :], red[:, 0:1],
                             start=True, stop=True)
                    t.wait_ge(thr, 2)
                    t.matmul(psT[0][0:1, 1:2], onesf[:, :], red[:, 1:2],
                             start=True, stop=True).then_inc(thr, 1)
                    t.wait_ge(thr, 4)
                    t.matmul(psT[1][:, 0:2], onesr[:, :], tpq[:, :],
                             start=True, stop=True).then_inc(thr, 1)

        with ExitStack() as mctx:
            sbm = lambda n_, s_, t_: mctx.enter_context(nc.sbuf_tensor(n_, s_, t_))
            h = sbm("p4_h", [P, NIT * HT], BF16)
            xT = sbm("p4_xT", [P, ND * HT], BF16)
            sumh = sbm("p4_sumh", [P, 8 * D], F32)
            wbuf1 = sbm("p4_wbuf1", [P, 2 * D], BF16)
            wbuf2 = sbm("p4_wbuf2", [P, 2 * 4096], BF16)
            aT = sbm("p4_aT", [P, 2 * HT], F32)
            xtmb = sbm("p4_xtmb", [P, D], BF16)
            th1b = sbm("p4_th1b", [P, 2], F32)

            with nc.Block() as block:
                @block.sync
                def _(sy):
                    for th in range(2):
                        if th == 1:
                            sy.wait_ge(dR, 16 * 8)
                        for tl in range(8):
                            sy.dma_start(
                                sumh[:, tl * D:(tl + 1) * D],
                                R3[(th * 8 + tl) * P:
                                   (th * 8 + tl + 1) * P, :],
                            ).then_inc(dC, 16)
                        for ip in range(2):
                            for ls in range(NIT):
                                gs = (th * 2 + ip) * NIT + ls
                                if gs >= 2:
                                    g2 = gs - 2
                                    sy.wait_ge(mm, w1_chunk(
                                        g2 // 64, (g2 % 64) // 32,
                                        g2 % 32, 1) + 1)
                                it = ip * NIT + ls
                                sy.dma_start(
                                    wbuf1[:, (gs % 2) * D:(gs % 2 + 1) * D],
                                    w1q[it, :, :],
                                ).then_inc(wld1[gs % 2], 16)
                            for dt in range(ND):
                                ws = (th * 2 + ip) * ND + dt
                                if ws >= 2:
                                    w2_ = ws - 2
                                    sy.wait_ge(mm, w2_chunk(
                                        w2_ // 32, (w2_ % 32) // 16,
                                        w2_ % 16, 1) + 1)
                                sy.dma_start(
                                    wbuf2[:, (ws % 2) * 4096:
                                          (ws % 2 + 1) * 4096],
                                    w2q[ip, dt, :, :],
                                ).then_inc(wld2[ws % 2], 16)
                        for tl in range(8):
                            gt = th * 8 + tl
                            sy.wait_ge(lnw, gt + 1)
                            sy.dma_start(
                                OUT[gt * P:(gt + 1) * P, :],
                                sumh[:, tl * D:(tl + 1) * D],
                            ).then_inc(dR, 16)
                    sy.wait_ge(dR, 16 * NT)

                @block.gpsimd
                def _(g):
                    for tt in range(NT):
                        if tt >= 1:
                            g.wait_ge(trA, 2 * tt)
                        g.dma_start(
                            xtmb[:, :],
                            R3[tt * P:(tt + 1) * P, :]).then_inc(dAx, 16)

                @block.scalar
                def _(s_):
                    for th in range(2):
                        for ip in range(2):
                            for ls in range(NIT):
                                il = ip * NIT + ls
                                for c in range(2):
                                    m = w1_chunk(th, ip, ls, c)
                                    s_.wait_ge(mm, m + 1)
                                    s_.activation(
                                        h[:, ls * HT + c * CH:
                                          ls * HT + (c + 1) * CH],
                                        psA[m % 4][:, :], AF.Gelu,
                                        bias=b1b[:, il:il + 1],
                                    ).then_inc(ev, 1)
                            for dt in range(ND):
                                s2g = (th * 2 + ip) * ND + dt
                                if s2g >= 2:
                                    s_.wait_ge(trC, 2 * (s2g - 1))
                                for c in range(2):
                                    m = w2_chunk(th, ip, dt, c)
                                    s_.wait_ge(mm, m + 1)
                                    if ip == 1:
                                        s_.activation(
                                            aT[:, (dt % 2) * HT + c * CH:
                                               (dt % 2) * HT + (c + 1) * CH],
                                            psA[m % 4][:, :], AF.Identity,
                                            bias=b2b[:, dt:dt + 1],
                                        ).then_inc(ev, 1)
                                    else:
                                        s_.activation(
                                            aT[:, (dt % 2) * HT + c * CH:
                                               (dt % 2) * HT + (c + 1) * CH],
                                            psA[m % 4][:, :],
                                            AF.Copy).then_inc(ev, 1)
                        for tl in range(8):
                            gt = th * 8 + tl
                            s_.wait_ge(lnv, 2 * (gt + 1))
                            if gt == 0:
                                s_.wait_ge(msc, 1)
                            s_.activation(rstd[:, :], mvb[:, 1:2], AF.Ln,
                                          bias=lnepsb[:, :]).then_inc(lnx, 1)
                            s_.wait_ge(lnx, gt + 1)
                            s_.activation(rstd[:, :], rstd[:, :], AF.Exp,
                                          scale=-0.5).then_inc(lnr, 1)

                @block.vector
                def _(v):
                    for th in range(2):
                        for tl in range(8):
                            tt = th * 8 + tl
                            if tt == 8:
                                v.wait_ge(mm, w1_chunk(0, 1, 31, 1) + 1)
                            for half in range(2):
                                v.wait_ge(trA, 2 * tt + half + 1)
                                v.tensor_copy(
                                    bass.AP(xT, half * 8 * HT + tl * P,
                                            [[ND * HT, P], [HT, 8], [1, P]]),
                                    ptr[half][:, :]).then_inc(evA, 1)
                        v.wait_ge(dC, 16 * 8 * (th + 1))
                        for ip in range(2):
                            for dt in range(ND):
                                s2g = (th * 2 + ip) * ND + dt
                                for pk in range(2):
                                    v.wait_ge(trC, 2 * s2g + pk + 1)
                                    dst = bass.AP(sumh,
                                                  pk * 4 * D + dt * P,
                                                  [[8 * D, P], [D, 4],
                                                   [1, P]])
                                    v.tensor_tensor(
                                        dst, psT[pk][:, :], dst,
                                        op=ALU.add).then_inc(addC, 1)
                        v.wait_ge(addC, 64 * (th + 1))
                        for tl in range(8):
                            gt = th * 8 + tl
                            tile = sumh[:, tl * D:(tl + 1) * D]
                            for c4 in range(4):
                                inst = v.bn_stats(
                                    bn6[:, c4 * 6:(c4 + 1) * 6],
                                    tile[:, c4 * CH:(c4 + 1) * CH])
                            inst.then_inc(lnv, 1)
                            v.wait_ge(lnv, 2 * gt + 1)
                            v.bn_aggr(mvb[:, :], bn6[:, :]).then_inc(lnv, 1)
                            v.wait_ge(lnr, gt + 1)
                            v.tensor_scalar(tile, tile, mvb[:, 0:1],
                                            rstd[:, :], op0=ALU.subtract,
                                            op1=ALU.mult).then_inc(lnw, 1)

                @block.tensor
                def _(t):
                    for th in range(2):
                        for tl in range(8):
                            tt = th * 8 + tl
                            t.wait_ge(dAx, 16 * (tt + 1))
                            if tt >= 1:
                                t.wait_ge(evA, 2 * tt)
                            for dt in range(ND):
                                inst = t.transpose(
                                    ptr[dt // 8][:, (dt % 8) * P:
                                                 (dt % 8 + 1) * P],
                                    xtmb[:, dt * P:(dt + 1) * P], eyeb[:, :])
                                if dt % 8 == 7:
                                    inst.then_inc(trA, 1)
                        t.wait_ge(evA, 2 * (th * 8 + 8))
                        for ip in range(2):
                            for ls in range(NIT):
                                gs = (th * 2 + ip) * NIT + ls
                                t.wait_ge(wld1[gs % 2], 16 * (gs // 2 + 1))
                                for c in range(2):
                                    m = w1_chunk(th, ip, ls, c)
                                    if m >= 4:
                                        t.wait_ge(ev, m - 3)
                                    for kt in range(ND):
                                        inst = t.matmul(
                                            psA[m % 4][:, :],
                                            wbuf1[:, (gs % 2) * D + kt * P:
                                                  (gs % 2) * D +
                                                  (kt + 1) * P],
                                            xT[:, kt * HT + c * CH:
                                               kt * HT + (c + 1) * CH],
                                            start=(kt == 0),
                                            stop=(kt == ND - 1))
                                    inst.then_inc(mm, 1)
                            t.wait_ge(ev, th * 192 + ip * 96 + 64)
                            for dt in range(ND):
                                ws = (th * 2 + ip) * ND + dt
                                s2g = ws
                                t.wait_ge(wld2[ws % 2], 16 * (ws // 2 + 1))
                                for c in range(2):
                                    m = w2_chunk(th, ip, dt, c)
                                    t.wait_ge(ev, m - 3)
                                    for kt in range(NIT):
                                        inst = t.matmul(
                                            psA[m % 4][:, :],
                                            wbuf2[:, (ws % 2) * 4096 +
                                                  kt * P:
                                                  (ws % 2) * 4096 +
                                                  (kt + 1) * P],
                                            h[:, kt * HT + c * CH:
                                              kt * HT + (c + 1) * CH],
                                            start=(kt == 0),
                                            stop=(kt == NIT - 1))
                                    inst.then_inc(mm, 1)
                                t.wait_ge(ev, w2_chunk(th, ip, dt, 1) + 1)
                                for pk in range(2):
                                    if s2g >= 1:
                                        t.wait_ge(addC,
                                                  2 * (s2g - 1) + pk + 1)
                                    for tq4 in range(4):
                                        tq = pk * 4 + tq4
                                        inst = t.transpose(
                                            psT[pk][:, tq4 * P:
                                                    (tq4 + 1) * P],
                                            aT[:, (dt % 2) * HT + tq * P:
                                               (dt % 2) * HT +
                                               (tq + 1) * P],
                                            eyef[:, :])
                                    inst.then_inc(trC, 1)
    return nc


def host_prep_p4(w1, b1, w2, b2):
    w1t_s = slab_weights(np.asarray(w1, np.float32).T, osz=P)
    W2T = np.asarray(w2, np.float32).T
    w2t_s = np.stack([slab_weights(W2T[ip * 4096:(ip + 1) * 4096, :], osz=P)
                      for ip in range(2)])
    b1b = np.ascontiguousarray(np.asarray(b1, np.float32).reshape(64, P).T)
    b2b = np.ascontiguousarray(np.asarray(b2, np.float32).reshape(16, P).T)
    return w1t_s, np.ascontiguousarray(w2t_s), b1b, b2b


def _mk_phase4_nc():
    nc = bass.Bass("TRN2", target_bir_lowering=False)
    R3 = nc.dram_tensor("R3", [T, D], F32, kind="ExternalInput")
    w1t = nc.dram_tensor("w1t", [64, P, 2048], F32, kind="ExternalInput")
    w2t = nc.dram_tensor("w2t", [2, 16, P, 4096], F32, kind="ExternalInput")
    b1 = nc.dram_tensor("b1", [P, 64], F32, kind="ExternalInput")
    b2 = nc.dram_tensor("b2", [P, 16], F32, kind="ExternalInput")
    eye32 = nc.dram_tensor("eye32", [P, P], F32, kind="ExternalInput")
    eye16 = nc.dram_tensor("eye16", [P, P], BF16, kind="ExternalInput")
    OUT = nc.dram_tensor("OUT", [T, D], F32, kind="ExternalOutput")
    build_phase4(nc, R3, w1t, w2t, b1, b2, eye32, eye16, OUT)
    return nc


def run_phase4(R3_full, w1, b1, w2, b2, cores=N_CORES):
    w1t, w2t, b1b, b2b = host_prep_p4(w1, b1, w2, b2)
    nc = _mk_phase4_nc()
    in_maps = []
    for c in range(cores):
        in_maps.append(dict(R3=np.ascontiguousarray(R3_full[c], np.float32),
                            w1t=w1t, w2t=w2t, b1=b1b, b2=b2b,
                            eye32=_EYE32, eye16=_EYE16))
    res = run_bass_kernel_spmd(nc, in_maps, core_ids=list(range(cores)))
    return np.stack([r["OUT"] for r in res.results])



def _mk_phase1_nc():
    nc = bass.Bass("TRN2", target_bir_lowering=False)
    x = nc.dram_tensor("x", [T, D], F32, kind="ExternalInput")
    wqkv = nc.dram_tensor("wqkv", [48, P, D], F32, kind="ExternalInput")
    wout = nc.dram_tensor("wout", [16, P, D], F32, kind="ExternalInput")
    eye32 = nc.dram_tensor("eye32", [P, P], F32, kind="ExternalInput")
    eye16 = nc.dram_tensor("eye16", [P, P], BF16, kind="ExternalInput")
    R1 = nc.dram_tensor("R1", [T, D], F32, kind="ExternalOutput")
    build_phase1(nc, x, wqkv, wout, eye32, eye16, R1)
    return nc


_EYE32 = np.eye(P, dtype=np.float32)
_EYE16 = _EYE32.astype(ml_dtypes.bfloat16)


def host_prep_p1(W_qkv, W_out):
    WT = np.asarray(W_qkv, np.float32).T
    cols = []
    for i in range(ND):
        cols.append(WT[:, 4096 + i * P:4096 + (i + 1) * P])
        cols.append(WT[:, 2048 + i * P:2048 + (i + 1) * P])
        cols.append(WT[:, 0 + i * P:0 + (i + 1) * P])
    WTp = np.concatenate(cols, axis=1)
    wqkv_s = slab_weights(WTp, osz=P)
    wout_s = slab_weights(np.asarray(W_out, np.float32).T, osz=P)
    return wqkv_s, wout_s


def run_phase1(x_full, W_qkv, W_out, cores=N_CORES):
    wqkv_s, wout_s = host_prep_p1(W_qkv, W_out)
    nc = _mk_phase1_nc()
    in_maps = []
    for c in range(cores):
        in_maps.append(dict(x=np.ascontiguousarray(x_full[c], np.float32),
                            wqkv=wqkv_s, wout=wout_s,
                            eye32=_EYE32, eye16=_EYE16))
    res = run_bass_kernel_spmd(nc, in_maps, core_ids=list(range(cores)))
    return np.stack([r["R1"] for r in res.results])


def _mk_fused_nc():
    nc = bass.Bass("TRN2", target_bir_lowering=False)
    x = nc.dram_tensor("x", [T, D], F32, kind="ExternalInput")
    wqkv = nc.dram_tensor("wqkv", [48, P, D], F32, kind="ExternalInput")
    wout = nc.dram_tensor("wout", [16, P, D], F32, kind="ExternalInput")
    wgu = nc.dram_tensor("wgu", [128, P, D], F32, kind="ExternalInput")
    wd = nc.dram_tensor("wd", [2, 16, P, 4096], F32, kind="ExternalInput")
    wq = nc.dram_tensor("wq", [16, P, D], F32, kind="ExternalInput")
    wk = nc.dram_tensor("wk", [16, P, D], F32, kind="ExternalInput")
    wv = nc.dram_tensor("wv", [16, P, D], F32, kind="ExternalInput")
    wo = nc.dram_tensor("wo", [16, P, D], F32, kind="ExternalInput")
    nw = nc.dram_tensor("nw", [D], F32, kind="ExternalInput")
    bq = nc.dram_tensor("bq", [P, 16], F32, kind="ExternalInput")
    bk = nc.dram_tensor("bk", [P, 16], F32, kind="ExternalInput")
    bv = nc.dram_tensor("bv", [P, 16], F32, kind="ExternalInput")
    bo = nc.dram_tensor("bo", [P, 16], F32, kind="ExternalInput")
    masks = nc.dram_tensor("masks", [4, P, CH], F32, kind="ExternalInput")
    w1t = nc.dram_tensor("w1t", [64, P, 2048], F32, kind="ExternalInput")
    w2t = nc.dram_tensor("w2t", [2, 16, P, 4096], F32, kind="ExternalInput")
    b1 = nc.dram_tensor("b1", [P, 64], F32, kind="ExternalInput")
    b2 = nc.dram_tensor("b2", [P, 16], F32, kind="ExternalInput")
    eye32 = nc.dram_tensor("eye32", [P, P], F32, kind="ExternalInput")
    eye16 = nc.dram_tensor("eye16", [P, P], BF16, kind="ExternalInput")
    R1 = nc.dram_tensor("R1", [T, D], F32)
    R2 = nc.dram_tensor("R2", [T, D], F32)
    R3 = nc.dram_tensor("R3", [T, D], F32)
    OUT = nc.dram_tensor("OUT", [T, D], F32, kind="ExternalOutput")
    build_phase1(nc, x, wqkv, wout, eye32, eye16, R1)
    build_phase2(nc, R1, wgu, wd, eye32, eye16, R2)
    build_phase3(nc, R2, wq, wk, wv, wo, nw, bq, bk, bv, bo, masks,
                 eye32, eye16, R3)
    build_phase4(nc, R3, w1t, w2t, b1, b2, eye32, eye16, OUT)
    return nc


def kernel(**inputs):
    x = np.asarray(inputs["L0_attn_x"], np.float32)
    wqkv_s, wout_s = host_prep_p1(inputs["L0_attn_W_qkv"],
                                  inputs["L0_attn_W_out"])
    wgu_s, wd_s = host_prep_p2(inputs["L0_ffn_w_gate_up"],
                               inputs["L0_ffn_w_down"])
    wqs, wks, wvs, wos, bqb, bkb, bvb, bob, masks = host_prep_p3(
        inputs["L1_attn_wq"], inputs["L1_attn_wk"], inputs["L1_attn_wv"],
        inputs["L1_attn_wo"], inputs["L1_attn_bq"], inputs["L1_attn_bk"],
        inputs["L1_attn_bv"], inputs["L1_attn_bo"])
    w1t, w2t, b1b, b2b = host_prep_p4(inputs["L1_ffn_w1"],
                                      inputs["L1_ffn_b1"],
                                      inputs["L1_ffn_w2"],
                                      inputs["L1_ffn_b2"])
    nw = np.ascontiguousarray(np.asarray(inputs["L1_attn_norm_w"],
                                         np.float32))
    nc = _mk_fused_nc()
    in_maps = []
    for c in range(N_CORES):
        in_maps.append(dict(
            x=np.ascontiguousarray(x[c]),
            wqkv=wqkv_s, wout=wout_s, wgu=wgu_s, wd=wd_s,
            wq=wqs, wk=wks, wv=wvs, wo=wos, nw=nw,
            bq=bqb, bk=bkb, bv=bvb, bo=bob, masks=masks,
            w1t=w1t, w2t=w2t, b1=b1b, b2=b2b,
            eye32=_EYE32, eye16=_EYE16))
    res = run_bass_kernel_spmd(nc, in_maps, core_ids=list(range(N_CORES)))
    return np.stack([r["OUT"] for r in res.results]).astype(np.float32)


def kernel_unfused(**inputs):
    x = np.asarray(inputs["L0_attn_x"], np.float32)
    R1 = run_phase1(x, inputs["L0_attn_W_qkv"], inputs["L0_attn_W_out"])
    R2 = run_phase2(R1, inputs["L0_ffn_w_gate_up"], inputs["L0_ffn_w_down"])
    R3 = run_phase3(R2, inputs["L1_attn_norm_w"],
                    inputs["L1_attn_wq"], inputs["L1_attn_bq"],
                    inputs["L1_attn_wk"], inputs["L1_attn_bk"],
                    inputs["L1_attn_wv"], inputs["L1_attn_bv"],
                    inputs["L1_attn_wo"], inputs["L1_attn_bo"])
    out = run_phase4(R3, inputs["L1_ffn_w1"], inputs["L1_ffn_b1"],
                     inputs["L1_ffn_w2"], inputs["L1_ffn_b2"])
    return out.astype(np.float32)


# version 13
# speedup vs baseline: 1.2310x; 1.0236x over previous
import os
import sys

sys.path.insert(0, "/opt/trn_rl_repo")

import numpy as np
import ml_dtypes
import concourse.bass as bass
import concourse.mybir as mybir
from concourse.bass_utils import run_bass_kernel_spmd
from concourse.alu_op_type import AluOpType as ALU
from contextlib import ExitStack

F32 = mybir.dt.float32
BF16 = mybir.dt.bfloat16
AF = mybir.ActivationFunctionType
AX = mybir.AxisListType

P = 128
T = 2048
D = 2048
NT = T // P
ND = D // P
CH = 512
NCH = T // CH

QK_SCALE = float(P) ** -0.5
L0_SCALE = float(P) ** -0.5
LN_EPS = 1e-5
RMS_EPS = 1e-6
L0_EPS = 1e-9
N_CORES = 8



def slab_weights(WT, osz=P):
    WT = np.ascontiguousarray(WT, dtype=np.float32)
    K, O = WT.shape
    kt = K // P
    A = WT.reshape(kt, P, O // osz, osz).transpose(2, 1, 0, 3)
    return np.ascontiguousarray(A.reshape(O // osz, P, kt * osz))



def emit_xT_build(block, src_dram, row0, nrow_tiles, xT, stage, sems,
                  cast=True):
    raise NotImplementedError



def build_phase1(nc, x, wqkv, wout, eye32, eye16, R1):
    NSLAB_B = 3 * ND

    def cb(s):
        if s < NSLAB_B:
            return 4 * (s + 1)
        return 4 * NSLAB_B + 2 * (s - NSLAB_B + 1)

    with ExitStack() as ctx:
        sb = lambda n_, s_, t_: ctx.enter_context(nc.sbuf_tensor(n_, s_, t_))
        sem = lambda n_: ctx.enter_context(nc.semaphore(n_))

        outT = sb("p1_outT", [P, ND * T], BF16)
        wbuf = sb("p1_wbuf", [P, 3 * D], BF16)
        eyeb = sb("p1_eyeb", [P, P], BF16)
        eyef = sb("p1_eyef", [P, P], F32)
        ones1 = sb("p1_ones1", [P, 1], BF16)
        epsb = sb("p1_epsb", [P, 1], F32)
        lnepsb = sb("p1_lnepsb", [P, 1], F32)

        dA = sem("p1_dA")
        wld = [sem("p1_wld0"), sem("p1_wld1"), sem("p1_wld2")]
        mmB = sem("p1_mmB")
        evB = sem("p1_evB")
        trA = sem("p1_trA")
        evA = sem("p1_evA")
        kst = sem("p1_kst")
        dsem = sem("p1_dsem")
        recS = sem("p1_recS")
        numS = sem("p1_numS")
        vout = sem("p1_vout")
        vch = sem("p1_vch")
        sch = sem("p1_sch")
        trC = sem("p1_trC")
        addC = sem("p1_addC")
        lnv = sem("p1_lnv")
        lnr = sem("p1_lnr")
        lnx = sem("p1_lnx")
        lnw = sem("p1_lnw")
        dC = sem("p1_dC")
        dR = [sem("p1_dR0"), sem("p1_dR1")]
        msc = sem("p1_msc")

        ones_bcast = bass.AP(ones1, 0, [[1, P], [0, T]])

        with ExitStack() as sctx:
            sbl = lambda n_, s_, t_: sctx.enter_context(nc.sbuf_tensor(n_, s_, t_))
            psl = lambda n_, s_, t_: sctx.enter_context(nc.psum_tensor(n_, s_, t_))
            xT = sbl("p1_xT", [P, ND * T], BF16)
            xtmb = sbl("p1_xtmb", [P, D], BF16)
            vbuf = sbl("p1_vbuf", [P, T], BF16)
            kp = sbl("p1_kp", [P, T], BF16)
            kv = sbl("p1_kv", [P, T], BF16)
            qp = sbl("p1_qp", [P, T], F32)
            abuf = sbl("p1_a", [P, T], F32)
            cbuf = sbl("p1_c", [P, T], F32)
            KVc = sbl("p1_KVc", [P, T], F32)
            Kc = sbl("p1_Kc", [P, T], F32)
            psA = [psl(f"p1_psA{i}", [P, CH], F32) for i in range(4)]
            ptr = [psl(f"p1_ptr{i}", [P, 8 * P], BF16) for i in range(2)]

            with nc.Block() as block:
                @block.gpsimd
                def _(g):
                    g.memset(ones1[:, :], 1.0).then_inc(msc, 1)
                    g.memset(epsb[:, :], L0_EPS).then_inc(msc, 1)
                    g.memset(lnepsb[:, :], LN_EPS).then_inc(msc, 1)
                    g.dma_start(eyeb[:, :], eye16[:, :]).then_inc(dA, 16)
                    g.dma_start(eyef[:, :], eye32[:, :]).then_inc(dA, 16)
                    for tt in range(NT):
                        if tt >= 1:
                            g.wait_ge(trA, 2 * tt)
                        g.dma_start(
                            xtmb[:, :], x[tt * P:(tt + 1) * P, :],
                        ).then_inc(dA, 16)
                    for s in range(NSLAB_B):
                        if s >= 3:
                            g.wait_ge(mmB, cb(s - 3))
                        g.dma_start(
                            wbuf[:, (s % 3) * D:(s % 3 + 1) * D],
                            wqkv[s, :, :],
                        ).then_inc(wld[s % 3], 16)

                @block.tensor
                def _(t):
                    for tt in range(NT):
                        t.wait_ge(dA, 32 + 16 * (tt + 1))
                        if tt >= 1:
                            t.wait_ge(evA, 2 * tt)
                        for dt in range(ND):
                            inst = t.transpose(
                                ptr[dt // 8][:, (dt % 8) * P:(dt % 8 + 1) * P],
                                xtmb[:, dt * P:(dt + 1) * P],
                                eyeb[:, :])
                            if dt % 8 == 7:
                                inst.then_inc(trA, 1)
                    t.wait_ge(evA, 2 * NT)
                    for s in range(NSLAB_B):
                        t.wait_ge(wld[s % 3], 16 * (s // 3 + 1))
                        for c in range(NCH):
                            m = 4 * s + c
                            if m >= 4:
                                t.wait_ge(evB, m - 3)
                            for kt in range(ND):
                                inst = t.matmul(
                                    psA[m % 4][:, :],
                                    wbuf[:, (s % 3) * D + kt * P:
                                         (s % 3) * D + (kt + 1) * P],
                                    xT[:, kt * T + c * CH:kt * T + (c + 1) * CH],
                                    start=(kt == 0), stop=(kt == ND - 1))
                            inst.then_inc(mmB, 1)

                @block.vector
                def _(v):
                    for tt in range(NT):
                        for half in range(2):
                            v.wait_ge(trA, 2 * tt + half + 1)
                            v.tensor_copy(
                                bass.AP(xT, half * 8 * T + tt * P,
                                        [[ND * T, P], [T, 8], [1, P]]),
                                ptr[half][:, :]).then_inc(evA, 1)
                    v.wait_ge(msc, 3)
                    for i in range(ND):
                        v.wait_ge(evB, 4 * (3 * i + 1) + 4)
                        v.scalar_tensor_tensor(
                            kp[:, :], cbuf[:, :], 1.0, abuf[:, :],
                            op0=ALU.min, op1=ALU.add).then_inc(kst, 1)
                        v.wait_ge(kst, i + 1)
                        v.tensor_tensor(kv[:, :], kp[:, :], vbuf[:, :],
                                        op=ALU.mult).then_inc(vch, 1)
                        v.wait_ge(vch, 5 * i + 1)
                        v.tensor_tensor_scan(KVc[:, :], ones_bcast, kv[:, :],
                                             0.0, op0=ALU.mult,
                                             op1=ALU.add).then_inc(vch, 1)
                        v.tensor_tensor_scan(Kc[:, :], ones_bcast, kp[:, :],
                                             0.0, op0=ALU.mult,
                                             op1=ALU.add).then_inc(vch, 1)
                        v.wait_ge(evB, 4 * (3 * i + 2) + 4)
                        v.scalar_tensor_tensor(
                            qp[:, :], cbuf[:, :], 1.0, abuf[:, :],
                            op0=ALU.min, op1=ALU.add).then_inc(vch, 1)
                        v.wait_ge(vch, 5 * i + 4)
                        v.tensor_tensor(abuf[:, :], qp[:, :], Kc[:, :],
                                        op=ALU.mult).then_inc(dsem, 1)
                        v.tensor_tensor(KVc[:, :], qp[:, :], KVc[:, :],
                                        op=ALU.mult).then_inc(vch, 1)
                        v.wait_ge(recS, i + 1)
                        v.wait_ge(vch, 5 * i + 5)
                        v.tensor_tensor(outT[:, i * T:(i + 1) * T],
                                        KVc[:, :], abuf[:, :],
                                        op=ALU.mult).then_inc(vout, 1)

                @block.scalar
                def _(s_):
                    for i in range(ND):
                        if i >= 1:
                            s_.wait_ge(vout, i)
                        for c in range(NCH):
                            m = 4 * (3 * i) + c
                            s_.wait_ge(mmB, m + 1)
                            s_.activation(vbuf[:, c * CH:(c + 1) * CH],
                                          psA[m % 4][:, :],
                                          AF.Copy).then_inc(evB, 1)
                        for c in range(NCH):
                            m = 4 * (3 * i + 1) + c
                            s_.wait_ge(mmB, m + 1)
                            s_.activation(abuf[:, c * CH:(c + 1) * CH],
                                          psA[m % 4][:, :], AF.Relu)
                            s_.activation(cbuf[:, c * CH:(c + 1) * CH],
                                          psA[m % 4][:, :],
                                          AF.Exp).then_inc(evB, 1)
                        s_.wait_ge(kst, i + 1)
                        for c in range(NCH):
                            m = 4 * (3 * i + 2) + c
                            s_.wait_ge(mmB, m + 1)
                            s_.activation(abuf[:, c * CH:(c + 1) * CH],
                                          psA[m % 4][:, :], AF.Relu)
                            s_.activation(cbuf[:, c * CH:(c + 1) * CH],
                                          psA[m % 4][:, :],
                                          AF.Exp).then_inc(evB, 1)
                        s_.wait_ge(dsem, i + 1)
                        if i == 0:
                            s_.wait_ge(msc, 3)
                        s_.activation(cbuf[:, :], abuf[:, :], AF.Ln,
                                      bias=epsb[:, :]).then_inc(sch, 1)
                        s_.wait_ge(sch, i + 1)
                        s_.activation(abuf[:, :], cbuf[:, :], AF.Exp,
                                      scale=-1.0).then_inc(recS, 1)


        with ExitStack() as sctx:
            sbl = lambda n_, s_, t_: sctx.enter_context(nc.sbuf_tensor(n_, s_, t_))
            psl = lambda n_, s_, t_: sctx.enter_context(nc.psum_tensor(n_, s_, t_))
            sumh = sbl("p1_sumh", [P, 8 * D], F32)
            aT = sbl("p1_aT", [P, 2 * 1024], F32)
            mvb = sbl("p1_mv", [P, 2], F32)
            bn6 = sbl("p1_bn6", [P, 24], F32)
            rstd = sbl("p1_rstd", [P, 1], F32)
            r1t = sbl("p1_r1t", [P, 2 * D], F32)
            psC = [psl(f"p1_psC{i}", [P, CH], F32) for i in range(4)]
            psT = [psl(f"p1_psT{i}", [P, CH], F32) for i in range(2)]

            with nc.Block() as block:
                @block.sync
                def _(sy):
                    sy.wait_ge(vout, ND)
                    for th in range(2):
                        for tq in range(8):
                            sy.dma_start(
                                sumh[:, tq * D:(tq + 1) * D],
                                x[(th * 8 + tq) * P:(th * 8 + tq + 1) * P, :],
                            ).then_inc(dC, 16)
                        for tq in range(8):
                            gidx = th * 8 + tq
                            sy.wait_ge(lnw, gidx + 1)
                            sy.dma_start(
                                R1[gidx * P:(gidx + 1) * P, :],
                                r1t[:, (tq % 2) * D:(tq % 2 + 1) * D],
                            ).then_inc(dR[gidx % 2], 16)
                    sy.wait_ge(dR[0], 16 * 8)
                    sy.wait_ge(dR[1], 16 * 8)

                @block.gpsimd
                def _(g):
                    for th in range(2):
                        for dt in range(ND):
                            s = NSLAB_B + th * ND + dt
                            g.wait_ge(mmB, cb(s - 3))
                            g.dma_start(
                                wbuf[:, (s % 3) * D:(s % 3 + 1) * D],
                                wout[dt, :, :]).then_inc(wld[s % 3], 16)

                @block.tensor
                def _(t):
                    t.wait_ge(vout, ND)
                    t.wait_ge(evB, 4 * NSLAB_B)
                    for th in range(2):
                        for dt in range(ND):
                            s = NSLAB_B + th * ND + dt
                            idx = th * ND + dt
                            t.wait_ge(wld[s % 3], 16 * (s // 3 + 1))
                            for c in range(2):
                                m = 4 * NSLAB_B + 2 * idx + c
                                t.wait_ge(evB, m - 3)
                                for kt in range(ND):
                                    inst = t.matmul(
                                        psC[m % 4][:, :],
                                        wbuf[:, (s % 3) * D + kt * P:
                                             (s % 3) * D + (kt + 1) * P],
                                        outT[:, kt * T + th * 1024 + c * CH:
                                             kt * T + th * 1024 + (c + 1) * CH],
                                        start=(kt == 0), stop=(kt == ND - 1))
                                inst.then_inc(mmB, 1)
                            t.wait_ge(evB, 4 * NSLAB_B + 2 * idx + 2)
                            for pk in range(2):
                                if idx >= 1:
                                    t.wait_ge(addC, 2 * (idx - 1) + pk + 1)
                                for tq4 in range(4):
                                    tq = pk * 4 + tq4
                                    inst = t.transpose(
                                        psT[pk][:, tq4 * P:(tq4 + 1) * P],
                                        aT[:, (dt % 2) * 1024 + tq * P:
                                           (dt % 2) * 1024 + (tq + 1) * P],
                                        eyef[:, :])
                                inst.then_inc(trC, 1)

                @block.scalar
                def _(s_):
                    for th in range(2):
                        for dt in range(ND):
                            idx = th * ND + dt
                            if idx >= 2:
                                s_.wait_ge(trC, 2 * (idx - 1))
                            for c in range(2):
                                m = 4 * NSLAB_B + 2 * idx + c
                                s_.wait_ge(mmB, m + 1)
                                s_.activation(
                                    aT[:, (dt % 2) * 1024 + c * CH:
                                       (dt % 2) * 1024 + (c + 1) * CH],
                                    psC[m % 4][:, :], AF.Copy,
                                    scale=L0_SCALE).then_inc(evB, 1)
                        for tq in range(8):
                            gidx = th * 8 + tq
                            s_.wait_ge(lnv, 2 * (gidx + 1))
                            s_.activation(rstd[:, :], mvb[:, 1:2], AF.Ln,
                                          bias=lnepsb[:, :]).then_inc(lnx, 1)
                            s_.wait_ge(lnx, gidx + 1)
                            s_.activation(rstd[:, :], rstd[:, :], AF.Exp,
                                          scale=-0.5).then_inc(lnr, 1)

                @block.vector
                def _(v):
                    for th in range(2):
                        v.wait_ge(dC, 16 * 8 * (th + 1))
                        for dt in range(ND):
                            idx = th * ND + dt
                            for pk in range(2):
                                v.wait_ge(trC, 2 * idx + pk + 1)
                                dst = bass.AP(sumh, pk * 4 * D + dt * P,
                                              [[8 * D, P], [D, 4], [1, P]])
                                v.tensor_tensor(dst, psT[pk][:, :], dst,
                                                op=ALU.add).then_inc(addC, 1)
                        v.wait_ge(addC, 32 * (th + 1))
                        for tq in range(8):
                            gidx = th * 8 + tq
                            tile = sumh[:, tq * D:(tq + 1) * D]
                            for c4 in range(4):
                                inst = v.bn_stats(
                                    bn6[:, c4 * 6:(c4 + 1) * 6],
                                    tile[:, c4 * CH:(c4 + 1) * CH])
                            inst.then_inc(lnv, 1)
                            v.wait_ge(lnv, 2 * gidx + 1)
                            v.bn_aggr(mvb[:, :], bn6[:, :]).then_inc(lnv, 1)
                            v.wait_ge(lnr, gidx + 1)
                            if gidx >= 2:
                                v.wait_ge(dR[gidx % 2], 16 * (gidx // 2))
                            v.tensor_scalar(
                                r1t[:, (tq % 2) * D:(tq % 2 + 1) * D],
                                tile, mvb[:, 0:1], rstd[:, :],
                                op0=ALU.subtract,
                                op1=ALU.mult).then_inc(lnw, 1)
    return nc



def _clear_sems(nc):
    with nc.Block() as block:
        @block.sync
        def _(sy):
            sy.sem_clear(range(155, 250))



def build_phase2(nc, R1, wgu, wd, eye32, eye16, R2):
    HT = 1024
    NIT = 32

    _clear_sems(nc)

    def base(th, ip):
        return th * 320 + ip * 160

    gu_done = {}
    dn_done = {}
    for th in range(2):
        for ip in range(2):
            for ls in range(64):
                gu_done[th * 128 + ip * 64 + ls] = base(th, ip) + 2 * (ls + 1)
            for dt in range(ND):
                dn_done[th * 32 + ip * 16 + dt] = \
                    base(th, ip) + 128 + 2 * (dt + 1)

    with ExitStack() as ctx:
        sb = lambda n_, s_, t_: ctx.enter_context(nc.sbuf_tensor(n_, s_, t_))
        psl = lambda n_, s_, t_: ctx.enter_context(nc.psum_tensor(n_, s_, t_))
        sem = lambda n_: ctx.enter_context(nc.semaphore(n_))

        h = sb("p2_h", [P, NIT * HT], BF16)
        xT = sb("p2_xT", [P, ND * HT], BF16)
        sumh = sb("p2_sumh", [P, 8 * D], F32)
        wbufG = sb("p2_wbufG", [P, 2 * D], BF16)
        wbufD = sb("p2_wbufD", [P, 2 * 4096], BF16)
        gbuf = sb("p2_gbuf", [P, 2 * HT], BF16)
        ubuf = sb("p2_ubuf", [P, 2 * HT], BF16)
        aT = sb("p2_aT", [P, 2 * HT], F32)
        xtmb = sb("p2_xtmb", [P, D], BF16)
        eyeb = sb("p2_eyeb", [P, P], BF16)
        eyef = sb("p2_eyef", [P, P], F32)
        mvb = sb("p2_mv", [P, 2], F32)
        bn6 = sb("p2_bn6", [P, 24], F32)
        rstd = sb("p2_rstd", [P, 1], F32)
        lnepsb = sb("p2_lneps", [P, 1], F32)
        psA = [psl(f"p2_psA{i}", [P, CH], F32) for i in range(4)]
        ptrA = [psl(f"p2_ptrA{i}", [P, 8 * P], BF16) for i in range(2)]
        psT = [psl(f"p2_psT{i}", [P, CH], F32) for i in range(2)]

        dA = sem("p2_dA")
        trA = sem("p2_trA")
        evA = sem("p2_evA")
        wldG = [sem("p2_wldG0"), sem("p2_wldG1")]
        wldD = [sem("p2_wldD0"), sem("p2_wldD1")]
        mm = sem("p2_mm")
        ev = sem("p2_ev")
        hm = sem("p2_hm")
        dC = sem("p2_dC")
        trC = sem("p2_trC")
        addC = sem("p2_addC")
        lnv = sem("p2_lnv")
        lnr = sem("p2_lnr")
        lnx = sem("p2_lnx")
        lnw = sem("p2_lnw")
        dR = sem("p2_dR")
        msc = sem("p2_msc")

        with nc.Block() as block:
            @block.gpsimd
            def _(g):
                g.memset(lnepsb[:, :], LN_EPS).then_inc(msc, 1)
                g.dma_start(eyeb[:, :], eye16[:, :]).then_inc(dA, 16)
                g.dma_start(eyef[:, :], eye32[:, :]).then_inc(dA, 16)
                for th in range(2):
                    for tl in range(8):
                        tt = th * 8 + tl
                        if tt >= 1:
                            g.wait_ge(trA, 2 * tt)
                        g.dma_start(
                            xtmb[:, :], R1[tt * P:(tt + 1) * P, :],
                        ).then_inc(dA, 16)
                    for ip in range(2):
                        for ls in range(64):
                            gs = th * 128 + ip * 64 + ls
                            if gs >= 2:
                                g.wait_ge(mm, gu_done[gs - 2])
                            g.dma_start(
                                wbufG[:, (gs % 2) * D:(gs % 2 + 1) * D],
                                wgu[ip * 64 + ls, :, :]).then_inc(wldG[gs % 2], 16)
                        for dt in range(ND):
                            ds = th * 32 + ip * 16 + dt
                            if ds >= 2:
                                g.wait_ge(mm, dn_done[ds - 2])
                            g.dma_start(
                                wbufD[:, (ds % 2) * 4096:(ds % 2 + 1) * 4096],
                                wd[ip, dt, :, :]).then_inc(wldD[ds % 2], 16)

            @block.tensor
            def _(t):
                for th in range(2):
                    for tl in range(8):
                        tt = th * 8 + tl
                        t.wait_ge(dA, 32 + 16 * (tt + 1))
                        if tt >= 1:
                            t.wait_ge(evA, 2 * tt)
                        for dt in range(ND):
                            inst = t.transpose(
                                ptrA[dt // 8][:, (dt % 8) * P:(dt % 8 + 1) * P],
                                xtmb[:, dt * P:(dt + 1) * P], eyeb[:, :])
                            if dt % 8 == 7:
                                inst.then_inc(trA, 1)
                    t.wait_ge(evA, 2 * (th * 8 + 8))
                    for ip in range(2):
                        for ls in range(64):
                            gs = th * 128 + ip * 64 + ls
                            t.wait_ge(wldG[gs % 2], 16 * (gs // 2 + 1))
                            for c in range(2):
                                m = base(th, ip) + 2 * ls + c
                                if m >= 4:
                                    t.wait_ge(ev, m - 3)
                                for kt in range(ND):
                                    inst = t.matmul(
                                        psA[m % 4][:, :],
                                        wbufG[:, (gs % 2) * D + kt * P:
                                              (gs % 2) * D + (kt + 1) * P],
                                        xT[:, kt * HT + c * CH:
                                           kt * HT + (c + 1) * CH],
                                        start=(kt == 0), stop=(kt == ND - 1))
                                inst.then_inc(mm, 1)
                        t.wait_ge(hm, 32 * (th * 2 + ip + 1))
                        for dt in range(ND):
                            ds = th * 32 + ip * 16 + dt
                            idxg = th * 32 + ip * 16 + dt
                            t.wait_ge(wldD[ds % 2], 16 * (ds // 2 + 1))
                            for c in range(2):
                                m = base(th, ip) + 128 + 2 * dt + c
                                t.wait_ge(ev, m - 3)
                                for kt in range(NIT):
                                    inst = t.matmul(
                                        psA[m % 4][:, :],
                                        wbufD[:, (ds % 2) * 4096 + kt * P:
                                              (ds % 2) * 4096 + (kt + 1) * P],
                                        h[:, kt * HT + c * CH:
                                          kt * HT + (c + 1) * CH],
                                        start=(kt == 0), stop=(kt == NIT - 1))
                                inst.then_inc(mm, 1)
                            t.wait_ge(ev, base(th, ip) + 128 + 2 * dt + 2)
                            for pk in range(2):
                                if idxg >= 1:
                                    t.wait_ge(addC, 2 * (idxg - 1) + pk + 1)
                                for tq4 in range(4):
                                    tq = pk * 4 + tq4
                                    inst = t.transpose(
                                        psT[pk][:, tq4 * P:(tq4 + 1) * P],
                                        aT[:, (dt % 2) * HT + tq * P:
                                           (dt % 2) * HT + (tq + 1) * P],
                                        eyef[:, :])
                                inst.then_inc(trC, 1)

            @block.scalar
            def _(s_):
                for th in range(2):
                    for ip in range(2):
                        for ls in range(64):
                            il = ls // 2
                            is_g = (ls % 2 == 0)
                            gidx = (th * 2 + ip) * 32 + il
                            if gidx >= 2 and is_g:
                                s_.wait_ge(hm, gidx - 1)
                            for c in range(2):
                                m = base(th, ip) + 2 * ls + c
                                s_.wait_ge(mm, m + 1)
                                dst = (gbuf if is_g else ubuf)
                                s_.activation(
                                    dst[:, (il % 2) * HT + c * CH:
                                        (il % 2) * HT + (c + 1) * CH],
                                    psA[m % 4][:, :],
                                    AF.Silu if is_g else AF.Copy,
                                ).then_inc(ev, 1)
                        for dt in range(ND):
                            idxg = th * 32 + ip * 16 + dt
                            if idxg >= 2:
                                s_.wait_ge(trC, 2 * (idxg - 1))
                            for c in range(2):
                                m = base(th, ip) + 128 + 2 * dt + c
                                s_.wait_ge(mm, m + 1)
                                s_.activation(
                                    aT[:, (dt % 2) * HT + c * CH:
                                       (dt % 2) * HT + (c + 1) * CH],
                                    psA[m % 4][:, :], AF.Copy).then_inc(ev, 1)
                    for tl in range(8):
                        gt = th * 8 + tl
                        s_.wait_ge(lnv, 2 * (gt + 1))
                        if gt == 0:
                            s_.wait_ge(msc, 1)
                        s_.activation(rstd[:, :], mvb[:, 1:2], AF.Ln,
                                      bias=lnepsb[:, :]).then_inc(lnx, 1)
                        s_.wait_ge(lnx, gt + 1)
                        s_.activation(rstd[:, :], rstd[:, :], AF.Exp,
                                      scale=-0.5).then_inc(lnr, 1)

            @block.vector
            def _(v):
                for th in range(2):
                    for tl in range(8):
                        tt = th * 8 + tl
                        for half in range(2):
                            v.wait_ge(trA, 2 * tt + half + 1)
                            if th == 1 and tl == 0 and half == 0:
                                v.wait_ge(mm, base(0, 1) + 128)
                            v.tensor_copy(
                                bass.AP(xT, half * 8 * HT + tl * P,
                                        [[ND * HT, P], [HT, 8], [1, P]]),
                                ptrA[half][:, :]).then_inc(evA, 1)
                    for ip in range(2):
                        for il in range(NIT):
                            gidx = (th * 2 + ip) * 32 + il
                            if il == 0 and gidx >= 32:
                                v.wait_ge(mm, base(th, ip - 1) + 160
                                          if ip == 1 else base(th - 1, 1) + 160)
                            v.wait_ge(ev, base(th, ip) + 4 * il + 4)
                            v.tensor_tensor(
                                h[:, il * HT:(il + 1) * HT],
                                gbuf[:, (il % 2) * HT:(il % 2 + 1) * HT],
                                ubuf[:, (il % 2) * HT:(il % 2 + 1) * HT],
                                op=ALU.mult).then_inc(hm, 1)
                        for dt in range(ND):
                            idxg = th * 32 + ip * 16 + dt
                            if ip == 0 and dt == 0:
                                v.wait_ge(dC, 16 * 8 * (th + 1))
                            for pk in range(2):
                                v.wait_ge(trC, 2 * idxg + pk + 1)
                                dst = bass.AP(sumh, pk * 4 * D + dt * P,
                                              [[8 * D, P], [D, 4], [1, P]])
                                v.tensor_tensor(dst, psT[pk][:, :], dst,
                                                op=ALU.add).then_inc(addC, 1)
                    v.wait_ge(addC, 64 * (th + 1))
                    for tl in range(8):
                        gt = th * 8 + tl
                        tile = sumh[:, tl * D:(tl + 1) * D]
                        for c4 in range(4):
                            inst = v.bn_stats(bn6[:, c4 * 6:(c4 + 1) * 6],
                                              tile[:, c4 * CH:(c4 + 1) * CH])
                        inst.then_inc(lnv, 1)
                        v.wait_ge(lnv, 2 * gt + 1)
                        v.bn_aggr(mvb[:, :], bn6[:, :]).then_inc(lnv, 1)
                        v.wait_ge(lnr, gt + 1)
                        v.tensor_scalar(tile, tile, mvb[:, 0:1], rstd[:, :],
                                        op0=ALU.subtract,
                                        op1=ALU.mult).then_inc(lnw, 1)

            @block.sync
            def _(sy):
                for th in range(2):
                    if th == 1:
                        sy.wait_ge(dR, 16 * 8)
                    for tl in range(8):
                        sy.dma_start(
                            sumh[:, tl * D:(tl + 1) * D],
                            R1[(th * 8 + tl) * P:(th * 8 + tl + 1) * P, :],
                        ).then_inc(dC, 16)
                    for tl in range(8):
                        gt = th * 8 + tl
                        sy.wait_ge(lnw, gt + 1)
                        sy.dma_start(
                            R2[gt * P:(gt + 1) * P, :],
                            sumh[:, tl * D:(tl + 1) * D]).then_inc(dR, 16)
                sy.wait_ge(dR, 16 * NT)
    return nc


def host_prep_p2(w_gate_up, w_down):
    WT = np.asarray(w_gate_up, np.float32).T
    cols = []
    for ip in range(2):
        for il in range(32):
            i = ip * 32 + il
            cols.append(WT[:, i * P:(i + 1) * P])
            cols.append(WT[:, 8192 + i * P:8192 + (i + 1) * P])
    wgu_s = slab_weights(np.concatenate(cols, axis=1), osz=P)
    WdT = np.asarray(w_down, np.float32).T
    wd_s = np.zeros((2, 16, P, 4096), np.float32)
    for ip in range(2):
        sl = slab_weights(WdT[ip * 4096:(ip + 1) * 4096, :], osz=P)
        wd_s[ip] = sl
    return wgu_s, np.ascontiguousarray(wd_s)


def _mk_phase2_nc():
    nc = bass.Bass("TRN2", target_bir_lowering=False)
    R1 = nc.dram_tensor("R1", [T, D], F32, kind="ExternalInput")
    wgu = nc.dram_tensor("wgu", [128, P, D], F32, kind="ExternalInput")
    wd = nc.dram_tensor("wd", [2, 16, P, 4096], F32, kind="ExternalInput")
    eye32 = nc.dram_tensor("eye32", [P, P], F32, kind="ExternalInput")
    eye16 = nc.dram_tensor("eye16", [P, P], BF16, kind="ExternalInput")
    R2 = nc.dram_tensor("R2", [T, D], F32, kind="ExternalOutput")
    build_phase2(nc, R1, wgu, wd, eye32, eye16, R2)
    return nc


def run_phase2(R1_full, w_gate_up, w_down, cores=N_CORES):
    wgu_s, wd_s = host_prep_p2(w_gate_up, w_down)
    nc = _mk_phase2_nc()
    in_maps = []
    for c in range(cores):
        in_maps.append(dict(R1=np.ascontiguousarray(R1_full[c], np.float32),
                            wgu=wgu_s, wd=wd_s,
                            eye32=_EYE32, eye16=_EYE16))
    res = run_bass_kernel_spmd(nc, in_maps, core_ids=list(range(cores)))
    return np.stack([r["R2"] for r in res.results])




def build_phase3(nc, R2, wq, wk, wv, wo, nw, bq, bk, bv, bo, masks,
                 eye32, eye16, R3, upto=3, att="full", d_lvl="full"):
    _clear_sems(nc)
    qs = nc.dram_tensor("p3_qs", [16, P, T], BF16)
    vs = nc.dram_tensor("p3_vs", [16, P, T], BF16)
    os_ = nc.dram_tensor("p3_os", [16, P, D], BF16)
    NB = 48

    with ExitStack() as ctx:
        sb = lambda n_, s_, t_: ctx.enter_context(nc.sbuf_tensor(n_, s_, t_))
        psl = lambda n_, s_, t_: ctx.enter_context(nc.psum_tensor(n_, s_, t_))
        sem = lambda n_: ctx.enter_context(nc.semaphore(n_))

        big = sb("p3_big", [P, ND * T], BF16)
        wbuf = sb("p3_wbuf", [P, 2 * D], BF16)
        qtile = sb("p3_qtile", [P, 2 * T], BF16)
        eyeb = sb("p3_eyeb", [P, P], BF16)
        eyef = sb("p3_eyef", [P, P], F32)
        lnepsb = sb("p3_lneps", [P, 1], F32)
        mvb = sb("p3_mv", [P, 2], F32)
        bn6 = sb("p3_bn6", [P, 24], F32)
        lrstd = sb("p3_lrstd", [P, 1], F32)
        bqb = sb("p3_bqb", [P, 16], F32)
        bkb = sb("p3_bkb", [P, 16], F32)
        bvb = sb("p3_bvb", [P, 16], F32)
        bob = sb("p3_bob", [P, 16], F32)

        psA = [psl(f"p3_psA{i}", [P, CH], F32) for i in range(2)]
        ptrB = [psl(f"p3_ptrB{i}", [P, 8 * P], BF16) for i in range(2)]
        psS = psl("p3_psS", [P, T], F32)

        dA = sem("p3_dA")
        dX = [sem("p3_dX0"), sem("p3_dX1")]
        rm1 = sem("p3_rm1")
        rm1b = sem("p3_rm1b")
        rm2 = sem("p3_rm2")
        xnv = sem("p3_xnv")
        trA = sem("p3_trA")
        evA = sem("p3_evA")
        wld = [sem("p3_wld0"), sem("p3_wld1")]
        mm = sem("p3_mm")
        ev = sem("p3_ev")
        dQ = [sem("p3_dQ0"), sem("p3_dQ1")]
        dVs = [sem("p3_dVs0"), sem("p3_dVs1")]
        dVr = [sem("p3_dVr0"), sem("p3_dVr1")]
        trV = sem("p3_trV")
        evV = sem("p3_evV")
        qld = [sem("p3_qld0"), sem("p3_qld1")]
        mmS = sem("p3_mmS")
        aux1 = sem("p3_aux1")
        sS = sem("p3_sS")
        sE = sem("p3_sE")
        trP = sem("p3_trP")
        evP = sem("p3_evP")
        rc = sem("p3_rc")
        mmO = sem("p3_mmO")
        evO = sem("p3_evO")
        dOs = [sem("p3_dOs0"), sem("p3_dOs1")]
        dOl = [sem("p3_dOl0"), sem("p3_dOl1")]
        trD = sem("p3_trD")
        evD = sem("p3_evD")
        wld2 = [sem("p3_wld20"), sem("p3_wld21")]
        mmD = sem("p3_mmD")
        evE = sem("p3_evE")
        trC = sem("p3_trC")
        addC = sem("p3_addC")
        lnv = sem("p3_lnv")
        lnr = sem("p3_lnr")
        lnx = sem("p3_lnx")
        lnw = sem("p3_lnw")
        dC = sem("p3_dC")
        dR = sem("p3_dR")
        msc = sem("p3_msc")

        with ExitStack() as bctx:
            sbb = lambda n_, s_, t_: bctx.enter_context(nc.sbuf_tensor(n_, s_, t_))
            Kreg = sbb("p3_K", [P, ND * T], BF16)
            vtile = sbb("p3_vtile", [P, 2 * T], BF16)

            with ExitStack() as actx:
                sba = lambda n_, s_, t_: actx.enter_context(
                    nc.sbuf_tensor(n_, s_, t_))
                xtm2 = sba("p3_xtm2", [P, 2 * D], F32)
                junk = sba("p3_junk", [P, D], F32)
                xtmb = sba("p3_xtmb", [P, D], BF16)
                nwb = sba("p3_nwb", [P, D], F32)
                rmsepsb = sba("p3_rmseps", [P, 1], F32)
                ssq = sba("p3_ssq", [P, 1], F32)
                rtmp = sba("p3_rtmp", [P, 1], F32)
                rstd = sba("p3_rstd", [P, 1], F32)

                with nc.Block() as block:
                    @block.sync
                    def _(sy):
                        sy.dma_start(eyef[:, :], eye32[:, :]).then_inc(dA, 16)
                        sy.dma_start(eyeb[:, :], eye16[:, :]).then_inc(dA, 16)
                        sy.dma_start(
                            nwb[:, :],
                            bass.AP(nw, 0, [[0, P], [1, D]])).then_inc(dA, 16)
                        sy.dma_start(bqb[:, :], bq[:, :]).then_inc(dA, 16)
                        sy.dma_start(bkb[:, :], bk[:, :]).then_inc(dA, 16)
                        sy.dma_start(bvb[:, :], bv[:, :]).then_inc(dA, 16)
                        sy.dma_start(bob[:, :], bo[:, :]).then_inc(dA, 16)
                        for tt in range(NT):
                            if tt >= 2:
                                sy.wait_ge(xnv, tt - 1)
                            sy.dma_start(
                                xtm2[:, (tt % 2) * D:(tt % 2 + 1) * D],
                                R2[tt * P:(tt + 1) * P, :],
                            ).then_inc(dX[tt % 2], 16)
                        for dt in range(ND):
                            sy.wait_ge(ev, 4 * dt + 4)
                            sy.dma_start(
                                qs[dt, :, :],
                                qtile[:, (dt % 2) * T:(dt % 2 + 1) * T],
                            ).then_inc(dQ[dt % 2], 16)
                        for dt in range(ND):
                            sy.wait_ge(ev, 4 * (32 + dt) + 4)
                            sy.dma_start(
                                vs[dt, :, :],
                                vtile[:, (dt % 2) * T:(dt % 2 + 1) * T],
                            ).then_inc(dVs[dt % 2], 16)
                        sy.wait_ge(dQ[0], 16 * 8)
                        sy.wait_ge(dQ[1], 16 * 8)
                        sy.wait_ge(dVs[0], 16 * 8)
                        sy.wait_ge(dVs[1], 16 * 8)

                    @block.gpsimd
                    def _(g):
                        g.memset(rmsepsb[:, :], RMS_EPS).then_inc(msc, 1)
                        g.memset(lnepsb[:, :], LN_EPS).then_inc(msc, 1)
                        for grp, wsrc in ((0, wq), (1, wk), (2, wv)):
                            for dt in range(ND):
                                s = grp * ND + dt
                                if s >= 2:
                                    g.wait_ge(mm, 4 * (s - 1))
                                g.dma_start(
                                    wbuf[:, (s % 2) * D:(s % 2 + 1) * D],
                                    wsrc[dt, :, :]).then_inc(wld[s % 2], 16)

                    @block.tensor
                    def _(t):
                        t.wait_ge(dA, 16 * 7)
                        for tt in range(NT):
                            t.wait_ge(xnv, tt + 1)
                            if tt >= 1:
                                t.wait_ge(evA, 2 * tt)
                            for dt in range(ND):
                                inst = t.transpose(
                                    ptrB[dt // 8][:, (dt % 8) * P:
                                                  (dt % 8 + 1) * P],
                                    xtmb[:, dt * P:(dt + 1) * P], eyeb[:, :])
                                if dt % 8 == 7:
                                    inst.then_inc(trA, 1)
                        t.wait_ge(evA, 2 * NT)
                        for s in range(NB):
                            t.wait_ge(wld[s % 2], 16 * (s // 2 + 1))
                            for c in range(NCH):
                                m = 4 * s + c
                                if m >= 2:
                                    t.wait_ge(ev, m - 1)
                                for kt in range(ND):
                                    inst = t.matmul(
                                        psA[m % 2][:, :],
                                        wbuf[:, (s % 2) * D + kt * P:
                                             (s % 2) * D + (kt + 1) * P],
                                        big[:, kt * T + c * CH:
                                            kt * T + (c + 1) * CH],
                                        start=(kt == 0), stop=(kt == ND - 1))
                                inst.then_inc(mm, 1)

                    @block.scalar
                    def _(s_):
                        s_.wait_ge(msc, 2)
                        for tt in range(NT):
                            s_.wait_ge(dX[tt % 2], 16 * (tt // 2 + 1))
                            if tt >= 1:
                                s_.wait_ge(xnv, tt)
                            s_.activation(
                                junk[:, :],
                                xtm2[:, (tt % 2) * D:(tt % 2 + 1) * D],
                                AF.Square,
                                accum_out=ssq[:, :]).then_inc(rm1, 1)
                            s_.wait_ge(rm1, tt + 1)
                            s_.activation(rtmp[:, :], ssq[:, :], AF.Ln,
                                          bias=rmsepsb[:, :],
                                          scale=1.0 / D).then_inc(rm1b, 1)
                            s_.wait_ge(rm1b, tt + 1)
                            s_.activation(rstd[:, :], rtmp[:, :], AF.Exp,
                                          scale=-0.5).then_inc(rm2, 1)
                        for grp in range(3):
                            for dt in range(ND):
                                s = grp * ND + dt
                                if grp == 0 and dt >= 2:
                                    s_.wait_ge(dQ[dt % 2], 16 * (dt // 2))
                                if grp == 2 and dt >= 2:
                                    s_.wait_ge(dVs[dt % 2], 16 * (dt // 2))
                                bias = (bqb, bkb, bvb)[grp]
                                for c in range(NCH):
                                    m = 4 * s + c
                                    s_.wait_ge(mm, m + 1)
                                    if grp == 0:
                                        dst = qtile[:, (dt % 2) * T + c * CH:
                                                    (dt % 2) * T + (c + 1) * CH]
                                    elif grp == 1:
                                        dst = Kreg[:, dt * T + c * CH:
                                                   dt * T + (c + 1) * CH]
                                    else:
                                        dst = vtile[:, (dt % 2) * T + c * CH:
                                                    (dt % 2) * T + (c + 1) * CH]
                                    s_.activation(
                                        dst, psA[m % 2][:, :], AF.Identity,
                                        bias=bias[:, dt:dt + 1]).then_inc(ev, 1)

                    @block.vector
                    def _(v):
                        v.wait_ge(dA, 16 * 7)
                        for tt in range(NT):
                            v.wait_ge(rm2, tt + 1)
                            if tt >= 1:
                                v.wait_ge(trA, 2 * tt)
                            v.scalar_tensor_tensor(
                                xtmb[:, :],
                                xtm2[:, (tt % 2) * D:(tt % 2 + 1) * D],
                                rstd[:, :], nwb[:, :],
                                op0=ALU.mult, op1=ALU.mult).then_inc(xnv, 1)
                            for half in range(2):
                                v.wait_ge(trA, 2 * tt + half + 1)
                                v.tensor_copy(
                                    bass.AP(big, half * 8 * T + tt * P,
                                            [[ND * T, P], [T, 8], [1, P]]),
                                    ptrB[half][:, :]).then_inc(evA, 1)

            if upto < 2:
                return nc
            Psb = sbb("p3_Psb", [P, T], BF16)
            PTsb = sbb("p3_PTsb", [P, T], BF16)
            qblk = sbb("p3_qblk", [P, 2 * T], BF16)
            Otile = sbb("p3_Otile", [P, 2 * D], BF16)
            maskb = sbb("p3_maskb", [P, 4 * CH], F32)
            rm = sbb("p3_rm", [P, 1], F32)
            nm = sbb("p3_nm", [P, 1], F32)
            lsum = sbb("p3_lsum", [P, 1], F32)
            linv = sbb("p3_linv", [P, 1], F32)
            psO = psA[0][:, 0:P]

            with nc.Block() as block:
                @block.sync
                def _(sy):
                    for v_ in range(4):
                        sy.dma_start(maskb[:, v_ * CH:(v_ + 1) * CH],
                                     masks[v_, :, :]).then_inc(dA, 16)
                    for dt in range(ND):
                        if dt >= 2:
                            sy.wait_ge(trV, 2 * (dt - 1))
                        sy.dma_start(
                            vtile[:, (dt % 2) * T:(dt % 2 + 1) * T],
                            vs[dt, :, :]).then_inc(dVr[dt % 2], 16)
                    if att != "none":
                        for qb in range(NT):
                            if qb >= 2:
                                sy.wait_ge(mmS, 16 * (qb - 1))
                            sy.dma_start(
                                qblk[:, (qb % 2) * T:(qb % 2 + 1) * T],
                                bass.AP(qs, qb * P,
                                        [[T, P], [P * T, 16], [1, P]]),
                            ).then_inc(qld[qb % 2], 16)
                    if att == "full":
                        sy.wait_ge(dOs[0], 16 * 8)
                        sy.wait_ge(dOs[1], 16 * 8)

                @block.tensor
                def _(t):
                    t.wait_ge(dA, 16 * 11)
                    for dt in range(ND):
                        t.wait_ge(dVr[dt % 2], 16 * (dt // 2 + 1))
                        if dt >= 1:
                            t.wait_ge(evV, 2 * dt)
                        for tt in range(NT):
                            inst = t.transpose(
                                ptrB[tt // 8][:, (tt % 8) * P:(tt % 8 + 1) * P],
                                vtile[:, (dt % 2) * T + tt * P:
                                      (dt % 2) * T + (tt + 1) * P],
                                eyeb[:, :])
                            if tt % 8 == 7:
                                inst.then_inc(trV, 1)
                    for qb in (range(NT) if att != "none" else []):
                        t.wait_ge(qld[qb % 2], 16 * (qb // 2 + 1))
                        if qb == 0:
                            t.wait_ge(evV, 2 * NT)
                        nchk = qb // 4 + 1
                        row = (qb + 1) * P
                        for h in range(ND):
                            gidx = qb * ND + h
                            t.wait_ge(sE, gidx)
                            for c in range(nchk):
                                w = min(CH, row - c * CH)
                                inst = t.matmul(
                                    psS[:, c * CH:c * CH + w],
                                    qblk[:, (qb % 2) * T + h * P:
                                         (qb % 2) * T + (h + 1) * P],
                                    Kreg[:, h * T + c * CH:h * T + c * CH + w],
                                    start=True, stop=True)
                            inst.then_inc(mmS, 1)
                            if att == "s":
                                continue
                            t.wait_ge(sE, gidx + 1)
                            t.wait_ge(evP, gidx)
                            for j in range(qb + 1):
                                inst = t.transpose(
                                    ptrB[j // 8][:, (j % 8) * P:(j % 8 + 1) * P],
                                    Psb[:, j * P:(j + 1) * P], eyeb[:, :])
                            inst.then_inc(trP, 1)
                            if att == "pt":
                                continue
                            t.wait_ge(evP, gidx + 1)
                            if att == "full":
                                t.wait_ge(evO, gidx)
                            for j in range(qb + 1):
                                inst = t.matmul(
                                    psO,
                                    PTsb[:, j * P:(j + 1) * P],
                                    big[:, j * T + h * P:j * T + (h + 1) * P],
                                    start=(j == 0), stop=(j == qb))
                            inst.then_inc(mmO, 1)

                @block.scalar
                def _(s_):
                    for qb in (range(NT) if att != "none" else []):
                        row = (qb + 1) * P
                        for h in range(ND):
                            gidx = qb * ND + h
                            s_.wait_ge(sS, gidx + 1)
                            if att == "full":
                                s_.wait_ge(rc, gidx)
                            s_.activation(Psb[:, 0:row], psS[:, 0:row],
                                          AF.Exp, bias=nm[:, :],
                                          scale=QK_SCALE,
                                          accum_out=lsum[:, :]).then_inc(sE, 1)
                            if att == "s":
                                continue
                            s_.wait_ge(trP, gidx + 1)
                            nbk = 1 if qb < 8 else 2
                            for bk_ in range(nbk):
                                wpt = min(8 * P, row - bk_ * 8 * P)
                                inst = s_.activation(
                                    PTsb[:, bk_ * 8 * P:bk_ * 8 * P + wpt],
                                    ptrB[bk_][:, 0:wpt], AF.Copy)
                            inst.then_inc(evP, 1)
                            if att in ("pt", "pv"):
                                continue
                            s_.wait_ge(mmO, gidx + 1)
                            s_.wait_ge(rc, gidx + 1)
                            if h == 0 and qb >= 2:
                                s_.wait_ge(dOs[qb % 2], 16 * (qb // 2))
                            s_.activation(
                                Otile[:, (qb % 2) * D + h * P:
                                      (qb % 2) * D + (h + 1) * P],
                                psO, AF.Identity,
                                scale=linv[:, :]).then_inc(evO, 1)
                            if h == ND - 1:
                                s_.wait_ge(evO, 16 * (qb + 1))
                                s_.dma_start(
                                    os_[qb, :, :],
                                    Otile[:, (qb % 2) * D:(qb % 2 + 1) * D],
                                ).then_inc(dOs[qb % 2], 16)

                @block.vector
                def _(v):
                    for dt in range(ND):
                        for half in range(2):
                            v.wait_ge(trV, 2 * dt + half + 1)
                            v.tensor_copy(
                                bass.AP(big, half * 8 * T + dt * P,
                                        [[ND * T, P], [T, 8], [1, P]]),
                                ptrB[half][:, :]).then_inc(evV, 1)
                    for qb in (range(NT) if att != "none" else []):
                        ci = qb // 4
                        vvar = qb % 4
                        wd_ = (vvar + 1) * P
                        row = (qb + 1) * P
                        for h in range(ND):
                            gidx = qb * ND + h
                            v.wait_ge(mmS, gidx + 1)
                            dst = psS[:, ci * CH:ci * CH + wd_]
                            v.tensor_tensor(
                                dst, dst,
                                maskb[:, vvar * CH:vvar * CH + wd_],
                                op=ALU.add).then_inc(aux1, 1)
                            v.wait_ge(aux1, 2 * gidx + 1)
                            v.reduce_max(rm[:, :], psS[:, 0:row], axis=AX.X,
                                         negate=True).then_inc(aux1, 1)
                            v.wait_ge(aux1, 2 * gidx + 2)
                            v.tensor_scalar(nm[:, :], rm[:, :], QK_SCALE,
                                            None, op0=ALU.mult).then_inc(sS, 1)
                            if att != "full":
                                continue
                            v.wait_ge(sE, gidx + 1)
                            if gidx >= 1:
                                v.wait_ge(evO, gidx)
                            v.reciprocal(linv[:, :],
                                         lsum[:, :]).then_inc(rc, 1)

        if upto < 3:
            return nc
        with ExitStack() as dctx:
            sbd = lambda n_, s_, t_: dctx.enter_context(nc.sbuf_tensor(n_, s_, t_))
            sumh = sbd("p3_sumh", [P, 8 * D], F32)
            aT = sbd("p3_aT", [P, 2 * 1024], F32)
            psD = psA
            psT = [psS[:, 0:CH], psS[:, CH:2 * CH]]

            with nc.Block() as block:
                @block.sync
                def _(sy):
                    for tt in range(NT):
                        if tt >= 2:
                            sy.wait_ge(trD, 2 * (tt - 1))
                        sy.dma_start(
                            qtile[:, (tt % 2) * T:(tt % 2 + 1) * T],
                            os_[tt, :, :]).then_inc(dOl[tt % 2], 16)
                    if d_lvl == "full":
                        for th in range(2):
                            if th == 1:
                                sy.wait_ge(dR, 16 * 8)
                            for tl in range(8):
                                sy.dma_start(
                                    sumh[:, tl * D:(tl + 1) * D],
                                    R2[(th * 8 + tl) * P:
                                       (th * 8 + tl + 1) * P, :],
                                ).then_inc(dC, 16)
                            for tl in range(8):
                                gt = th * 8 + tl
                                sy.wait_ge(lnw, gt + 1)
                                sy.dma_start(
                                    R3[gt * P:(gt + 1) * P, :],
                                    sumh[:, tl * D:(tl + 1) * D],
                                ).then_inc(dR, 16)
                        sy.wait_ge(dR, 16 * NT)

                @block.gpsimd
                def _(g):
                    for th in (range(2) if d_lvl != "ot" else []):
                        for dt in range(ND):
                            s2 = th * ND + dt
                            if s2 >= 2:
                                g.wait_ge(mmD, 2 * (s2 - 1))
                            g.dma_start(
                                wbuf[:, (s2 % 2) * D:(s2 % 2 + 1) * D],
                                wo[dt, :, :]).then_inc(wld2[s2 % 2], 16)

                @block.tensor
                def _(t):
                    for tt in range(NT):
                        t.wait_ge(dOl[tt % 2], 16 * (tt // 2 + 1))
                        if tt >= 1:
                            t.wait_ge(evD, 2 * tt)
                        for dt in range(ND):
                            inst = t.transpose(
                                ptrB[dt // 8][:, (dt % 8) * P:(dt % 8 + 1) * P],
                                qtile[:, (tt % 2) * T + dt * P:
                                      (tt % 2) * T + (dt + 1) * P],
                                eyeb[:, :])
                            if dt % 8 == 7:
                                inst.then_inc(trD, 1)
                    if d_lvl != "ot":
                        t.wait_ge(evD, 2 * NT)
                        for th in range(2):
                            for dt in range(ND):
                                s2 = th * ND + dt
                                t.wait_ge(wld2[s2 % 2], 16 * (s2 // 2 + 1))
                                for c in range(2):
                                    m = 2 * s2 + c
                                    if m >= 2:
                                        t.wait_ge(evE, m - 1)
                                    for kt in range(ND):
                                        inst = t.matmul(
                                            psD[m % 2][:, :],
                                            wbuf[:, (s2 % 2) * D + kt * P:
                                                 (s2 % 2) * D + (kt + 1) * P],
                                            big[:, kt * T + th * 1024 + c * CH:
                                                kt * T + th * 1024 +
                                                (c + 1) * CH],
                                            start=(kt == 0),
                                            stop=(kt == ND - 1))
                                    inst.then_inc(mmD, 1)
                                if d_lvl == "mm":
                                    continue
                                t.wait_ge(evE, 2 * s2 + 2)
                                for pk in range(2):
                                    if s2 >= 1:
                                        t.wait_ge(addC, 2 * (s2 - 1) + pk + 1)
                                    for tq4 in range(4):
                                        tq = pk * 4 + tq4
                                        inst = t.transpose(
                                            psT[pk][:, tq4 * P:(tq4 + 1) * P],
                                            aT[:, (dt % 2) * 1024 + tq * P:
                                               (dt % 2) * 1024 + (tq + 1) * P],
                                            eyef[:, :])
                                    inst.then_inc(trC, 1)

                @block.scalar
                def _(s_):
                    for th in (range(2) if d_lvl != "ot" else []):
                        for dt in range(ND):
                            s2 = th * ND + dt
                            if s2 >= 2 and d_lvl == "full":
                                s_.wait_ge(trC, 2 * (s2 - 1))
                            for c in range(2):
                                m = 2 * s2 + c
                                s_.wait_ge(mmD, m + 1)
                                s_.activation(
                                    aT[:, (dt % 2) * 1024 + c * CH:
                                       (dt % 2) * 1024 + (c + 1) * CH],
                                    psD[m % 2][:, :], AF.Identity,
                                    bias=bob[:, dt:dt + 1]).then_inc(evE, 1)
                        if d_lvl == "mm":
                            continue
                        for tl in range(8):
                            gt = th * 8 + tl
                            s_.wait_ge(lnv, 2 * (gt + 1))
                            s_.activation(lrstd[:, :], mvb[:, 1:2], AF.Ln,
                                          bias=lnepsb[:, :]).then_inc(lnx, 1)
                            s_.wait_ge(lnx, gt + 1)
                            s_.activation(lrstd[:, :], lrstd[:, :], AF.Exp,
                                          scale=-0.5).then_inc(lnr, 1)

                @block.vector
                def _(v):
                    for tt in range(NT):
                        for half in range(2):
                            v.wait_ge(trD, 2 * tt + half + 1)
                            v.tensor_copy(
                                bass.AP(big, half * 8 * T + tt * P,
                                        [[ND * T, P], [T, 8], [1, P]]),
                                ptrB[half][:, :]).then_inc(evD, 1)
                    for th in (range(2) if d_lvl == "full" else []):
                        v.wait_ge(dC, 16 * 8 * (th + 1))
                        for dt in range(ND):
                            s2 = th * ND + dt
                            for pk in range(2):
                                v.wait_ge(trC, 2 * s2 + pk + 1)
                                dst = bass.AP(sumh, pk * 4 * D + dt * P,
                                              [[8 * D, P], [D, 4], [1, P]])
                                v.tensor_tensor(dst, psT[pk][:, :], dst,
                                                op=ALU.add).then_inc(addC, 1)
                        v.wait_ge(addC, 32 * (th + 1))
                        for tl in range(8):
                            gt = th * 8 + tl
                            tile = sumh[:, tl * D:(tl + 1) * D]
                            for c4 in range(4):
                                inst = v.bn_stats(
                                    bn6[:, c4 * 6:(c4 + 1) * 6],
                                    tile[:, c4 * CH:(c4 + 1) * CH])
                            inst.then_inc(lnv, 1)
                            v.wait_ge(lnv, 2 * gt + 1)
                            v.bn_aggr(mvb[:, :], bn6[:, :]).then_inc(lnv, 1)
                            v.wait_ge(lnr, gt + 1)
                            v.tensor_scalar(tile, tile, mvb[:, 0:1],
                                            lrstd[:, :], op0=ALU.subtract,
                                            op1=ALU.mult).then_inc(lnw, 1)
    return nc


def host_prep_p3(wq, wk, wv, wo, bq, bk, bv, bo):
    sl = lambda W: slab_weights(np.asarray(W, np.float32).T, osz=P)
    rb = lambda b: np.ascontiguousarray(
        np.asarray(b, np.float32).reshape(16, P).T)
    masks = np.zeros((4, P, CH), np.float32)
    for v_ in range(4):
        r = np.arange(P)[:, None]
        c = np.arange(CH)[None, :]
        masks[v_] = np.where(c <= v_ * P + r, 0.0, -1e30)
    return (sl(wq), sl(wk), sl(wv), sl(wo),
            rb(bq), rb(bk), rb(bv), rb(bo), masks)


def _mk_phase3_nc(upto=3, att="full", d_lvl="full"):
    nc = bass.Bass("TRN2", target_bir_lowering=False)
    R2 = nc.dram_tensor("R2", [T, D], F32, kind="ExternalInput")
    wq = nc.dram_tensor("wq", [16, P, D], F32, kind="ExternalInput")
    wk = nc.dram_tensor("wk", [16, P, D], F32, kind="ExternalInput")
    wv = nc.dram_tensor("wv", [16, P, D], F32, kind="ExternalInput")
    wo = nc.dram_tensor("wo", [16, P, D], F32, kind="ExternalInput")
    nw = nc.dram_tensor("nw", [D], F32, kind="ExternalInput")
    bq = nc.dram_tensor("bq", [P, 16], F32, kind="ExternalInput")
    bk = nc.dram_tensor("bk", [P, 16], F32, kind="ExternalInput")
    bv = nc.dram_tensor("bv", [P, 16], F32, kind="ExternalInput")
    bo = nc.dram_tensor("bo", [P, 16], F32, kind="ExternalInput")
    masks = nc.dram_tensor("masks", [4, P, CH], F32, kind="ExternalInput")
    eye32 = nc.dram_tensor("eye32", [P, P], F32, kind="ExternalInput")
    eye16 = nc.dram_tensor("eye16", [P, P], BF16, kind="ExternalInput")
    R3 = nc.dram_tensor("R3", [T, D], F32, kind="ExternalOutput")
    build_phase3(nc, R2, wq, wk, wv, wo, nw, bq, bk, bv, bo, masks,
                 eye32, eye16, R3, upto=upto, att=att, d_lvl=d_lvl)
    return nc


def run_phase3(R2_full, norm_w, wq, bq, wk, bk, wv, bv, wo, bo,
               cores=N_CORES, upto=3, att="full", d_lvl="full"):
    wqs, wks, wvs, wos, bqb, bkb, bvb, bob, masks = host_prep_p3(
        wq, wk, wv, wo, bq, bk, bv, bo)
    nc = _mk_phase3_nc(upto=upto, att=att, d_lvl=d_lvl)
    in_maps = []
    for c in range(cores):
        in_maps.append(dict(
            R2=np.ascontiguousarray(R2_full[c], np.float32),
            wq=wqs, wk=wks, wv=wvs, wo=wos,
            nw=np.ascontiguousarray(norm_w, np.float32),
            bq=bqb, bk=bkb, bv=bvb, bo=bob, masks=masks,
            eye32=_EYE32, eye16=_EYE16))
    res = run_bass_kernel_spmd(nc, in_maps, core_ids=list(range(cores)))
    return np.stack([r["R3"] for r in res.results])




def build_phase4(nc, R3, w1t, w2t, b1, b2, eye32, eye16, OUT):
    _clear_sems(nc)
    w1q = nc.dram_tensor("p4_w1q", [64, P, 2048], BF16)
    w2q = nc.dram_tensor("p4_w2q", [2, 16, P, 4096], BF16)
    HT = 1024
    NIT = 32
    NP1 = 64
    NPT = 128
    NW1 = 16777216.0
    NW2 = 16777216.0

    def w1_chunk(th, ip, ls, c):
        return th * 192 + ip * 96 + 2 * ls + c

    def w2_chunk(th, ip, dt, c):
        return th * 192 + ip * 96 + 64 + 2 * dt + c

    with ExitStack() as ctx:
        sb = lambda n_, s_, t_: ctx.enter_context(nc.sbuf_tensor(n_, s_, t_))
        psl = lambda n_, s_, t_: ctx.enter_context(nc.psum_tensor(n_, s_, t_))
        sem = lambda n_: ctx.enter_context(nc.semaphore(n_))

        eyeb = sb("p4_eyeb", [P, P], BF16)
        eyef = sb("p4_eyef", [P, P], F32)
        onesf = sb("p4_onesf", [P, 1], F32)
        onesr = sb("p4_onesr", [1, P], F32)
        mvb = sb("p4_mv", [P, 2], F32)
        bn6 = sb("p4_bn6", [P, 24], F32)
        rstd = sb("p4_rstd", [P, 1], F32)
        lnepsb = sb("p4_lneps", [P, 1], F32)
        b1b = sb("p4_b1b", [P, 64], F32)
        b2b = sb("p4_b2b", [P, 16], F32)
        th1 = sb("p4_th1", [P, 2], F32)
        th2 = sb("p4_th2", [P, 2], F32)
        psA = [psl(f"p4_psA{i}", [P, CH], F32) for i in range(4)]
        ptr = [psl(f"p4_ptr{i}", [P, 8 * P], BF16) for i in range(2)]
        psT = [psl(f"p4_psT{i}", [P, CH], F32) for i in range(2)]

        dA = sem("p4_dA")
        dAx = sem("p4_dAx")
        dS = [sem("p4_dS0"), sem("p4_dS1")]
        qsc = sem("p4_qsc")
        qv = sem("p4_qv")
        qg = sem("p4_qg")
        dQw = [sem("p4_dQw0"), sem("p4_dQw1")]
        thr = sem("p4_thr")
        trA = sem("p4_trA")
        evA = sem("p4_evA")
        wld1 = [sem("p4_wld10"), sem("p4_wld11")]
        wld2 = [sem("p4_wld20"), sem("p4_wld21")]
        mm = sem("p4_mm")
        ev = sem("p4_ev")
        trC = sem("p4_trC")
        addC = sem("p4_addC")
        lnv = sem("p4_lnv")
        lnr = sem("p4_lnr")
        lnx = sem("p4_lnx")
        lnw = sem("p4_lnw")
        dC = sem("p4_dC")
        dR = sem("p4_dR")
        msc = sem("p4_msc")
        qv2 = sem("p4_qv2")
        dS2a = [sem("p4_dS2a0"), sem("p4_dS2a1")]
        dQ2 = [sem("p4_dQ20"), sem("p4_dQ21")]
        HH = 1024
        w2h_src = lambda j: w2t[j // 64, (j % 64) // 4, :,
                                (j % 4) * HH:(j % 4 + 1) * HH]
        w2h_dst = lambda j: w2q[j // 64, (j % 64) // 4, :,
                                (j % 4) * HH:(j % 4 + 1) * HH]

        def src_ap(i):
            if i < NP1:
                return w1t[i, :, :]
            j = i - NP1
            ip, dt, hf = j // 32, (j % 32) // 2, j % 2
            return w2t[ip, dt, :, hf * D:(hf + 1) * D]

        def dst_ap(i):
            if i < NP1:
                return w1q[i, :, :]
            j = i - NP1
            ip, dt, hf = j // 32, (j % 32) // 2, j % 2
            return w2q[ip, dt, :, hf * D:(hf + 1) * D]

        with ExitStack() as qctx:
            sbq = lambda n_, s_, t_: qctx.enter_context(nc.sbuf_tensor(n_, s_, t_))
            strip = sbq("p4_strip", [P, 8 * D], F32)
            qb_ = sbq("p4_qb", [P, 2 * D], F32)
            qq = sbq("p4_qq", [P, 4 * D], BF16)
            pb = sbq("p4_pb", [P, NPT], F32)
            red = sbq("p4_red", [P, 2], F32)
            tpq = sbq("p4_tpq", [1, 2], F32)
            dS8 = [sem(f"p4_dS8{k}") for k in range(8)]
            dQ4 = [sem(f"p4_dQ4{k}") for k in range(4)]
            ssl = lambda i: strip[:, (i % 8) * D:(i % 8 + 1) * D]
            qsl = lambda i: qq[:, (i % 4) * D:(i % 4 + 1) * D]

            with nc.Block() as block:
                @block.sync
                def _(sy):
                    sy.dma_start(eyef[:, :], eye32[:, :]).then_inc(dA, 16)
                    sy.dma_start(eyeb[:, :], eye16[:, :]).then_inc(dA, 16)
                    sy.dma_start(b1b[:, :], b1[:, :]).then_inc(dA, 16)
                    sy.dma_start(b2b[:, :], b2[:, :]).then_inc(dA, 16)
                    for i in range(0, NPT, 2):
                        if i >= 8:
                            sy.wait_ge(qsc, i - 7)
                        sy.dma_start(ssl(i), src_ap(i)).then_inc(dS8[i % 8], 16)
                    for i in range(NP1):
                        if i % 2 == 0:
                            if i < 8:
                                sy.wait_ge(qsc, NPT)
                            else:
                                sy.wait_ge(qv, 2 * (i - 8) + 2)
                            sy.dma_start(ssl(i),
                                         src_ap(i)).then_inc(dS8[i % 8], 16)
                        j = i - 3
                        if j >= 1 and j % 2 == 1:
                            sy.wait_ge(qv, 2 * j + 2)
                            sy.dma_start(dst_ap(j),
                                         qsl(j)).then_inc(dQ4[j % 4], 16)
                    for j in (61, 63):
                        sy.wait_ge(qv, 2 * j + 2)
                        sy.dma_start(dst_ap(j),
                                     qsl(j)).then_inc(dQ4[j % 4], 16)
                    sy.wait_ge(dQ4[1], 16 * 16)
                    sy.wait_ge(dQ4[3], 16 * 16)

                @block.gpsimd
                def _(g):
                    g.memset(lnepsb[:, :], LN_EPS).then_inc(msc, 1)
                    g.memset(onesf[:, :], 1.0).then_inc(msc, 1)
                    g.memset(onesr[:, :], 1.0).then_inc(msc, 1)

                @block.scalar
                def _(s_):
                    for i in (1, 3, 5, 7):
                        s_.dma_start(ssl(i), src_ap(i)).then_inc(dS8[i], 16)
                    for i in range(NPT):
                        s_.wait_ge(dS8[i % 8], 16 * (i // 8 + 1))
                        s_.activation(
                            ssl(i), ssl(i), AF.Abs,
                            accum_out=pb[:, i:i + 1]).then_inc(qsc, 1)
                        j = i + 8
                        if j < NPT and j % 2 == 1:
                            s_.wait_ge(qsc, i + 1)
                            s_.dma_start(ssl(j),
                                         src_ap(j)).then_inc(dS8[j % 8], 16)
                    s_.wait_ge(thr, 3)
                    s_.activation(tpq[:, :], psT[0][0:1, 0:2], AF.Copy,
                                  scale=0.7 / NW1).then_inc(thr, 1)
                    s_.wait_ge(thr, 5)
                    s_.activation(th1[:, 0:1], psT[1][:, 0:1], AF.Copy)
                    s_.activation(th2[:, 0:1], psT[1][:, 1:2], AF.Copy)
                    s_.activation(th1[:, 1:2], psT[1][:, 0:1], AF.Copy,
                                  scale=-1.0)
                    s_.activation(th2[:, 1:2], psT[1][:, 1:2], AF.Copy,
                                  scale=-1.0).then_inc(thr, 1)
                    for i in range(NP1):
                        if i % 2 == 1:
                            if i < 8:
                                s_.wait_ge(qsc, NPT)
                            else:
                                s_.wait_ge(qv, 2 * (i - 8) + 2)
                            s_.dma_start(ssl(i),
                                         src_ap(i)).then_inc(dS8[i % 8], 16)
                        j = i - 3
                        if j >= 0 and j % 2 == 0:
                            s_.wait_ge(qv, 2 * j + 2)
                            s_.dma_start(dst_ap(j),
                                         qsl(j)).then_inc(dQ4[j % 4], 16)
                    for j in (62,):
                        s_.wait_ge(qv, 2 * j + 2)
                        s_.dma_start(dst_ap(j),
                                     qsl(j)).then_inc(dQ4[j % 4], 16)
                    s_.wait_ge(dQ4[0], 16 * 16)
                    s_.wait_ge(dQ4[2], 16 * 16)

                @block.vector
                def _(v):
                    v.wait_ge(qsc, NP1)
                    v.reduce_sum(red[:, 0:1], pb[:, 0:NP1],
                                 axis=AX.X).then_inc(thr, 1)
                    v.wait_ge(qsc, NPT)
                    v.reduce_sum(red[:, 1:2], pb[:, NP1:NPT],
                                 axis=AX.X).then_inc(thr, 1)
                    v.wait_ge(thr, 6)
                    for i in range(NP1):
                        tcol = th1
                        bsl = qb_[:, (i % 2) * D:(i % 2 + 1) * D]
                        v.wait_ge(dS8[i % 8], 16 * (16 + i // 8 + 1))
                        if i >= 4:
                            v.wait_ge(dQ4[i % 4], 16 * (i // 4))
                        if i >= 2:
                            v.wait_ge(qv, 2 * i - 2)
                        v.tensor_scalar(
                            bsl, ssl(i), tcol[:, 1:2], None,
                            op0=ALU.is_le).then_inc(qv, 1)
                        v.wait_ge(qv, 2 * i + 1)
                        v.scalar_tensor_tensor(
                            qsl(i), ssl(i), tcol[:, 0:1], bsl,
                            op0=ALU.is_ge,
                            op1=ALU.subtract).then_inc(qv, 1)

                @block.tensor
                def _(t):
                    t.wait_ge(dA, 64)
                    t.wait_ge(msc, 3)
                    t.wait_ge(thr, 1)
                    t.matmul(psT[0][0:1, 0:1], onesf[:, :], red[:, 0:1],
                             start=True, stop=True)
                    t.wait_ge(thr, 2)
                    t.matmul(psT[0][0:1, 1:2], onesf[:, :], red[:, 1:2],
                             start=True, stop=True).then_inc(thr, 1)
                    t.wait_ge(thr, 4)
                    t.matmul(psT[1][:, 0:2], onesr[:, :], tpq[:, :],
                             start=True, stop=True).then_inc(thr, 1)

        with ExitStack() as mctx:
            sbm = lambda n_, s_, t_: mctx.enter_context(nc.sbuf_tensor(n_, s_, t_))
            h = sbm("p4_h", [P, NIT * HT], BF16)
            xT = sbm("p4_xT", [P, ND * HT], BF16)
            sumh = sbm("p4_sumh", [P, 8 * D], F32)
            wbuf1 = sbm("p4_wbuf1", [P, 2 * D], BF16)
            wbuf2 = sbm("p4_wbuf2", [P, 2 * 4096], BF16)
            aT = sbm("p4_aT", [P, 2 * HT], F32)
            xtmb = sbm("p4_xtmb", [P, D], BF16)
            strip2 = sbm("p4_strip2", [P, 2 * HH], F32)
            qq2 = sbm("p4_qq2", [P, HH], BF16)

            with nc.Block() as block:
                @block.sync
                def _(sy):
                    for th in range(2):
                        if th == 1:
                            sy.wait_ge(dR, 16 * 8)
                        for tl in range(8):
                            sy.dma_start(
                                sumh[:, tl * D:(tl + 1) * D],
                                R3[(th * 8 + tl) * P:
                                   (th * 8 + tl + 1) * P, :],
                            ).then_inc(dC, 16)
                        for ip in range(2):
                            for ls in range(NIT):
                                gs = (th * 2 + ip) * NIT + ls
                                if gs >= 2:
                                    g2 = gs - 2
                                    sy.wait_ge(mm, w1_chunk(
                                        g2 // 64, (g2 % 64) // 32,
                                        g2 % 32, 1) + 1)
                                it = ip * NIT + ls
                                sy.dma_start(
                                    wbuf1[:, (gs % 2) * D:(gs % 2 + 1) * D],
                                    w1q[it, :, :],
                                ).then_inc(wld1[gs % 2], 16)
                                if th == 0:
                                    j2 = 2 * gs
                                    if gs >= 2:
                                        js = 2 * gs - 4
                                        sy.wait_ge(qv2, 2 * js + 2)
                                        sy.dma_start(
                                            w2h_dst(js),
                                            qq2[:, :],
                                        ).then_inc(dQ2[0], 16)
                                    if gs >= 1:
                                        sy.wait_ge(qv2, 2 * (j2 - 2) + 2)
                                    sy.dma_start(
                                        strip2[:, 0:HH],
                                        w2h_src(j2),
                                    ).then_inc(dS2a[0], 16)
                        if th == 0:
                            for js in (124, 126):
                                sy.wait_ge(qv2, 2 * js + 2)
                                sy.dma_start(
                                    w2h_dst(js),
                                    qq2[:, :]).then_inc(dQ2[0], 16)
                        for tl in range(8):
                            gt = th * 8 + tl
                            sy.wait_ge(lnw, gt + 1)
                            sy.dma_start(
                                OUT[gt * P:(gt + 1) * P, :],
                                sumh[:, tl * D:(tl + 1) * D],
                            ).then_inc(dR, 16)
                    sy.wait_ge(dR, 16 * NT)

                @block.gpsimd
                def _(g):
                    for tt in range(NT):
                        if tt >= 1:
                            g.wait_ge(trA, 2 * tt)
                        g.dma_start(
                            xtmb[:, :],
                            R3[tt * P:(tt + 1) * P, :]).then_inc(dAx, 16)

                @block.scalar
                def _(s_):
                    odd_cur = [1]
                    for th in range(2):
                        for ip in range(2):
                            for ls in range(NIT):
                                il = ip * NIT + ls
                                for c in range(2):
                                    m = w1_chunk(th, ip, ls, c)
                                    s_.wait_ge(mm, m + 1)
                                    s_.activation(
                                        h[:, ls * HT + c * CH:
                                          ls * HT + (c + 1) * CH],
                                        psA[m % 4][:, :], AF.Gelu,
                                        bias=b1b[:, il:il + 1],
                                    ).then_inc(ev, 1)
                                if th == 0:
                                    gs_ = ip * NIT + ls
                                    j2 = 2 * gs_ + 1
                                    if gs_ >= 2:
                                        js = odd_cur[0]
                                        if js <= 2 * gs_ - 3:
                                            s_.wait_ge(qv2, 2 * js + 2)
                                            s_.dma_start(
                                                w2h_dst(js),
                                                qq2[:, :],
                                            ).then_inc(dQ2[1], 16)
                                            odd_cur[0] = js + 2
                                    if gs_ >= 1:
                                        s_.wait_ge(qv2, 2 * (j2 - 2) + 2)
                                    s_.dma_start(
                                        strip2[:, HH:2 * HH],
                                        w2h_src(j2),
                                    ).then_inc(dS2a[1], 16)
                            for dt in range(ND):
                                s2g = (th * 2 + ip) * ND + dt
                                if s2g >= 2:
                                    w2_ = s2g - 2
                                    s_.wait_ge(mm, w2_chunk(
                                        w2_ // 32, (w2_ % 32) // 16,
                                        w2_ % 16, 1) + 1)
                                if th == 0:
                                    kp = (th * 2 + ip) * ND + dt
                                    while odd_cur[0] <= 4 * kp + 3:
                                        js = odd_cur[0]
                                        s_.wait_ge(qv2, 2 * js + 2)
                                        s_.dma_start(
                                            w2h_dst(js),
                                            qq2[:, :],
                                        ).then_inc(dQ2[1], 16)
                                        odd_cur[0] = js + 2
                                    s_.wait_ge(dQ2[0], 16 * (2 * kp + 2))
                                    s_.wait_ge(dQ2[1], 16 * (2 * kp + 2))
                                elif ip == 0 and dt == 0:
                                    s_.wait_ge(dQ2[0], 16 * 64)
                                    s_.wait_ge(dQ2[1], 16 * 64)
                                s_.dma_start(
                                    wbuf2[:, (s2g % 2) * 4096:
                                          (s2g % 2 + 1) * 4096],
                                    w2q[ip, dt, :, :],
                                ).then_inc(wld2[s2g % 2], 16)
                                if s2g >= 2:
                                    s_.wait_ge(trC, 2 * (s2g - 1))
                                for c in range(2):
                                    m = w2_chunk(th, ip, dt, c)
                                    s_.wait_ge(mm, m + 1)
                                    if ip == 1:
                                        s_.activation(
                                            aT[:, (dt % 2) * HT + c * CH:
                                               (dt % 2) * HT + (c + 1) * CH],
                                            psA[m % 4][:, :], AF.Identity,
                                            bias=b2b[:, dt:dt + 1],
                                        ).then_inc(ev, 1)
                                    else:
                                        s_.activation(
                                            aT[:, (dt % 2) * HT + c * CH:
                                               (dt % 2) * HT + (c + 1) * CH],
                                            psA[m % 4][:, :],
                                            AF.Copy).then_inc(ev, 1)
                        for tl in range(8):
                            gt = th * 8 + tl
                            s_.wait_ge(lnv, 2 * (gt + 1))
                            if gt == 0:
                                s_.wait_ge(msc, 1)
                            s_.activation(rstd[:, :], mvb[:, 1:2], AF.Ln,
                                          bias=lnepsb[:, :]).then_inc(lnx, 1)
                            s_.wait_ge(lnx, gt + 1)
                            s_.activation(rstd[:, :], rstd[:, :], AF.Exp,
                                          scale=-0.5).then_inc(lnr, 1)

                @block.vector
                def _(v):
                    for th in range(2):
                        for tl in range(8):
                            tt = th * 8 + tl
                            if tt == 8:
                                v.wait_ge(mm, w1_chunk(0, 1, 31, 1) + 1)
                            for half in range(2):
                                v.wait_ge(trA, 2 * tt + half + 1)
                                v.tensor_copy(
                                    bass.AP(xT, half * 8 * HT + tl * P,
                                            [[ND * HT, P], [HT, 8], [1, P]]),
                                    ptr[half][:, :]).then_inc(evA, 1)
                        if th == 0:
                            def _add_pair(s2g):
                                for pk in range(2):
                                    v.wait_ge(trC, 2 * s2g + pk + 1)
                                    dst = bass.AP(sumh,
                                                  pk * 4 * D +
                                                  (s2g % ND) * P,
                                                  [[8 * D, P], [D, 4],
                                                   [1, P]])
                                    v.tensor_tensor(
                                        dst, psT[pk][:, :], dst,
                                        op=ALU.add).then_inc(addC, 1)
                            for j2 in range(2 * NP1):
                                if j2 == 62:
                                    v.wait_ge(dC, 16 * 8)
                                    for kp_ in range(15):
                                        _add_pair(kp_)
                                elif j2 == 126:
                                    for kp_ in range(15, 31):
                                        _add_pair(kp_)
                                sl2 = strip2[:, (j2 % 2) * HH:
                                             (j2 % 2 + 1) * HH]
                                v.wait_ge(dS2a[j2 % 2], 16 * (j2 // 2 + 1))
                                if j2 >= 1:
                                    v.wait_ge(dQ2[(j2 - 1) % 2],
                                              16 * ((j2 - 1) // 2 + 1))
                                v.tensor_scalar(
                                    qq2[:, :], sl2, th2[:, 1:2], None,
                                    op0=ALU.is_le).then_inc(qv2, 1)
                                v.wait_ge(qv2, 2 * j2 + 1)
                                v.scalar_tensor_tensor(
                                    qq2[:, :], sl2, th2[:, 0:1], qq2[:, :],
                                    op0=ALU.is_ge,
                                    op1=ALU.subtract).then_inc(qv2, 1)
                            _add_pair(31)
                        v.wait_ge(dC, 16 * 8 * (th + 1))
                        if th == 1:
                            for ip in range(2):
                                for dt in range(ND):
                                    s2g = (th * 2 + ip) * ND + dt
                                    for pk in range(2):
                                        v.wait_ge(trC, 2 * s2g + pk + 1)
                                        dst = bass.AP(sumh,
                                                      pk * 4 * D + dt * P,
                                                      [[8 * D, P], [D, 4],
                                                       [1, P]])
                                        v.tensor_tensor(
                                            dst, psT[pk][:, :], dst,
                                            op=ALU.add).then_inc(addC, 1)
                        v.wait_ge(addC, 64 * (th + 1))
                        for tl in range(8):
                            gt = th * 8 + tl
                            tile = sumh[:, tl * D:(tl + 1) * D]
                            for c4 in range(4):
                                inst = v.bn_stats(
                                    bn6[:, c4 * 6:(c4 + 1) * 6],
                                    tile[:, c4 * CH:(c4 + 1) * CH])
                            inst.then_inc(lnv, 1)
                            v.wait_ge(lnv, 2 * gt + 1)
                            v.bn_aggr(mvb[:, :], bn6[:, :]).then_inc(lnv, 1)
                            v.wait_ge(lnr, gt + 1)
                            v.tensor_scalar(tile, tile, mvb[:, 0:1],
                                            rstd[:, :], op0=ALU.subtract,
                                            op1=ALU.mult).then_inc(lnw, 1)

                @block.tensor
                def _(t):
                    for th in range(2):
                        for tl in range(8):
                            tt = th * 8 + tl
                            t.wait_ge(dAx, 16 * (tt + 1))
                            if tt >= 1:
                                t.wait_ge(evA, 2 * tt)
                            for dt in range(ND):
                                inst = t.transpose(
                                    ptr[dt // 8][:, (dt % 8) * P:
                                                 (dt % 8 + 1) * P],
                                    xtmb[:, dt * P:(dt + 1) * P], eyeb[:, :])
                                if dt % 8 == 7:
                                    inst.then_inc(trA, 1)
                        t.wait_ge(evA, 2 * (th * 8 + 8))
                        for ip in range(2):
                            for ls in range(NIT):
                                gs = (th * 2 + ip) * NIT + ls
                                t.wait_ge(wld1[gs % 2], 16 * (gs // 2 + 1))
                                for c in range(2):
                                    m = w1_chunk(th, ip, ls, c)
                                    if m >= 4:
                                        t.wait_ge(ev, m - 3)
                                    for kt in range(ND):
                                        inst = t.matmul(
                                            psA[m % 4][:, :],
                                            wbuf1[:, (gs % 2) * D + kt * P:
                                                  (gs % 2) * D +
                                                  (kt + 1) * P],
                                            xT[:, kt * HT + c * CH:
                                               kt * HT + (c + 1) * CH],
                                            start=(kt == 0),
                                            stop=(kt == ND - 1))
                                    inst.then_inc(mm, 1)
                            t.wait_ge(ev, th * 192 + ip * 96 + 64)
                            for dt in range(ND):
                                ws = (th * 2 + ip) * ND + dt
                                s2g = ws
                                t.wait_ge(wld2[ws % 2], 16 * (ws // 2 + 1))
                                for c in range(2):
                                    m = w2_chunk(th, ip, dt, c)
                                    t.wait_ge(ev, m - 3)
                                    for kt in range(NIT):
                                        inst = t.matmul(
                                            psA[m % 4][:, :],
                                            wbuf2[:, (ws % 2) * 4096 +
                                                  kt * P:
                                                  (ws % 2) * 4096 +
                                                  (kt + 1) * P],
                                            h[:, kt * HT + c * CH:
                                              kt * HT + (c + 1) * CH],
                                            start=(kt == 0),
                                            stop=(kt == NIT - 1))
                                    inst.then_inc(mm, 1)
                                t.wait_ge(ev, w2_chunk(th, ip, dt, 1) + 1)
                                for pk in range(2):
                                    if s2g >= 1:
                                        t.wait_ge(addC,
                                                  2 * (s2g - 1) + pk + 1)
                                    for tq4 in range(4):
                                        tq = pk * 4 + tq4
                                        inst = t.transpose(
                                            psT[pk][:, tq4 * P:
                                                    (tq4 + 1) * P],
                                            aT[:, (dt % 2) * HT + tq * P:
                                               (dt % 2) * HT +
                                               (tq + 1) * P],
                                            eyef[:, :])
                                    inst.then_inc(trC, 1)
    return nc


def host_prep_p4(w1, b1, w2, b2):
    w1t_s = slab_weights(np.asarray(w1, np.float32).T, osz=P)
    W2T = np.asarray(w2, np.float32).T
    w2t_s = np.stack([slab_weights(W2T[ip * 4096:(ip + 1) * 4096, :], osz=P)
                      for ip in range(2)])
    b1b = np.ascontiguousarray(np.asarray(b1, np.float32).reshape(64, P).T)
    b2b = np.ascontiguousarray(np.asarray(b2, np.float32).reshape(16, P).T)
    return w1t_s, np.ascontiguousarray(w2t_s), b1b, b2b


def _mk_phase4_nc():
    nc = bass.Bass("TRN2", target_bir_lowering=False)
    R3 = nc.dram_tensor("R3", [T, D], F32, kind="ExternalInput")
    w1t = nc.dram_tensor("w1t", [64, P, 2048], F32, kind="ExternalInput")
    w2t = nc.dram_tensor("w2t", [2, 16, P, 4096], F32, kind="ExternalInput")
    b1 = nc.dram_tensor("b1", [P, 64], F32, kind="ExternalInput")
    b2 = nc.dram_tensor("b2", [P, 16], F32, kind="ExternalInput")
    eye32 = nc.dram_tensor("eye32", [P, P], F32, kind="ExternalInput")
    eye16 = nc.dram_tensor("eye16", [P, P], BF16, kind="ExternalInput")
    OUT = nc.dram_tensor("OUT", [T, D], F32, kind="ExternalOutput")
    build_phase4(nc, R3, w1t, w2t, b1, b2, eye32, eye16, OUT)
    return nc


def run_phase4(R3_full, w1, b1, w2, b2, cores=N_CORES):
    w1t, w2t, b1b, b2b = host_prep_p4(w1, b1, w2, b2)
    nc = _mk_phase4_nc()
    in_maps = []
    for c in range(cores):
        in_maps.append(dict(R3=np.ascontiguousarray(R3_full[c], np.float32),
                            w1t=w1t, w2t=w2t, b1=b1b, b2=b2b,
                            eye32=_EYE32, eye16=_EYE16))
    res = run_bass_kernel_spmd(nc, in_maps, core_ids=list(range(cores)))
    return np.stack([r["OUT"] for r in res.results])



def _mk_phase1_nc():
    nc = bass.Bass("TRN2", target_bir_lowering=False)
    x = nc.dram_tensor("x", [T, D], F32, kind="ExternalInput")
    wqkv = nc.dram_tensor("wqkv", [48, P, D], F32, kind="ExternalInput")
    wout = nc.dram_tensor("wout", [16, P, D], F32, kind="ExternalInput")
    eye32 = nc.dram_tensor("eye32", [P, P], F32, kind="ExternalInput")
    eye16 = nc.dram_tensor("eye16", [P, P], BF16, kind="ExternalInput")
    R1 = nc.dram_tensor("R1", [T, D], F32, kind="ExternalOutput")
    build_phase1(nc, x, wqkv, wout, eye32, eye16, R1)
    return nc


_EYE32 = np.eye(P, dtype=np.float32)
_EYE16 = _EYE32.astype(ml_dtypes.bfloat16)


def host_prep_p1(W_qkv, W_out):
    WT = np.asarray(W_qkv, np.float32).T
    cols = []
    for i in range(ND):
        cols.append(WT[:, 4096 + i * P:4096 + (i + 1) * P])
        cols.append(WT[:, 2048 + i * P:2048 + (i + 1) * P])
        cols.append(WT[:, 0 + i * P:0 + (i + 1) * P])
    WTp = np.concatenate(cols, axis=1)
    wqkv_s = slab_weights(WTp, osz=P)
    wout_s = slab_weights(np.asarray(W_out, np.float32).T, osz=P)
    return wqkv_s, wout_s


def run_phase1(x_full, W_qkv, W_out, cores=N_CORES):
    wqkv_s, wout_s = host_prep_p1(W_qkv, W_out)
    nc = _mk_phase1_nc()
    in_maps = []
    for c in range(cores):
        in_maps.append(dict(x=np.ascontiguousarray(x_full[c], np.float32),
                            wqkv=wqkv_s, wout=wout_s,
                            eye32=_EYE32, eye16=_EYE16))
    res = run_bass_kernel_spmd(nc, in_maps, core_ids=list(range(cores)))
    return np.stack([r["R1"] for r in res.results])


def _mk_fused_nc():
    nc = bass.Bass("TRN2", target_bir_lowering=False)
    x = nc.dram_tensor("x", [T, D], F32, kind="ExternalInput")
    wqkv = nc.dram_tensor("wqkv", [48, P, D], F32, kind="ExternalInput")
    wout = nc.dram_tensor("wout", [16, P, D], F32, kind="ExternalInput")
    wgu = nc.dram_tensor("wgu", [128, P, D], F32, kind="ExternalInput")
    wd = nc.dram_tensor("wd", [2, 16, P, 4096], F32, kind="ExternalInput")
    wq = nc.dram_tensor("wq", [16, P, D], F32, kind="ExternalInput")
    wk = nc.dram_tensor("wk", [16, P, D], F32, kind="ExternalInput")
    wv = nc.dram_tensor("wv", [16, P, D], F32, kind="ExternalInput")
    wo = nc.dram_tensor("wo", [16, P, D], F32, kind="ExternalInput")
    nw = nc.dram_tensor("nw", [D], F32, kind="ExternalInput")
    bq = nc.dram_tensor("bq", [P, 16], F32, kind="ExternalInput")
    bk = nc.dram_tensor("bk", [P, 16], F32, kind="ExternalInput")
    bv = nc.dram_tensor("bv", [P, 16], F32, kind="ExternalInput")
    bo = nc.dram_tensor("bo", [P, 16], F32, kind="ExternalInput")
    masks = nc.dram_tensor("masks", [4, P, CH], F32, kind="ExternalInput")
    w1t = nc.dram_tensor("w1t", [64, P, 2048], F32, kind="ExternalInput")
    w2t = nc.dram_tensor("w2t", [2, 16, P, 4096], F32, kind="ExternalInput")
    b1 = nc.dram_tensor("b1", [P, 64], F32, kind="ExternalInput")
    b2 = nc.dram_tensor("b2", [P, 16], F32, kind="ExternalInput")
    eye32 = nc.dram_tensor("eye32", [P, P], F32, kind="ExternalInput")
    eye16 = nc.dram_tensor("eye16", [P, P], BF16, kind="ExternalInput")
    R1 = nc.dram_tensor("R1", [T, D], F32)
    R2 = nc.dram_tensor("R2", [T, D], F32)
    R3 = nc.dram_tensor("R3", [T, D], F32)
    OUT = nc.dram_tensor("OUT", [T, D], F32, kind="ExternalOutput")
    build_phase1(nc, x, wqkv, wout, eye32, eye16, R1)
    build_phase2(nc, R1, wgu, wd, eye32, eye16, R2)
    build_phase3(nc, R2, wq, wk, wv, wo, nw, bq, bk, bv, bo, masks,
                 eye32, eye16, R3)
    build_phase4(nc, R3, w1t, w2t, b1, b2, eye32, eye16, OUT)
    return nc


def kernel(**inputs):
    x = np.asarray(inputs["L0_attn_x"], np.float32)
    wqkv_s, wout_s = host_prep_p1(inputs["L0_attn_W_qkv"],
                                  inputs["L0_attn_W_out"])
    wgu_s, wd_s = host_prep_p2(inputs["L0_ffn_w_gate_up"],
                               inputs["L0_ffn_w_down"])
    wqs, wks, wvs, wos, bqb, bkb, bvb, bob, masks = host_prep_p3(
        inputs["L1_attn_wq"], inputs["L1_attn_wk"], inputs["L1_attn_wv"],
        inputs["L1_attn_wo"], inputs["L1_attn_bq"], inputs["L1_attn_bk"],
        inputs["L1_attn_bv"], inputs["L1_attn_bo"])
    w1t, w2t, b1b, b2b = host_prep_p4(inputs["L1_ffn_w1"],
                                      inputs["L1_ffn_b1"],
                                      inputs["L1_ffn_w2"],
                                      inputs["L1_ffn_b2"])
    nw = np.ascontiguousarray(np.asarray(inputs["L1_attn_norm_w"],
                                         np.float32))
    nc = _mk_fused_nc()
    in_maps = []
    for c in range(N_CORES):
        in_maps.append(dict(
            x=np.ascontiguousarray(x[c]),
            wqkv=wqkv_s, wout=wout_s, wgu=wgu_s, wd=wd_s,
            wq=wqs, wk=wks, wv=wvs, wo=wos, nw=nw,
            bq=bqb, bk=bkb, bv=bvb, bo=bob, masks=masks,
            w1t=w1t, w2t=w2t, b1=b1b, b2=b2b,
            eye32=_EYE32, eye16=_EYE16))
    res = run_bass_kernel_spmd(nc, in_maps, core_ids=list(range(N_CORES)))
    return np.stack([r["OUT"] for r in res.results]).astype(np.float32)


def kernel_unfused(**inputs):
    x = np.asarray(inputs["L0_attn_x"], np.float32)
    R1 = run_phase1(x, inputs["L0_attn_W_qkv"], inputs["L0_attn_W_out"])
    R2 = run_phase2(R1, inputs["L0_ffn_w_gate_up"], inputs["L0_ffn_w_down"])
    R3 = run_phase3(R2, inputs["L1_attn_norm_w"],
                    inputs["L1_attn_wq"], inputs["L1_attn_bq"],
                    inputs["L1_attn_wk"], inputs["L1_attn_bk"],
                    inputs["L1_attn_wv"], inputs["L1_attn_bv"],
                    inputs["L1_attn_wo"], inputs["L1_attn_bo"])
    out = run_phase4(R3, inputs["L1_ffn_w1"], inputs["L1_ffn_b1"],
                     inputs["L1_ffn_w2"], inputs["L1_ffn_b2"])
    return out.astype(np.float32)
